# version 08
# baseline (speedup 1.0000x reference)
import contextlib
import numpy as np
import ml_dtypes
import concourse.bass as bass
import concourse.mybir as mybir
from concourse.bass_utils import run_bass_kernel_spmd

F32 = mybir.dt.float32
BF16 = mybir.dt.bfloat16
AF = mybir.ActivationFunctionType
ALU = mybir.AluOpType

NCORES = 8
B, T, D = 8, 2048, 1024
NCH = D // 128
TW = T // NCORES
S = 256
FF1 = 4096
NEXP, CAPG = 8, 2048
T_WIN2 = 2048
PAD = 384
MFF = 4096

STAGES = dict(l0=True, l1a=True, l1f=True, out=True)
DEBUG = ()


class Rec:
    __slots__ = ("idx", "eng", "fn", "reads", "writes", "kind", "dma_sem",
                 "flag", "waits", "drain", "seq", "_deps")

    def __init__(self, idx, eng, fn, reads, writes, kind, dma_sem):
        self.idx, self.eng, self.fn = idx, eng, fn
        self.reads, self.writes = tuple(reads), tuple(writes)
        self.kind, self.dma_sem = kind, dma_sem
        self.flag = False
        self.waits = []
        self.drain = False
        self.seq = 0
        self._deps = []


class Prog:
    ENGINES = ("tensor", "scalar", "vector", "sync", "gpsimd")

    def __init__(self):
        self.recs = []
        self.last_w = {}
        self.readers = {}
        self.final_waits = []

    def emit(self, eng, fn, reads=(), writes=(), mm=False, dma_sem=None, cc=False,
             conflicts=()):
        kind = 'cc' if cc else ('d' if dma_sem else ('mm' if mm else 'c'))
        r = Rec(len(self.recs), eng, fn, reads, writes, kind, dma_sem)
        self.recs.append(r)
        deps, seen = [], set()

        def add(d):
            if d is not None and d is not r and d.idx not in seen:
                seen.add(d.idx)
                deps.append(d)
        for k in r.reads:
            add(self.last_w.get(k))
        for k in tuple(writes) + tuple(conflicts):
            add(self.last_w.get(k))
            for d in self.readers.get(k, ()):
                add(d)
        r._deps = deps
        for k in r.reads:
            self.readers.setdefault(k, []).append(r)
        for k in r.writes:
            self.last_w[k] = r
            self.readers[k] = []
        return r

    def resolve(self):
        for r in self.recs:
            for d in r._deps:
                if d.kind in ('d', 'cc'):
                    d.flag = True
                elif d.eng == r.eng and r.kind not in ('d', 'cc'):
                    if d.kind == 'mm' and r.kind == 'mm':
                        continue
                    r.drain = True
                else:
                    d.flag = True
        cnt = {}
        for r in self.recs:
            if r.kind == 'd':
                r.flag = True
                key = 'D:' + r.dma_sem
                cnt[key] = cnt.get(key, 0) + 16
                r.seq = cnt[key]
            elif r.kind == 'cc':
                r.flag = True
                key = 'D:' + r.dma_sem
                cnt[key] = cnt.get(key, 0) + 1
                r.seq = cnt[key]
            elif r.flag:
                key = 'E:' + r.eng
                cnt[key] = cnt.get(key, 0) + 1
                r.seq = cnt[key]
        import bisect
        semlist = {}
        for r in self.recs:
            if r.kind in ('d', 'cc'):
                semlist.setdefault('D:' + r.dma_sem, []).append((r.idx, r.seq))
        marks = {}
        for r in self.recs:
            for d in r._deps:
                if d.kind in ('d', 'cc'):
                    sem = 'D:' + d.dma_sem
                    lst = semlist[sem]
                    pos = bisect.bisect_left(lst, (r.idx, -1)) - 1
                    val = lst[pos][1]
                    assert val >= d.seq
                elif d.eng == r.eng:
                    continue
                else:
                    sem = 'E:' + d.eng
                    val = d.seq
                wm = marks.get((r.eng, sem), 0)
                if val > wm:
                    marks[(r.eng, sem)] = val
                    r.waits.append((sem, val))
        self.semnames = sorted(cnt.keys())
        self.final_counts = dict(cnt)

    def make_bodies(self, sems):
        per = {e: [] for e in self.ENGINES}
        for r in self.recs:
            per[r.eng].append(r)

        def body(eng_name):
            def go(e):
                for r in per[eng_name]:
                    if r.drain:
                        e.drain()
                    for sem, v in r.waits:
                        e.wait_ge(sems[sem], v)
                    instr = r.fn(e)
                    if r.flag:
                        if r.kind == 'd':
                            instr.then_inc(sems['D:' + r.dma_sem], 16)
                        elif r.kind == 'cc':
                            instr.then_inc(sems['D:' + r.dma_sem], 1)
                        else:
                            instr.then_inc(sems['E:' + r.eng], 1)
                for eng2, sem, v in self.final_waits:
                    if eng2 == eng_name:
                        e.wait_ge(sems[sem], v)
            return go
        return {e: body(e) for e in self.ENGINES}


def build_program():
    nc = bass.Bass("TRN2", target_bir_lowering=False)
    es = contextlib.ExitStack()
    P = Prog()
    dt = {}

    def din(name, shape, dtype=F32):
        dt[name] = nc.dram_tensor(name, list(shape), dtype, kind="ExternalInput").ap()

    def dout(name, shape, dtype=F32):
        dt[name] = nc.dram_tensor(name, list(shape), dtype, kind="ExternalOutput").ap()

    def dint(name, shape, dtype=F32):
        dt[name] = nc.dram_tensor(name, list(shape), dtype, kind="Internal").ap()

    def sb(name, shape, dtype):
        return es.enter_context(nc.sbuf_tensor(name, list(shape), dtype))

    def psalloc(name, shape=(128, 512)):
        return es.enter_context(nc.psum_tensor(name, list(shape), F32))

    din("xT", (D, T))
    din("wq", (D, D)); din("wk", (D, D)); din("wv", (D, D))
    din("bq_col", (128, NCH)); din("bk_col", (128, NCH)); din("bv_row", (1, D))
    din("w0r", (D, D)); din("w0k", (D, D)); din("w0v", (D, D))
    din("mur_col", (128, NCH)); din("muk_col", (128, NCH))
    din("bmT", (D, S)); din("dT", (D, D)); din("aT", (S, S)); din("cT", (S, D))
    din("lnw_col", (128, NCH)); din("lnb_col", (128, NCH))
    din("w1T", (D, FF1)); din("b1_col", (128, FF1 // 128))
    din("w2T", (FF1, D)); din("b2_col", (128, NCH))
    din("ipw", (D, 3 * D)); din("ipb_row", (1, D)); din("ipb_col", (128, 3 * NCH))
    din("opw", (D, D)); din("opb_col", (128, NCH))
    din("ln1g_col", (128, NCH)); din("ln1b_col", (128, NCH))
    din("ln2g_col", (128, NCH)); din("ln2b_col", (128, NCH))
    din("wg", (D, NEXP))
    din("myw1T", (MFF, D)); din("myw2", (MFF, D))
    din("myb1c", (128, 32)); din("myb2r", (1, D))
    din("mha_mask", (128, 128))
    din("tri_all", (128, 128))
    din("selcol", (128, NCORES)); din("m64", (64, NEXP))
    din("iota8r", (128, NEXP))
    din("ident", (128, 128)); din("identf", (128, 128))
    dout("out_fm", (B, D, TW))
    for d in DEBUG:
        if d == "y1":
            dout("dbg_y1", (D, T))
        elif d == "states":
            dout("dbg_states", (256, T))
        elif d in ("ypre", "y1a"):
            dout("dbg_" + d, (D, 256))
        elif d == "h":
            dout("dbg_h", (512, 256))
        else:
            dout("dbg_" + d, (D, T))

    dint("ag_xn1_in", (D, T), BF16)
    dint("ag_xn1_out", (NCORES, D, T), BF16)
    dint("ag_s_in", (D, T), F32)
    dint("ag_s_out", (NCORES, D, T), F32)
    dint("ag_cnt_in", (B, NEXP), F32)
    dint("ag_cnt_out", (NCORES, B, NEXP), F32)
    dint("hb_w1b", (D, FF1), BF16)
    dint("hb_opb", (D, D), BF16)
    dint("hb_w2b", (FF1, D), BF16)
    dint("hb_y1", (D, T), BF16)
    dint("hb_st", (D, T_WIN2), BF16)
    dint("ag_wf_in", (1025, D), BF16)
    dint("ag_wf_out", (NCORES, 1025, D), BF16)

    x1 = sb("x1", (128, NCH * T), F32)
    xn = sb("xn", (128, NCH * T), BF16)
    onesr_f = sb("onesr_f", (128, 128), F32)
    onesr_b = sb("onesr_b", (1, 128), BF16)
    onesc_f = sb("onesc_f", (128, 1), F32)
    onesc_b = sb("onesc_b", (128, 1), BF16)
    zrow_b = sb("zrow_b", (1, 512), BF16)
    eps_col = sb("eps_col", (128, 1), F32)
    pb = [psalloc(f"pb{i}") for i in range(8)]
    PSK = [("ps", i) for i in range(8)]

    smalls = {}
    for nm, shp in [("bq_col", (128, NCH)), ("bk_col", (128, NCH)),
                    ("mur_col", (128, NCH)), ("muk_col", (128, NCH)),
                    ("lnw_col", (128, NCH)), ("lnb_col", (128, NCH)),
                    ("b1_col", (128, FF1 // 128)), ("b2_col", (128, NCH)),
                    ("opb_col", (128, NCH)),
                    ("ln1g_col", (128, NCH)), ("ln1b_col", (128, NCH)),
                    ("ln2g_col", (128, NCH)), ("ln2b_col", (128, NCH)),
                    ("identf", (128, 128))]:
        smalls[nm] = sb("s_" + nm, shp, F32)
    smallsb = {}
    for nm, shp in [("bv_row", (1, D)), ("ipb_row", (1, D)),
                    ("mha_mask", (128, 128)),
                    ("tri_all", (128, 128)),
                    ("ident", (128, 128))]:
        smallsb[nm] = sb("b_" + nm, shp, BF16)

    sqt = [sb("sqt0", (128, 512), F32)] * 2
    stat_m = sb("stat_m", (128, 512), F32)
    stat_r = sb("stat_r", (128, 512), F32)
    vsc = [sb(f"vsc{i}", (128, 512), F32) for i in range(2)]
    stat_t = vsc[0]

    E = P.emit
    K_SM = lambda nm: ("small", nm)

    for i in range(NCH):
        E("sync", (lambda e, i=i: e.dma_start(x1[:, i * T:(i + 1) * T],
                                              dt["xT"][i * 128:(i + 1) * 128, :])),
          writes=[("x1", i)], dma_sem="dx")
    for nm in smalls:
        E("sync", (lambda e, nm=nm: e.dma_start(smalls[nm][:, :], dt[nm][:, :])),
          writes=[K_SM(nm)], dma_sem="dsm")
    for nm in smallsb:
        E("gpsimd", (lambda e, nm=nm: e.dma_start(smallsb[nm][:, :], dt[nm][:, :])),
          writes=[K_SM(nm)], dma_sem="dsb")
    for buf, val in [(onesr_f, 1.0), (onesr_b, 1.0), (onesc_f, 1.0), (onesc_b, 1.0),
                     (zrow_b, 0.0), (eps_col, 1e-5)]:
        E("gpsimd", (lambda e, buf=buf, val=val: e.memset(buf[:, :], val)),
          writes=[("c", id(buf))])

    def KC(*bufs):
        return [("c", id(b)) for b in bufs]

    def emit_ln(src, src_key, dst, dst_key, tokens, gamma_nm=None, beta_nm=None, onescol=None, dst_extra=(), dst_conflicts=()):
        onescol = onescol if onescol is not None else onesc_f
        qw = min(512, tokens)
        nqc = tokens // qw
        for bank in (6, 7):
            E("tensor", (lambda e, bank=bank: e.matmul(pb[bank][:, 0:512], zrow_b[0:1, 0:128],
                                                       zrow_b[0:1, 0:512], start=True, stop=True)),
              reads=KC(zrow_b), writes=[PSK[bank]], mm=True)
        for qc in range(nqc):
            for i in range(NCH):
                sq = sqt[(qc * NCH + i) % 2]
                sl = (i * tokens + qc * qw, i * tokens + (qc + 1) * qw)
                E("scalar", (lambda e, sq=sq, sl=sl, qw=qw:
                             e.activation(sq[:, 0:qw], src[:, sl[0]:sl[1]], AF.Square)),
                  reads=[src_key(i, qc)], writes=[("sq", id(sq))])
                E("tensor", (lambda e, i=i, qc=qc, sl=sl, qw=qw:
                             e.matmul(pb[6][32 * qc:32 * qc + 1, 0:qw], onescol[:, 0:1],
                                      src[:, sl[0]:sl[1]], start=False, stop=(i == NCH - 1),
                                      skip_group_check=True, tile_position=(0, 32 * qc))),
                  reads=[src_key(i, qc)] + KC(onescol), writes=[PSK[6]], mm=True)
                E("tensor", (lambda e, i=i, qc=qc, sq=sq, qw=qw:
                             e.matmul(pb[7][32 * qc:32 * qc + 1, 0:qw], onesc_f[:, 0:1],
                                      sq[:, 0:qw], start=False, stop=(i == NCH - 1),
                                      skip_group_check=True, tile_position=(0, 32 * qc))),
                  reads=[("sq", id(sq))] + KC(onesc_f), writes=[PSK[7]], mm=True)
                yield
        E("vector", (lambda e: e.tensor_scalar(stat_m[:, :], pb[6][:, :], 1.0 / D, None, ALU.mult)),
          reads=[PSK[6]], writes=[("ln", "m")])
        E("vector", (lambda e: e.tensor_mul(stat_t[:, :], stat_m[:, :], stat_m[:, :])),
          reads=[("ln", "m")], writes=[("vsc", id(vsc[0]))])
        E("vector", (lambda e: e.scalar_tensor_tensor(stat_r[:, :], pb[7][:, :], 1.0 / D,
                                                      stat_t[:, :], ALU.mult, ALU.subtract)),
          reads=[PSK[7], ("vsc", id(vsc[0]))], writes=[("ln", "v")])
        E("scalar", (lambda e: e.activation(stat_t[:, :], stat_r[:, :], AF.Ln, bias=eps_col[:, 0:1])),
          reads=[("ln", "v")] + KC(eps_col), writes=[("vsc", id(vsc[0]))])
        E("scalar", (lambda e: e.activation(stat_r[:, :], stat_t[:, :], AF.Exp, scale=-0.5)),
          reads=[("vsc", id(vsc[0]))], writes=[("ln", "r")])
        yield
        for qc in range(nqc):
            E("tensor", (lambda e, qc=qc, qw=qw: e.matmul(pb[4][:, 0:qw], onesr_f[32 * qc:32 * qc + 1, :],
                                                   stat_m[32 * qc:32 * qc + 1, 0:qw],
                                                   start=True, stop=True,
                                                   tile_position=(32 * qc, 0))),
              reads=[("ln", "m")] + KC(onesr_f), writes=[PSK[4]], mm=True)
            E("tensor", (lambda e, qc=qc, qw=qw: e.matmul(pb[5][:, 0:qw], onesr_f[32 * qc:32 * qc + 1, :],
                                                   stat_r[32 * qc:32 * qc + 1, 0:qw],
                                                   start=True, stop=True,
                                                   tile_position=(32 * qc, 0))),
              reads=[("ln", "r")] + KC(onesr_f), writes=[PSK[5]], mm=True)
            for i in range(NCH):
                v = vsc[i % 2]
                sl = (i * tokens + qc * qw, i * tokens + (qc + 1) * qw)
                E("vector", (lambda e, v=v, sl=sl, qw=qw:
                             e.tensor_sub(v[:, 0:qw], src[:, sl[0]:sl[1]], pb[4][:, 0:qw])),
                  reads=[src_key(i, qc), PSK[4]], writes=[("vsc", id(v))])
                if gamma_nm is None:
                    E("vector", (lambda e, v=v, sl=sl, qw=qw:
                                 e.tensor_mul(dst[:, sl[0]:sl[1]], v[:, 0:qw], pb[5][:, 0:qw])),
                      reads=[("vsc", id(v)), PSK[5]] + list(dst_extra), writes=[dst_key(i, qc)],
                      conflicts=list(dst_conflicts))
                    yield
                else:
                    g, bcol = smalls[gamma_nm], smalls[beta_nm]
                    E("vector", (lambda e, v=v, qw=qw:
                                 e.tensor_mul(v[:, 0:qw], v[:, 0:qw], pb[5][:, 0:qw])),
                      reads=[("vsc", id(v)), PSK[5]], writes=[("vsc", id(v))])
                    E("vector", (lambda e, v=v, sl=sl, g=g, bcol=bcol, i=i, qw=qw:
                                 e.tensor_scalar(dst[:, sl[0]:sl[1]], v[:, 0:qw],
                                                 g[:, i:i + 1], bcol[:, i:i + 1],
                                                 ALU.mult, ALU.add)),
                      reads=[("vsc", id(v)), K_SM(gamma_nm), K_SM(beta_nm)] + list(dst_extra),
                      writes=[dst_key(i, qc)], conflicts=list(dst_conflicts))
                    yield

    def DRAIN(gen):
        for _ in gen:
            pass

    arena = sb("arena", (128, 36864), BF16)
    tenants = []

    def claim(c0, c1, keys):
        old = []
        keep = []
        for ent in tenants:
            if ent[0] < c1 and c0 < ent[1]:
                old.extend(ent[2])
                if ent[0] < c0:
                    keep.append([ent[0], c0, ent[2]])
                if ent[1] > c1:
                    keep.append([c1, ent[1], ent[2]])
            else:
                keep.append(ent)
        tenants[:] = keep + [[c0, c1, list(keys)]]
        return old

    WA, WB, WC = 0, 8192, 16384
    XRC, RC, KC2 = 24576, 28672, 32768
    VTC = 16384
    QHC, KHC = 32768, 34816

    def load_wslot(nm, base, region, keyname):
        keys = [(keyname, i) for i in range(NCH)]
        old = claim(base, base + NCH * 1024, keys)
        for i in range(NCH):
            E("gpsimd", (lambda e, nm=nm, base=base, i=i:
                         e.dma_start(arena[:, base + i * 1024: base + (i + 1) * 1024],
                                     dt[nm][i * 128:(i + 1) * 128, :])),
              writes=[(keyname, i)], conflicts=old, dma_sem="dw")

    pt = [sb(f"pt{i}", (128, 512), BF16) for i in range(2)]
    attq = sb("attq", (128, 2048), BF16)
    y1tb = sb("y1tb", (128, 2 * 1024), BF16)
    attp = y1tb
    recip_sb = sb("recip_sb", (128, 128), F32)
    recb = sb("recb", (128, 160), F32)
    ctmp = sb("ctmp", (128, 512), F32)

    if STAGES["l0"]:
        DRAIN(emit_ln(x1, lambda i, qc: ("x1", i), xn, lambda i, qc: ("xn", i, qc), T))

        load_wslot("w0r", WA, "WA", "w0r")
        load_wslot("w0k", WB, "WB", "w0k")
        load_wslot("w0v", WC, "WC", "w0v")

        def emit_mix(qc, mu_nm, tag):
            mu = smalls[mu_nm]
            keys = [(tag, qc, i) for i in range(NCH)]
            old = claim(XRC, XRC + 4096, keys)
            for i in range(NCH):
                lo = i * T + qc * 512
                d0 = XRC + i * 512
                if qc == 0:
                    E("vector", (lambda e, lo=lo:
                                 e.tensor_sub(vsc[0][:, 1:512], xn[:, lo + 1:lo + 512],
                                              xn[:, lo:lo + 511])),
                      reads=[("xn", i, 0)], writes=[("vsc", 0)])
                    E("vector", (lambda e, mu=mu, i=i:
                                 e.tensor_scalar(vsc[1][:, 1:512], vsc[0][:, 1:512],
                                                 mu[:, i:i + 1], None, ALU.mult)),
                      reads=[("vsc", 0), K_SM(mu_nm)], writes=[("vsc", 1)])
                    E("vector", (lambda e, lo=lo, d0=d0:
                                 e.tensor_add(arena[:, d0 + 1:d0 + 512], vsc[1][:, 1:512],
                                              xn[:, lo:lo + 511])),
                      reads=[("vsc", 1), ("xn", i, 0)], writes=[(tag, qc, i)], conflicts=old)
                    E("vector", (lambda e, lo=lo, d0=d0, mu=mu, i=i:
                                 e.tensor_scalar(arena[:, d0:d0 + 1], xn[:, lo:lo + 1],
                                                 mu[:, i:i + 1], None, ALU.mult)),
                      reads=[("xn", i, 0), K_SM(mu_nm)], writes=[(tag, qc, i)])
                else:
                    E("vector", (lambda e, lo=lo, i=i, qc=qc:
                                 e.tensor_sub(vsc[0][:, 0:512], xn[:, lo:lo + 512],
                                              xn[:, lo - 1:lo + 511])),
                      reads=[("xn", i, qc), ("xn", i, qc - 1)], writes=[("vsc", 0)])
                    E("vector", (lambda e, mu=mu, i=i:
                                 e.tensor_scalar(vsc[1][:, 0:512], vsc[0][:, 0:512],
                                                 mu[:, i:i + 1], None, ALU.mult)),
                      reads=[("vsc", 0), K_SM(mu_nm)], writes=[("vsc", 1)])
                    E("vector", (lambda e, lo=lo, d0=d0, i=i, qc=qc:
                                 e.tensor_add(arena[:, d0:d0 + 512], vsc[1][:, 0:512],
                                              xn[:, lo - 1:lo + 511])),
                      reads=[("vsc", 1), ("xn", i, qc - 1), ("xn", i, qc)],
                      writes=[(tag, qc, i)], conflicts=old)

        for qc in range(4):
            emit_mix(qc, "mur_col", "xr")
            rkeys = [("rblk", qc, oc) for oc in range(NCH)]
            rold = claim(RC, RC + 4096, rkeys)
            for oc in range(NCH):
                bank = 2 + oc % 2
                for i in range(NCH):
                    E("tensor", (lambda e, oc=oc, i=i, bank=bank, qc=qc:
                                 e.matmul(pb[bank][:, 0:512],
                                          arena[:, WA + i * 1024 + oc * 128: WA + i * 1024 + (oc + 1) * 128],
                                          arena[:, XRC + i * 512: XRC + (i + 1) * 512],
                                          start=(i == 0), stop=(i == NCH - 1),
                                          skip_group_check=True)),
                      reads=[("xr", qc, i), ("w0r", i)], writes=[PSK[bank]], mm=True)
                E("scalar", (lambda e, oc=oc, bank=bank:
                             e.activation(arena[:, RC + oc * 512: RC + (oc + 1) * 512],
                                          pb[bank][:, 0:512], AF.Sigmoid)),
                  reads=[PSK[bank]], writes=[("rblk", qc, oc)], conflicts=rold)
            emit_mix(qc, "muk_col", "xk")
            kkeys = [("kablk", qc, oc) for oc in range(NCH)]
            kold = claim(KC2, KC2 + 4096, kkeys)
            for oc in range(NCH):
                bank = 2 + oc % 2
                for i in range(NCH):
                    E("tensor", (lambda e, oc=oc, i=i, bank=bank, qc=qc:
                                 e.matmul(pb[bank][:, 0:512],
                                          arena[:, WB + i * 1024 + oc * 128: WB + i * 1024 + (oc + 1) * 128],
                                          arena[:, XRC + i * 512: XRC + (i + 1) * 512],
                                          start=(i == 0), stop=(i == NCH - 1),
                                          skip_group_check=True)),
                      reads=[("xk", qc, i), ("w0k", i)], writes=[PSK[bank]], mm=True)
                E("scalar", (lambda e, oc=oc, bank=bank:
                             e.activation(arena[:, KC2 + oc * 512: KC2 + (oc + 1) * 512],
                                          pb[bank][:, 0:512], AF.Relu)),
                  reads=[PSK[bank]], writes=[("kablk", qc, oc)], conflicts=kold)
                E("vector", (lambda e, oc=oc:
                             e.tensor_mul(arena[:, KC2 + oc * 512: KC2 + (oc + 1) * 512],
                                          arena[:, KC2 + oc * 512: KC2 + (oc + 1) * 512],
                                          arena[:, KC2 + oc * 512: KC2 + (oc + 1) * 512])),
                  reads=[("kablk", qc, oc)], writes=[("kablk", qc, oc)])
            for oc in range(NCH):
                bank = 2 + oc % 2
                for i in range(NCH):
                    E("tensor", (lambda e, oc=oc, i=i, bank=bank, qc=qc:
                                 e.matmul(pb[bank][:, 0:512],
                                          arena[:, WC + i * 1024 + oc * 128: WC + i * 1024 + (oc + 1) * 128],
                                          arena[:, KC2 + i * 512: KC2 + (i + 1) * 512],
                                          start=(i == 0), stop=(i == NCH - 1),
                                          skip_group_check=True)),
                      reads=[("kablk", qc, i), ("w0v", i)], writes=[PSK[bank]], mm=True)
                E("vector", (lambda e, oc=oc, bank=bank:
                             e.tensor_mul(ctmp[:, :], pb[bank][:, 0:512],
                                          arena[:, RC + oc * 512: RC + (oc + 1) * 512])),
                  reads=[PSK[bank], ("rblk", qc, oc)], writes=[("ctmp",)])
                E("vector", (lambda e, oc=oc, qc=qc:
                             e.tensor_add(x1[:, oc * T + qc * 512: oc * T + (qc + 1) * 512],
                                          x1[:, oc * T + qc * 512: oc * T + (qc + 1) * 512],
                                          ctmp[:, :])),
                  reads=[("ctmp",), ("x1", oc)], writes=[("x1", oc)])

        WTMP, KHF, VHF, WQR = 0, 8192, 16384, 24576
        load_wslot("wq", WQR, "WQR", "wq")
        den = sqt[0]
        for mhalf in range(2):
            mb0 = mhalf * 1024
            wvold = claim(WTMP, WTMP + 8192, [("wv", mhalf, i) for i in range(NCH)])
            for i in range(NCH):
                E("gpsimd", (lambda e, i=i: e.dma_start(
                    arena[:, WTMP + i * 1024: WTMP + (i + 1) * 1024],
                    dt["wv"][i * 128:(i + 1) * 128, :])),
                  writes=[("wv", mhalf, i)], conflicts=wvold, dma_sem="dw")
            vkeys = [("vtok", mhalf, tl, half) for tl in range(8) for half in range(2)]
            vold = claim(VHF, VHF + 8192, vkeys)
            for tl in range(8):
                for half in range(2):
                    bank = (tl * 2 + half) % 2
                    for i in range(NCH):
                        E("tensor", (lambda e, tl=tl, half=half, i=i, bank=bank, mb0=mb0:
                                     e.matmul(pb[bank][:, 0:512],
                                              xn[:, i * T + mb0 + tl * 128:
                                                 i * T + mb0 + (tl + 1) * 128],
                                              arena[:, WTMP + i * 1024 + half * 512:
                                                    WTMP + i * 1024 + (half + 1) * 512],
                                              start=(i == 0), stop=False,
                                              skip_group_check=True)),
                          reads=[("xn", i, (mb0 + tl * 128) // 512), ("wv", mhalf, i)],
                          writes=[PSK[bank]], mm=True)
                    E("tensor", (lambda e, half=half, bank=bank:
                                 e.matmul(pb[bank][:, 0:512], onesr_b[0:1, :],
                                          smallsb["bv_row"][0:1, half * 512:(half + 1) * 512],
                                          start=False, stop=True, skip_group_check=True)),
                      reads=[K_SM("bv_row")] + KC(onesr_b), writes=[PSK[bank]], mm=True)
                    E("scalar", (lambda e, tl=tl, half=half, bank=bank:
                                 e.copy(arena[:, VHF + tl * 1024 + half * 512:
                                              VHF + tl * 1024 + (half + 1) * 512],
                                        pb[bank][:, 0:512])),
                      reads=[PSK[bank]], writes=[("vtok", mhalf, tl, half)],
                      conflicts=vold)
            wkold = claim(WTMP, WTMP + 8192, [("wk", mhalf, i) for i in range(NCH)])
            for i in range(NCH):
                E("gpsimd", (lambda e, i=i: e.dma_start(
                    arena[:, WTMP + i * 1024: WTMP + (i + 1) * 1024],
                    dt["wk"][i * 128:(i + 1) * 128, :])),
                  writes=[("wk", mhalf, i)], conflicts=wkold, dma_sem="dw")
            kkeys = [("khf", mhalf, g, h2) for g in range(8) for h2 in range(2)]
            kold = claim(KHF, KHF + 8192, kkeys)
            for g in range(8):
                for h2 in range(2):
                    bank = (g * 2 + h2) % 2
                    for i in range(NCH):
                        E("tensor", (lambda e, g=g, h2=h2, i=i, bank=bank, mb0=mb0:
                                     e.matmul(pb[bank][:, 0:512],
                                              arena[:, WTMP + i * 1024 + g * 128:
                                                    WTMP + i * 1024 + (g + 1) * 128],
                                              xn[:, i * T + mb0 + h2 * 512:
                                                 i * T + mb0 + (h2 + 1) * 512],
                                              start=(i == 0), stop=(i == NCH - 1),
                                              skip_group_check=True)),
                          reads=[("xn", i, (mb0 + h2 * 512) // 512), ("wk", mhalf, i)],
                          writes=[PSK[bank]], mm=True)
                    E("scalar", (lambda e, g=g, h2=h2, bank=bank:
                                 e.activation(arena[:, KHF + g * 1024 + h2 * 512:
                                                    KHF + g * 1024 + (h2 + 1) * 512],
                                              pb[bank][:, 0:512], AF.Identity,
                                              bias=smalls["bk_col"][:, g:g + 1])),
                      reads=[PSK[bank], K_SM("bk_col")], writes=[("khf", mhalf, g, h2)],
                      conflicts=kold)
            ap2old = claim(WTMP, WTMP + 2048, [("attp2", p_) for p_ in range(4)])
            attp2 = arena[:, WTMP:WTMP + 2048]

            def emit_ctx(tl, mhalf, buf, bkey):
                for g in range(8):
                    E("tensor", (lambda e, g=g, tl=tl, mhalf=mhalf, buf=buf:
                                 e.matmul(pb[4 + g // 2][:, (g % 2) * 256:
                                                        (g % 2) * 256 + 256],
                                          arena[:, VHF + tl * 1024 + g * 128:
                                                VHF + tl * 1024 + (g + 1) * 128],
                                          buf[:, g * 256:(g + 1) * 256],
                                          start=False, stop=(tl == 7),
                                          skip_group_check=True)),
                      reads=[(bkey, g // 2), ("vtok", mhalf, tl, g // 4)],
                      writes=[PSK[4 + g // 2]], mm=True)

            for qc in range(8):
                for g in range(8):
                    bank = g % 2
                    for i in range(NCH):
                        E("tensor", (lambda e, g=g, i=i, bank=bank, qc=qc:
                                     e.matmul(pb[bank][:, 0:256],
                                              arena[:, WQR + i * 1024 + g * 128:
                                                    WQR + i * 1024 + (g + 1) * 128],
                                              xn[:, i * T + qc * 256: i * T + (qc + 1) * 256],
                                              start=(i == 0), stop=(i == NCH - 1),
                                              skip_group_check=True)),
                          reads=[("xn", i, qc // 2), ("wq", i)], writes=[PSK[bank]],
                          mm=True)
                    E("scalar", (lambda e, g=g, bank=bank:
                                 e.activation(attq[:, g * 256:(g + 1) * 256],
                                              pb[bank][:, 0:256], AF.Identity,
                                              bias=smalls["bq_col"][:, g:g + 1])),
                      reads=[PSK[bank], K_SM("bq_col")], writes=[("attq", g)],
                      conflicts=[("attq", g)])
                for bank in (4, 5, 6, 7):
                    E("tensor", (lambda e, bank=bank: e.matmul(
                        pb[bank][:, 0:512], zrow_b[0:1, 0:128], zrow_b[0:1, 0:512],
                        start=True, stop=True)),
                      reads=KC(zrow_b), writes=[PSK[bank]], mm=True)
                for tl in range(8):
                    buf = attp if tl % 2 == 0 else attp2
                    bkey = "attp" if tl % 2 == 0 else "attp2"
                    for pair in range(4):
                        bank = 2 + pair % 2
                        for sub in range(2):
                            g = pair * 2 + sub
                            E("tensor", (lambda e, g=g, sub=sub, bank=bank, tl=tl,
                                         mhalf=mhalf:
                                         e.matmul(pb[bank][:, sub * 256:(sub + 1) * 256],
                                                  arena[:, KHF + g * 1024 + tl * 128:
                                                        KHF + g * 1024 + (tl + 1) * 128],
                                                  attq[:, g * 256:(g + 1) * 256],
                                                  start=True, stop=True,
                                                  skip_group_check=True)),
                              reads=[("khf", mhalf, g, tl // 4), ("attq", g)],
                              writes=[PSK[bank]], mm=True)
                        E("scalar", (lambda e, pair=pair, bank=bank, buf=buf:
                                     e.activation(buf[:, pair * 512:(pair + 1) * 512],
                                                  pb[bank][:, 0:512], AF.Exp,
                                                  scale=1.0 / 32.0)),
                          reads=[PSK[bank]], writes=[(bkey, pair)],
                          conflicts=(list(ap2old) if (qc == 0 and tl == 1 and pair == 0)
                                     else ()))
                    E("vector", (lambda e, buf=buf: e.tensor_add(
                        pt[1][:, 0:256], buf[:, 0:256], buf[:, 256:512])),
                      reads=[(bkey, 0)], writes=[("d01",)])
                    E("vector", (lambda e, buf=buf: e.tensor_add(
                        pt[1][:, 256:512], buf[:, 512:768], buf[:, 768:1024])),
                      reads=[(bkey, 1)], writes=[("d23",)])
                    E("vector", (lambda e, buf=buf: e.tensor_add(
                        pt[0][:, 256:512], buf[:, 1024:1280], buf[:, 1280:1536])),
                      reads=[(bkey, 2)], writes=[("d45",)])
                    E("vector", (lambda e, buf=buf: e.tensor_add(
                        den[:, 256:512], buf[:, 1536:1792], buf[:, 1792:2048])),
                      reads=[(bkey, 3)], writes=[("d67",)])
                    E("vector", (lambda e: e.tensor_add(pt[1][:, 0:256], pt[1][:, 0:256],
                                                        pt[1][:, 256:512])),
                      reads=[("d01",), ("d23",)], writes=[("d01",)])
                    E("vector", (lambda e: e.tensor_add(den[:, 0:256], pt[0][:, 256:512],
                                                        den[:, 256:512])),
                      reads=[("d45",), ("d67",)], writes=[("den",)])
                    E("vector", (lambda e: e.tensor_add(den[:, 0:256], den[:, 0:256],
                                                        pt[1][:, 0:256])),
                      reads=[("den",), ("d01",)], writes=[("den",)])
                    E("vector", (lambda e: e.reciprocal(stat_m[:, 0:256], den[:, 0:256])),
                      reads=[("den",)], writes=[("recm",)])
                    E("vector", (lambda e: e.tensor_copy(pt[0][:, 0:256],
                                                         stat_m[:, 0:256])),
                      reads=[("recm",)], writes=[("rec16",)])
                    for g in range(8):
                        E("vector", (lambda e, g=g, buf=buf: e.tensor_mul(
                            buf[:, g * 256:(g + 1) * 256],
                            buf[:, g * 256:(g + 1) * 256], pt[0][:, 0:256])),
                          reads=[(bkey, g // 2), ("rec16",)], writes=[(bkey, g // 2)])
                    if tl >= 1:
                        pbuf = attp if (tl - 1) % 2 == 0 else attp2
                        pkey = "attp" if (tl - 1) % 2 == 0 else "attp2"
                        emit_ctx(tl - 1, mhalf, pbuf, pkey)
                emit_ctx(7, mhalf, attp2, "attp2")
                for g in range(8):
                    E("vector", (lambda e, g=g, qc=qc:
                                 e.tensor_add(x1[:, g * T + qc * 256:
                                                 g * T + (qc + 1) * 256],
                                              x1[:, g * T + qc * 256:
                                                 g * T + (qc + 1) * 256],
                                              pb[4 + g // 2][:, (g % 2) * 256:
                                                             (g % 2) * 256 + 256])),
                      reads=[PSK[4 + g // 2], ("x1", g)], writes=[("x1", g)])

    if STAGES["l1a"]:
        at_sb = sb("at_sb", (128, 512), BF16)
        h_sb = sb("h_sb", (128, 4 * 256), BF16)
        zero2 = sb("zero2", (128, 2), BF16)
        DTC = 0
        BMC = 8192
        CTC = 10240
        XBF = 12288
        W1S = [16384, 20480]
        W2S = [24576, 25600]
        STB = 28672
        Y1A = 32768
        YPR = 34816

        DRAIN(emit_ln(x1, lambda i, qc: ("x1", i), xn, lambda i, qc: ("xn1", i, qc), T))

        E("gpsimd", (lambda e: e.memset(zero2[:, :], 0.0)), writes=[("zero2",)])
        dold = claim(DTC, DTC + 8192, [("dT", i) for i in range(NCH)])
        for i in range(NCH):
            E("gpsimd", (lambda e, i=i: e.dma_start(arena[:, DTC + i * 1024: DTC + (i + 1) * 1024],
                                                    dt["dT"][i * 128:(i + 1) * 128, :])),
              writes=[("dT", i)], conflicts=dold, dma_sem="dw")
        bold = claim(BMC, BMC + 2048, [("bmT", i) for i in range(NCH)])
        for i in range(NCH):
            E("gpsimd", (lambda e, i=i: e.dma_start(arena[:, BMC + i * 256: BMC + (i + 1) * 256],
                                                    dt["bmT"][i * 128:(i + 1) * 128, :])),
              writes=[("bmT", i)], conflicts=bold, dma_sem="dw")
        cold = claim(CTC, CTC + 2048, [("cT", j) for j in range(2)])
        for j in range(2):
            E("gpsimd", (lambda e, j=j: e.dma_start(arena[:, CTC + j * 1024: CTC + (j + 1) * 1024],
                                                    dt["cT"][j * 128:(j + 1) * 128, :])),
              writes=[("cT", j)], conflicts=cold, dma_sem="dw")
        for idx, (i, j) in enumerate(((0, 0), (0, 1), (1, 0), (1, 1))):
            E("gpsimd", (lambda e, idx=idx, i=i, j=j:
                         e.dma_start(at_sb[:, (2 * i + j) * 128:(2 * i + j + 1) * 128],
                                     dt["aT"][i * 128:(i + 1) * 128, j * 128:(j + 1) * 128])),
              writes=[("at", i, j)], dma_sem="dw")

        stold = claim(STB, STB + 4096, [("streg",)])
        E("gpsimd", (lambda e: e.memset(arena[:, STB:STB + 4096], 0.0)),
          writes=[("streg",)], conflicts=stold)
        y1old = claim(Y1A, Y1A + 2048, [("y1reg",)])
        E("gpsimd", (lambda e: e.memset(arena[:, Y1A:Y1A + 2048], 0.0)),
          writes=[("y1reg",)], conflicts=y1old)
        ypold = claim(YPR, YPR + 2048, [("ypreg",)])
        E("gpsimd", (lambda e: e.memset(arena[:, YPR:YPR + 2048], 0.0)),
          writes=[("ypreg",)], conflicts=ypold)

        xbold = claim(XBF, XBF + 4096, [("xB", sc, tc) for sc in range(2) for tc in range(4)])
        for sc in range(2):
            for tc in range(4):
                bank = 4 + tc % 2
                for i in range(NCH):
                    E("tensor", (lambda e, sc=sc, tc=tc, i=i, bank=bank:
                                 e.matmul(pb[bank][:, 0:512],
                                          arena[:, BMC + i * 256 + sc * 128: BMC + i * 256 + (sc + 1) * 128],
                                          xn[:, i * T + tc * 512: i * T + (tc + 1) * 512],
                                          start=(i == 0), stop=(i == NCH - 1),
                                          skip_group_check=True)),
                      reads=[("xn1", i, tc), ("bmT", i)], writes=[PSK[bank]], mm=True)
                E("vector", (lambda e, sc=sc, tc=tc, bank=bank:
                             e.tensor_copy(arena[:, XBF + sc * T + tc * 512: XBF + sc * T + (tc + 1) * 512],
                                           pb[bank][:, 0:512])),
                  reads=[PSK[bank]], writes=[("xB", sc, tc)], conflicts=xbold)

        RUN = 256
        n_runs = T // RUN

        def emit_scan_bank_prep(r):
            bank = 2 + r % 2
            E("tensor", (lambda e, bank=bank: e.matmul(pb[bank][:, 0:512], zrow_b[0:1, 0:128],
                                                       zrow_b[0:1, 0:512], start=True, stop=True)),
              reads=KC(zrow_b), writes=[PSK[bank]], mm=True)
            for j in range(2):
                E("vector", (lambda e, r=r, j=j, bank=bank:
                             e.tensor_copy(pb[bank][:, j:512:2],
                                           arena[:, XBF + j * T + r * RUN: XBF + j * T + (r + 1) * RUN])),
                  reads=[("xB", j, (r * RUN) // 512)], writes=[PSK[bank]])

        def cast_gen():
            slot = [0]

            def pair(src_ap_fn, dst_ap_fn, wkey):
                sl2 = slot[0] % 4
                slot[0] += 1
                E("gpsimd", (lambda e, sl2=sl2, src_ap_fn=src_ap_fn: e.dma_start(
                    attq[:, sl2 * 512:(sl2 + 1) * 512], src_ap_fn())),
                  writes=[("cst", sl2)],
                  conflicts=[("cst", sl2)] + ([("attq", 2 * sl2), ("attq", 2 * sl2 + 1)]
                                              if slot[0] <= 4 else []),
                  dma_sem="dci" + str(sl2))
                E("sync", (lambda e, sl2=sl2, dst_ap_fn=dst_ap_fn: e.dma_start(
                    dst_ap_fn(), attq[:, sl2 * 512:(sl2 + 1) * 512])),
                  reads=[("cst", sl2)], writes=[wkey], dma_sem="dco" + str(sl2))
            for f in range(8):
                for i in range(8):
                    pair((lambda f=f, i=i: dt["w1T"][i * 128:(i + 1) * 128,
                                                     f * 512:(f + 1) * 512]),
                         (lambda f=f, i=i: dt["hb_w1b"][i * 128:(i + 1) * 128,
                                                        f * 512:(f + 1) * 512]),
                         ("hw1c", f, i))
                    yield
            for rc in range(32):
                for hf in range(2):
                    pair((lambda rc=rc, hf=hf: dt["w2T"][rc * 128:(rc + 1) * 128,
                                                         hf * 512:(hf + 1) * 512]),
                         (lambda rc=rc, hf=hf: dt["hb_w2b"][rc * 128:(rc + 1) * 128,
                                                            hf * 512:(hf + 1) * 512]),
                         ("hw2c", rc, hf))
                    yield
            for rc in range(8):
                for hf in range(2):
                    pair((lambda rc=rc, hf=hf: dt["opw"][rc * 128:(rc + 1) * 128,
                                                         hf * 512:(hf + 1) * 512]),
                         (lambda rc=rc, hf=hf: dt["hb_opb"][rc * 128:(rc + 1) * 128,
                                                            hf * 512:(hf + 1) * 512]),
                         ("hopc", rc, hf))
                    yield

        def win_outs_gen(w):
            t0 = w * 256
            for oc in range(NCH):
                bank = oc % 2
                for i in range(NCH):
                    E("tensor", (lambda e, oc=oc, i=i, bank=bank, t0=t0:
                                 e.matmul(pb[bank][:, 0:256],
                                          arena[:, DTC + i * 1024 + oc * 128: DTC + i * 1024 + (oc + 1) * 128],
                                          xn[:, i * T + t0: i * T + t0 + 256],
                                          start=(i == 0), stop=False, skip_group_check=True)),
                      reads=[("xn1", i, t0 // 512), ("dT", i)], writes=[PSK[bank]], mm=True)
                    if i % 3 == 2:
                        yield
                for j in range(2):
                    E("tensor", (lambda e, oc=oc, j=j, bank=bank, t0=t0:
                                 e.matmul(pb[bank][:, 0:256],
                                          arena[:, CTC + j * 1024 + oc * 128: CTC + j * 1024 + (oc + 1) * 128],
                                          arena[:, STB + j * T + t0: STB + j * T + t0 + 256],
                                          start=False, stop=(j == 1), skip_group_check=True)),
                      reads=[("st", t0 + 255, j), ("cT", j)], writes=[PSK[bank]], mm=True)
                E("vector", (lambda e, oc=oc, bank=bank, t0=t0:
                             e.tensor_add(arena[:, YPR + oc * 256: YPR + (oc + 1) * 256],
                                          pb[bank][:, 0:256],
                                          xn[:, oc * T + t0: oc * T + t0 + 256])),
                  reads=[PSK[bank], ("xn1", oc, t0 // 512), ("ypreg",)],
                  writes=[("ypre", w, oc)],
                  conflicts=([("ypre", w - 1, oc)] if w >= 1 else []))
                yield
            yield from emit_ln(arena[:, YPR:YPR + 2048], lambda i, qc, w=w: ("ypre", w, i),
                               arena[:, Y1A:Y1A + 2048], lambda i, qc, w=w: ("y1a", w, i),
                               256, "lnw_col", "lnb_col", onescol=onesc_b,
                               dst_extra=[("y1reg",)],
                               dst_conflicts=([("y1a", w - 1, i_) for i_ in range(NCH)]
                                              if w >= 1 else []))

        def win_ffn_gen(w):
            t0 = w * 256
            for bk in range(4, 8):
                E("tensor", (lambda e, bk=bk: e.matmul(pb[bk][:, 0:512], zrow_b[0:1, 0:128],
                                                       zrow_b[0:1, 0:512], start=True, stop=True)),
                  reads=KC(zrow_b), writes=[PSK[bk]], mm=True)
            yield
            for f in range(NCH):
                ws = f % 2
                w1keys = [("w1b", ws, i) for i in range(NCH)]
                w1old = claim(W1S[ws], W1S[ws] + 4096, w1keys)
                for i in range(NCH):
                    E("sync", (lambda e, f=f, i=i, ws=ws:
                                 e.dma_start(arena[:, W1S[ws] + i * 512: W1S[ws] + (i + 1) * 512],
                                             dt["hb_w1b"][i * 128:(i + 1) * 128, f * 512:(f + 1) * 512])),
                      reads=[("hw1c", f, i)],
                      writes=[("w1b", ws, i)], conflicts=w1old, dma_sem="dw1" + str(ws))
                yield
                for sub in range(4):
                    ws2 = (f * 4 + sub) % 2
                    w2old = claim(W2S[ws2], W2S[ws2] + 1024, [("w2b", ws2)])
                    E("sync", (lambda e, f=f, sub=sub, ws2=ws2:
                                 e.dma_start(arena[:, W2S[ws2]: W2S[ws2] + 1024],
                                             dt["hb_w2b"][f * 512 + sub * 128: f * 512 + (sub + 1) * 128, :])),
                      reads=[("hw2c", f * 4 + sub, 0), ("hw2c", f * 4 + sub, 1)],
                      writes=[("w2b", ws2)], conflicts=[("w2b", ws2)] + list(w2old),
                      dma_sem="dw2" + str(ws2))
                    bank = sub % 2
                    for i in range(NCH):
                        E("tensor", (lambda e, sub=sub, i=i, bank=bank, ws=ws:
                                     e.matmul(pb[bank][:, 0:256],
                                              arena[:, W1S[ws] + i * 512 + sub * 128: W1S[ws] + i * 512 + (sub + 1) * 128],
                                              arena[:, Y1A + i * 256: Y1A + (i + 1) * 256],
                                              start=(i == 0), stop=(i == NCH - 1),
                                              skip_group_check=True)),
                          reads=[("y1a", w, i), ("w1b", ws, i)], writes=[PSK[bank]], mm=True)
                        if i % 3 == 2:
                            yield
                    E("scalar", (lambda e, f=f, sub=sub, bank=bank:
                                 e.activation(h_sb[:, sub * 256:(sub + 1) * 256], pb[bank][:, 0:256],
                                              AF.Gelu, bias=smalls["b1_col"][:, f * 4 + sub: f * 4 + sub + 1])),
                      reads=[PSK[bank], K_SM("b1_col")], writes=[("h", sub)])
                    yield
                    for oc in range(NCH):
                        E("tensor", (lambda e, f=f, sub=sub, oc=oc, ws2=ws2:
                                     e.matmul(pb[4 + oc // 2][:, (oc % 2) * 256:(oc % 2) * 256 + 256],
                                              arena[:, W2S[ws2] + oc * 128: W2S[ws2] + (oc + 1) * 128],
                                              h_sb[:, sub * 256:(sub + 1) * 256],
                                              start=False, stop=(f == NCH - 1 and sub == 3),
                                              skip_group_check=True)),
                          reads=[("h", sub), ("w2b", ws2)], writes=[PSK[4 + oc // 2]], mm=True)
                        if oc % 3 == 2:
                            yield
                    yield
            for oc in range(NCH):
                E("vector", (lambda e, oc=oc:
                             e.tensor_scalar(vsc[0][:, 0:256],
                                             pb[4 + oc // 2][:, (oc % 2) * 256:(oc % 2) * 256 + 256],
                                             smalls["b2_col"][:, oc:oc + 1], None, ALU.add)),
                  reads=[PSK[4 + oc // 2], K_SM("b2_col")], writes=[("vsc", id(vsc[0]))])
                E("vector", (lambda e, oc=oc, w=w:
                             e.tensor_add(vsc[1][:, 0:256], vsc[0][:, 0:256],
                                          arena[:, Y1A + oc * 256: Y1A + (oc + 1) * 256])),
                  reads=[("vsc", id(vsc[0])), ("y1a", w, oc)], writes=[("vsc", id(vsc[1]))])
                E("vector", (lambda e, oc=oc, t0=t0:
                             e.tensor_add(x1[:, oc * T + t0: oc * T + t0 + 256],
                                          x1[:, oc * T + t0: oc * T + t0 + 256],
                                          vsc[1][:, 0:256])),
                  reads=[("vsc", id(vsc[1])), ("x1", oc)], writes=[("x1", oc)])
                yield

        import collections as _coll
        fillers = _coll.deque()

        def pump(k=1):
            done = 0
            while fillers and done < k:
                try:
                    next(fillers[0])
                    done += 1
                except StopIteration:
                    fillers.popleft()

        emit_scan_bank_prep(0)
        emit_scan_bank_prep(1)
        for step in range(T):
            r, u = divmod(step, RUN)
            if u == 0:
                if r == 0:
                    fillers.append(cast_gen())
                if r >= 2:
                    fillers.append(win_ffn_gen(r - 2))
                if r >= 1:
                    fillers.append(win_outs_gen(r - 1))
            bank = 2 + r % 2
            for (jj, oj) in ((0, 0), (1, 0), (0, 1), (1, 1)):
                if step == 0:
                    rhs_fn = (lambda e, jj=jj: zero2[:, jj:jj + 1])
                    rkey = ("zero2",)
                else:
                    rhs_fn = (lambda e, jj=jj, step=step: arena[:, STB + jj * T + step - 1: STB + jj * T + step])
                    rkey = ("st", step - 1, jj)
                E("tensor", (lambda e, u=u, bank=bank, jj=jj, oj=oj, rhs_fn=rhs_fn:
                             e.matmul(pb[bank][:, 2 * u + oj:2 * u + oj + 1],
                                      at_sb[:, (2 * jj + oj) * 128:(2 * jj + oj + 1) * 128],
                                      rhs_fn(e),
                                      start=False, stop=(jj == 1), skip_group_check=True)),
                  reads=[rkey, ("at", jj, oj)], writes=[PSK[bank]], mm=True)
            E("scalar", (lambda e, step=step, u=u, bank=bank:
                         e.activation(arena[:, STB + step: STB + 2 * T: T],
                                      pb[bank][:, 2 * u:2 * u + 2], AF.Tanh)),
              reads=[PSK[bank], ("streg",)], writes=[("st", step, 0), ("st", step, 1)])
            if u == 0 and 2 <= r + 1 < n_runs:
                emit_scan_bank_prep(r + 1)
            pump(1)
        fillers.append(win_ffn_gen(6))
        fillers.append(win_outs_gen(7))
        fillers.append(win_ffn_gen(7))
        while fillers:
            pump(1000)

    if STAGES["l1f"]:
        selc = sb("sb_selc", (128, NCORES), F32)
        ipbc = sb("sb_ipbc", (128, 3 * NCH), F32)
        opwb0 = attq[:, 0:1024]
        ptile2 = sb("ptile2", (128, 6 * 128), BF16)
        mrec = recip_sb[:, 0:128]
        ohsb = sb("ohsb", (128, 16 * 8), BF16)
        sgk = sb("sgk", (128, 16 * 4), F32)
        mx8 = sb("mx8", (128, 8), F32)
        ix8 = sb("ix8", (128, 8), mybir.dt.uint32)
        gsc = sb("gsc", (128, 8), F32)
        priors = sb("priors", (1, 8), F32)
        cntall = sb("cntall", (64, 8), F32)
        m64sb = sb("sb_m64", (64, 8), F32)
        iota8r = sb("sb_iota8r", (128, 8), F32)
        sbst = x1[:, 4096:6144]

        for nm, buf in [("selcol", selc), ("ipb_col", ipbc), ("m64", m64sb),
                        ("iota8r", iota8r)]:
            if buf.dtype == F32:
                E("sync", (lambda e, nm=nm, buf=buf: e.dma_start(buf[:, :], dt[nm][:, :])),
                  writes=[("small", nm)], dma_sem="dsm2")
            else:
                E("gpsimd", (lambda e, nm=nm, buf=buf: e.dma_start(buf[:, :], dt[nm][:, :])),
                  writes=[("small", nm)], dma_sem="dsb2")

        for i in range(NCH):
            E("sync", (lambda e, i=i: e.dma_start(dt["ag_xn1_in"][i * 128:(i + 1) * 128, :],
                                                  xn[:, i * T:(i + 1) * T])),
              reads=[("xn1", i, 0), ("xn1", i, 1), ("xn1", i, 2), ("xn1", i, 3)],
              writes=[("agin", i)], dma_sem="dagi")
        E("gpsimd", (lambda e: e.collective_compute(
            "AllGather", ALU.bypass, replica_groups=[list(range(NCORES))],
            ins=[dt["ag_xn1_in"].opt()], outs=[dt["ag_xn1_out"].opt()])),
          reads=[("agin", i) for i in range(NCH)], writes=[("ago_xn1",)], cc=True,
          dma_sem="ccx")
        for i in range(NCH):
            E("gpsimd", (lambda e, i=i: e.dma_start(dt["ag_s_in"][i * 128:(i + 1) * 128, :],
                                                    x1[:, i * T:(i + 1) * T])),
              reads=[("x1", i)], writes=[("agsin", i)], dma_sem="dags")
        E("gpsimd", (lambda e: e.collective_compute(
            "AllGather", ALU.bypass, replica_groups=[list(range(NCORES))],
            ins=[dt["ag_s_in"].opt()], outs=[dt["ag_s_out"].opt()])),
          reads=[("agsin", i) for i in range(NCH)], writes=[("ago_s",)], cc=True,
          dma_sem="ccs")

        fr = x1[:, 8192:9984].bitcast(BF16)
        for bh in range(2):
            for fch in range(32):
                s1 = fch % 2
                E("gpsimd", (lambda e, fch=fch, s1=s1: e.dma_start(
                    fr[:, s1 * 1024:(s1 + 1) * 1024],
                    dt["myw1T"][fch * 128:(fch + 1) * 128, :])),
                  writes=[("fw1", s1)],
                  conflicts=[("fw1", s1)] + ([("x1", 4)] if (bh == 0 and fch < 2) else []),
                  dma_sem="dfw" + str(s1))
                E("gpsimd", (lambda e, fch=fch, s1=s1, bh=bh: e.dma_start(
                    fr[:, 2048 + s1 * 512: 2048 + (s1 + 1) * 512],
                    dt["myw2"][fch * 128:(fch + 1) * 128, bh * 512:(bh + 1) * 512])),
                  writes=[("fw2", s1)], conflicts=[("fw2", s1)], dma_sem="dfv" + str(s1))
                for a in range(8):
                    E("tensor", (lambda e, s1=s1, a=a, fch=fch:
                                 e.matmul(pb[a][:, 0:512],
                                          fr[:, s1 * 1024 + a * 128:
                                             s1 * 1024 + (a + 1) * 128],
                                          fr[:, 2048 + s1 * 512: 2048 + (s1 + 1) * 512],
                                          start=(fch == 0), stop=(fch == 31),
                                          skip_group_check=True)),
                      reads=[("fw1", s1), ("fw2", s1)], writes=[PSK[a]], mm=True)
            for a in range(8):
                E("scalar", (lambda e, a=a: e.copy(fr[:, 3072:3584], pb[a][:, 0:512])),
                  reads=[PSK[a]], writes=[("fstg",)])
                E("sync", (lambda e, a=a, bh=bh: e.dma_start(
                    dt["ag_wf_in"][a * 128:(a + 1) * 128, bh * 512:(bh + 1) * 512],
                    fr[:, 3072:3584])),
                  reads=[("fstg",)], writes=[("wfin", bh, a)], dma_sem="dwfi")
        E("gpsimd", (lambda e: e.dma_start(ohsb[:, 0:32], dt["myb1c"][:, :])),
          writes=[("b1c",)], dma_sem="dsb2")
        for bh in range(2):
            for fch in range(32):
                s1 = fch % 2
                E("gpsimd", (lambda e, fch=fch, s1=s1, bh=bh: e.dma_start(
                    fr[:, 2048 + s1 * 512: 2048 + (s1 + 1) * 512],
                    dt["myw2"][fch * 128:(fch + 1) * 128, bh * 512:(bh + 1) * 512])),
                  writes=[("fw2", s1)], conflicts=[("fw2", s1)], dma_sem="dfv" + str(s1))
                E("tensor", (lambda e, fch=fch, s1=s1:
                             e.matmul(pb[2][0:1, 0:512], ohsb[:, fch:fch + 1],
                                      fr[:, 2048 + s1 * 512: 2048 + (s1 + 1) * 512],
                                      start=(fch == 0), stop=False,
                                      skip_group_check=True)),
                  reads=[("b1c",), ("fw2", s1)], writes=[PSK[2]], mm=True)
            E("gpsimd", (lambda e, bh=bh: e.dma_start(
                pt[0][0:1, 0:512], dt["myb2r"][0:1, bh * 512:(bh + 1) * 512])),
              writes=[("b2r",)], conflicts=[("b2r",), ("rec16",), ("d45",)],
              dma_sem="db2r")
            E("tensor", (lambda e: e.matmul(pb[2][0:1, 0:512], onesr_b[0:1, 0:1],
                                            pt[0][0:1, 0:512], start=False, stop=True,
                                            skip_group_check=True)),
              reads=[("b2r",)] + KC(onesr_b), writes=[PSK[2]], mm=True)
            E("scalar", (lambda e: e.copy(fr[0:1, 3072:3584], pb[2][0:1, 0:512])),
              reads=[PSK[2]], writes=[("fstg",)])
            E("sync", (lambda e, bh=bh: e.dma_start(
                dt["ag_wf_in"][1024:1025, bh * 512:(bh + 1) * 512],
                fr[0:1, 3072:3584])),
              reads=[("fstg",)], writes=[("wfinb", bh)], dma_sem="dwfi")
        E("gpsimd", (lambda e: e.collective_compute(
            "AllGather", ALU.bypass, replica_groups=[list(range(NCORES))],
            ins=[dt["ag_wf_in"].opt()], outs=[dt["ag_wf_out"].opt()])),
          reads=[("wfin", bh, a) for bh in range(2) for a in range(8)]
                + [("wfinb", bh) for bh in range(2)],
          writes=[("ago_wf",)], cc=True, dma_sem="ccw")

        selold = claim(0, 8192, [("cselscr",)])
        for i in range(NCH):
            for hh in range(2):
                E("gpsimd", (lambda e, i=i, hh=hh: e.dma_start(
                    arena[:, 0:8192].rearrange("p (b t) -> p b t", b=4),
                    dt["ag_xn1_out"][hh * 4:(hh + 1) * 4, i * 128:(i + 1) * 128, :]
                        .transpose([1, 0, 2]))),
                  reads=[("ago_xn1",)], writes=[("cselscr",)], dma_sem="dsel",
                  conflicts=(selold if (i == 0 and hh == 0) else ()))
                for w in range(NCORES):
                    mkdst = (lambda e, i=i, hh=hh:
                             xn[:, i * 2048: (i + 1) * 2048]
                             .rearrange("p (t b) -> p t b", b=8)
                             [:, :, hh * 4:(hh + 1) * 4])
                    mksrc = (lambda e, w=w: arena[:, 0:8192]
                             .rearrange("p (b t) -> p t b", b=4)
                             [:, w * 256:(w + 1) * 256, :])
                    if w == 0:
                        E("vector", (lambda e, w=w, mkdst=mkdst, mksrc=mksrc:
                                     e.tensor_scalar(mkdst(e), mksrc(e), selc[:, w:w + 1],
                                                     None, ALU.mult)),
                          reads=[("cselscr",), ("small", "selcol")],
                          writes=[("xn1t", i, hh)],
                          conflicts=[("xn1", i, q_) for q_ in range(4)])
                    else:
                        E("vector", (lambda e, w=w, mkdst=mkdst, mksrc=mksrc:
                                     e.scalar_tensor_tensor(mkdst(e), mksrc(e),
                                                            selc[:, w:w + 1], mkdst(e),
                                                            ALU.mult, ALU.add)),
                          reads=[("cselscr",), ("small", "selcol"), ("xn1t", i, hh)],
                          writes=[("xn1t", i, hh)])
        XNT = lambda i: [("xn1t", i, 0), ("xn1t", i, 1)]

        def xslab(e, i, t0, n=64):
            sl = t0 // 64
            return xn[:, i * 2048 + sl * 512: i * 2048 + (sl + 1) * 512]

        ipold = claim(0, 24576, [("ipw", i) for i in range(NCH)])
        for i in range(NCH):
            E("gpsimd", (lambda e, i=i: e.dma_start(arena[:, i * 3072:(i + 1) * 3072],
                                                    dt["ipw"][i * 128:(i + 1) * 128, :])),
              writes=[("ipw", i)], conflicts=ipold, dma_sem="dw")
        QS, KS, VS = 24576, 28672, 32768
        opw_last = {}
        ypsl = arena[:, 32768:36864]
        y1sl = arena[:, 28672:32768]
        qold = claim(QS, QS + 4096, [("qsb", j) for j in range(8)])
        kold = claim(KS, KS + 4096, [("ksb", j) for j in range(8)])
        vold2 = claim(VS, VS + 4096, [("vsb", j, hf) for j in range(4) for hf in range(2)])

        ypsl = x1[:, 10240:12288].bitcast(BF16)
        y1sl = x1[:, 12288:14336].bitcast(BF16)

        def c1_qkv_gen(sl_):
            for och in range(16):
                bank = och % 2
                dstoff = QS + och * 512 if och < 8 else KS + (och - 8) * 512
                for i in range(NCH):
                    E("tensor", (lambda e, och=och, i=i, bank=bank, sl_=sl_:
                                 e.matmul(pb[bank][:, 0:512],
                                          arena[:, i * 3072 + och * 128: i * 3072 + (och + 1) * 128],
                                          xn[:, i * 2048 + sl_ * 512:
                                             i * 2048 + (sl_ + 1) * 512],
                                          start=(i == 0), stop=(i == NCH - 1),
                                          skip_group_check=True)),
                      reads=XNT(i) + [("ipw", i)], writes=[PSK[bank]], mm=True)
                    if i % 3 == 2:
                        yield
                E("scalar", (lambda e, och=och, bank=bank, dstoff=dstoff:
                             e.activation(arena[:, dstoff:dstoff + 512], pb[bank][:, 0:512],
                                          AF.Identity, bias=ipbc[:, och:och + 1])),
                  reads=[PSK[bank], ("small", "ipb_col")],
                  writes=[("qsb", och) if och < 8 else ("ksb", och - 8)],
                  conflicts=((qold if sl_ == 0 else []) if och == 0 else
                             ((kold if sl_ == 0 else []) if och == 8 else []))
                            + ([("ctx", och, nb_) for nb_ in range(4)] if och < 8 else []))
                yield
            for vt in range(4):
                for half in range(2):
                    bank = half
                    for i in range(NCH):
                        E("tensor", (lambda e, vt=vt, half=half, i=i, bank=bank, sl_=sl_:
                                     e.matmul(pb[bank][:, 0:512],
                                              xn[:, i * 2048 + sl_ * 512 + vt * 128:
                                                 i * 2048 + sl_ * 512 + (vt + 1) * 128],
                                              arena[:, i * 3072 + 2048 + half * 512:
                                                    i * 3072 + 2048 + (half + 1) * 512],
                                              start=(i == 0), stop=False,
                                              skip_group_check=True)),
                          reads=XNT(i) + [("ipw", i)], writes=[PSK[bank]], mm=True)
                        if i % 3 == 2:
                            yield
                    E("tensor", (lambda e, half=half, bank=bank:
                                 e.matmul(pb[bank][:, 0:512], onesr_b[0:1, :],
                                          smallsb["ipb_row"][0:1, half * 512:
                                                             (half + 1) * 512],
                                          start=False, stop=True, skip_group_check=True)),
                      reads=[K_SM("ipb_row")] + KC(onesr_b), writes=[PSK[bank]], mm=True)
                    E("scalar", (lambda e, vt=vt, half=half, bank=bank:
                                 e.copy(arena[:, VS + vt * 1024 + half * 512:
                                              VS + vt * 1024 + (half + 1) * 512],
                                        pb[bank][:, 0:512])),
                      reads=[PSK[bank]], writes=[("vsb", vt, half)],
                      conflicts=(vold2 if (sl_ == 0 and vt == 0 and half == 0) else ()))
                    yield

        def c1_tail_gen(sl_):
            t0 = sl_ * 64
            for th in range(2):
                for bk in range(4, 8):
                    E("tensor", (lambda e, bk=bk: e.matmul(
                        pb[bk][:, 0:512], zrow_b[0:1, 0:128], zrow_b[0:1, 0:512],
                        start=True, stop=True)),
                      reads=KC(zrow_b), writes=[PSK[bk]], mm=True)
                for i in range(NCH):
                    ws3 = i % 2
                    wb = opwb0 if ws3 == 0 else h_sb
                    prevk = opw_last.get(ws3)
                    E("sync", (lambda e, i=i, wb=wb, th=th:
                                 e.dma_start(wb[:, 0:1024], dt["hb_opb"][i * 128:(i + 1) * 128, :])),
                      reads=[("hopc", i, 0), ("hopc", i, 1)],
                      writes=[("opwb", sl_, th, i)],
                      conflicts=([prevk] if prevk else
                                 ([("cst", s_) for s_ in range(4)] if ws3 == 0 else [])),
                      dma_sem="dop" + str(ws3))
                    opw_last[ws3] = ("opwb", sl_, th, i)
                    for oc in range(NCH):
                        E("tensor", (lambda e, oc=oc, i=i, wb=wb, th=th:
                                     e.matmul(pb[4 + oc // 2][:, (oc % 2) * 256:
                                                             (oc % 2) * 256 + 256],
                                              wb[:, oc * 128:(oc + 1) * 128],
                                              arena[:, QS + i * 512 + th * 256:
                                                    QS + i * 512 + th * 256 + 256],
                                              start=False, stop=(i == NCH - 1),
                                              skip_group_check=True)),
                          reads=[("ctx", i, 0), ("ctx", i, 1), ("ctx", i, 2), ("ctx", i, 3),
                                 ("opwb", sl_, th, i)],
                          writes=[PSK[4 + oc // 2]], mm=True)
                for oc in range(NCH):
                    E("vector", (lambda e, oc=oc, t0=t0, th=th, sl_=sl_:
                                 e.tensor_add(vsc[0][:, 0:256],
                                              pb[4 + oc // 2][:, (oc % 2) * 256:
                                                              (oc % 2) * 256 + 256],
                                              xn[:, oc * 2048 + sl_ * 512 + th * 256:
                                                 oc * 2048 + sl_ * 512 + th * 256 + 256])),
                      reads=[PSK[4 + oc // 2]] + XNT(oc), writes=[("vsc", id(vsc[0]))])
                    E("vector", (lambda e, oc=oc, th=th:
                                 e.tensor_scalar(ypsl[:, oc * 512 + th * 256:
                                                     oc * 512 + th * 256 + 256],
                                                 vsc[0][:, 0:256],
                                                 smalls["opb_col"][:, oc:oc + 1],
                                                 None, ALU.add)),
                      reads=[("vsc", id(vsc[0])), K_SM("opb_col")], writes=[("ypsl", oc)],
                      conflicts=([("x1", 5)] if (sl_ == 0 and th == 0 and oc == 0) else ()))
                    if th == 1:
                        yield
            yield from emit_ln(ypsl, lambda i, qc: ("ypsl", i), y1sl,
                               lambda i, qc: ("y1sl", i), 512,
                               "ln1g_col", "ln1b_col", onescol=onesc_b,
                               dst_conflicts=([("x1", 6)] if sl_ == 0 else ()))
            for i in range(NCH):
                E("vector", (lambda e, i=i: e.tensor_copy(
                    pt[i % 2][:, 0:512].rearrange("p (b t) -> p b t", b=8),
                    y1sl[:, i * 512:(i + 1) * 512].rearrange("p (t b) -> p b t", b=8))),
                  reads=[("y1sl", i)], writes=[("y1stg", i % 2)],
                  conflicts=([("rec16",), ("d01",), ("d23",), ("d45",)]
                             if (sl_ == 0 and i < 2) else ()))
                E("gpsimd", (lambda e, i=i, t0=t0: e.dma_start(
                    dt["hb_y1"][i * 128:(i + 1) * 128, :]
                        .rearrange("p (b t) -> p b t", b=8)[:, :, t0:t0 + 64],
                    pt[i % 2][:, 0:512].rearrange("p (b t) -> p b t", b=8))),
                  reads=[("y1stg", i % 2)], writes=[("hby1", sl_, i)], dma_sem="dy1")
                yield

        def rr(g1, g2):
            gens = [g for g in (g1, g2) if g is not None]
            while gens:
                nxt = []
                for g in gens:
                    try:
                        next(g)
                        nxt.append(g)
                    except StopIteration:
                        pass
                gens = nxt

        def c1_soft_gen():
            for nb in range(4):
                E("tensor", (lambda e: e.matmul(pb[6][:, 0:512], zrow_b[0:1, 0:128],
                                                zrow_b[0:1, 0:512], start=True, stop=True)),
                  reads=KC(zrow_b), writes=[PSK[6]], mm=True)
                for h in range(4):
                    bank = 2 + h % 2
                    for ch in range(2):
                        E("tensor", (lambda e, nb=nb, h=h, ch=ch, bank=bank:
                                     e.matmul(pb[bank][:, 0:128],
                                              arena[:, KS + (2 * h + ch) * 512 + nb * 128:
                                                    KS + (2 * h + ch) * 512 + (nb + 1) * 128],
                                              arena[:, QS + (2 * h + ch) * 512 + nb * 128:
                                                    QS + (2 * h + ch) * 512 + (nb + 1) * 128],
                                              start=(ch == 0), stop=(ch == 1),
                                              skip_group_check=True)),
                          reads=[("qsb", 2 * h + ch), ("ksb", 2 * h + ch)],
                          writes=[PSK[bank]], mm=True)
                    E("vector", (lambda e, bank=bank:
                                 e.tensor_add(vsc[0][:, 0:128], pb[bank][:, 0:128],
                                              smallsb["mha_mask"][:, :])),
                      reads=[PSK[bank], K_SM("mha_mask")], writes=[("vsc", id(vsc[0]))])
                    E("scalar", (lambda e, h=h:
                                 e.activation(ptile2[:, 256 + h * 128: 256 + (h + 1) * 128],
                                              vsc[0][:, 0:128], AF.Exp, scale=1.0 / 16.0)),
                      reads=[("vsc", id(vsc[0]))], writes=[("pt2s", h)])
                    E("tensor", (lambda e, h=h:
                                 e.matmul(pb[6][32 * h:32 * h + 1, 0:128], onesc_b[:, 0:1],
                                          ptile2[:, 256 + h * 128: 256 + (h + 1) * 128],
                                          start=False, stop=True, skip_group_check=True,
                                          tile_position=(0, 32 * h))),
                      reads=[("pt2s", h)] + KC(onesc_b), writes=[PSK[6]], mm=True)
                yield
                E("vector", (lambda e: e.reciprocal(mrec[:, :], pb[6][:, 0:128])),
                  reads=[PSK[6]], writes=[("mrec",)])
                for h in range(4):
                    E("tensor", (lambda e, h=h: e.matmul(pb[7][:, 0:128],
                                                         onesr_f[32 * h:32 * h + 1, :],
                                                         mrec[32 * h:32 * h + 1, 0:128],
                                                         start=True, stop=True,
                                                         tile_position=(32 * h, 0))),
                      reads=[("mrec",)] + KC(onesr_f), writes=[PSK[7]], mm=True)
                    E("vector", (lambda e, h=h:
                                 e.tensor_mul(ptile2[:, (h % 2) * 128:(h % 2) * 128 + 128],
                                              ptile2[:, 256 + h * 128: 256 + (h + 1) * 128],
                                              pb[7][:, 0:128])),
                      reads=[("pt2s", h), PSK[7]], writes=[("pt2", h % 2)])
                    for ch in range(2):
                        bank = 4 + (2 * h + ch) // 4
                        col0 = ((2 * h + ch) % 4) * 128
                        E("tensor", (lambda e, nb=nb, h=h, ch=ch, bank=bank, col0=col0:
                                     e.matmul(pb[bank][:, col0:col0 + 128],
                                              arena[:, VS + nb * 1024 + h * 256 + ch * 128:
                                                    VS + nb * 1024 + h * 256 + (ch + 1) * 128],
                                              ptile2[:, (h % 2) * 128:(h % 2) * 128 + 128],
                                              start=True, stop=True)),
                          reads=[("pt2", h % 2), ("vsb", nb, (h * 2 + ch) // 4)],
                          writes=[PSK[bank]], mm=True)
                        E("scalar", (lambda e, nb=nb, h=h, ch=ch, bank=bank, col0=col0:
                                     e.copy(arena[:, QS + (2 * h + ch) * 512 + nb * 128:
                                                  QS + (2 * h + ch) * 512 + (nb + 1) * 128],
                                            pb[bank][:, col0:col0 + 128])),
                          reads=[PSK[bank]], writes=[("ctx", 2 * h + ch, nb)],
                          conflicts=[("qsb", 2 * h + ch)])
                    yield

        DRAIN(c1_qkv_gen(0))
        for sl_ in range(4):
            t0 = sl_ * 64
            DRAIN(c1_soft_gen())
            tail = c1_tail_gen(sl_)
            rr(tail, c1_qkv_gen(sl_ + 1) if sl_ < 3 else None)
        wgb = sb("wgb", (128, NCH * NEXP), BF16)
        slrow = recb[0:1, 0:128]
        for i in range(NCH):
            E("gpsimd", (lambda e, i=i: e.dma_start(wgb[:, i * 8:(i + 1) * 8],
                                                    dt["wg"][i * 128:(i + 1) * 128, :])),
              writes=[("wgb", i)], dma_sem="dwgb")
        for bk in (6, 7):
            E("tensor", (lambda e, bk=bk: e.matmul(pb[bk][:, 0:512], zrow_b[0:1, 0:128],
                                                   zrow_b[0:1, 0:512], start=True, stop=True)),
              reads=KC(zrow_b), writes=[PSK[bk]], mm=True)
        for tt in range(16):
            ws = tt % 2
            E("gpsimd", (lambda e, tt=tt, ws=ws: e.dma_start(
                y1tb[:, ws * 1024:(ws + 1) * 1024].rearrange("p (i c) -> p i c", i=8),
                dt["hb_y1"][:, tt * 128:(tt + 1) * 128].rearrange("(i p) c -> p i c", p=128))),
              reads=[("hby1", s_, i_) for s_ in range(4) for i_ in range(NCH)],
              writes=[("y1g", ws)],
              conflicts=[("y1g", ws)] + ([("attp", p_) for p_ in range(4)]
                                         if tt < 2 else []),
              dma_sem="dyg" + str(ws))
            bank = 2 + ws
            for i in range(NCH):
                E("tensor", (lambda e, tt=tt, ws=ws, i=i, bank=bank:
                             e.matmul(pb[bank][:, 0:8],
                                      y1tb[:, ws * 1024 + i * 128: ws * 1024 + (i + 1) * 128],
                                      wgb[:, i * 8:(i + 1) * 8],
                                      start=(i == 0), stop=(i == NCH - 1),
                                      skip_group_check=True)),
                  reads=[("y1g", ws), ("wgb", i)], writes=[PSK[bank]], mm=True)
            E("vector", (lambda e, bank=bank: e.tensor_copy(gsc[:, :], pb[bank][:, 0:8])),
              reads=[PSK[bank]], writes=[("gsc",)])
            E("vector", (lambda e: e.max(mx8[:, :], gsc[:, :])),
              reads=[("gsc",)], writes=[("mx8",)])
            E("vector", (lambda e: e.max_index(ix8[:, :], mx8[:, :], gsc[:, :])),
              reads=[("gsc",), ("mx8",)], writes=[("ix8",)])
            E("vector", (lambda e: e.tensor_scalar(vsc[0][:, 0:1], mx8[:, 0:1], -1.0,
                                                   None, ALU.mult)),
              reads=[("mx8",)], writes=[("vsc", id(vsc[0]))])
            E("scalar", (lambda e: e.activation(vsc[1][:, 0:8], gsc[:, :], AF.Exp,
                                                bias=vsc[0][:, 0:1])),
              reads=[("gsc",), ("vsc", id(vsc[0]))], writes=[("vsc", id(vsc[1]))])
            E("vector", (lambda e: e.reduce_sum(vsc[0][:, 1:2], vsc[1][:, 0:8],
                                                mybir.AxisListType.X)),
              reads=[("vsc", id(vsc[1]))], writes=[("vsc", id(vsc[0]))])
            E("vector", (lambda e, tt=tt: e.reciprocal(sgk[:, 32 + tt:32 + tt + 1],
                                                       vsc[0][:, 1:2])),
              reads=[("vsc", id(vsc[0]))], writes=[("gv", tt)])
            E("vector", (lambda e, tt=tt: e.tensor_copy(sgk[:, 16 + tt:16 + tt + 1],
                                                        ix8[:, 0:1])),
              reads=[("ix8",)], writes=[("ixf", tt)])
            E("vector", (lambda e, tt=tt: e.tensor_scalar(ohsb[:, tt * 8:(tt + 1) * 8],
                                                          iota8r[:, :],
                                                          sgk[:, 16 + tt:16 + tt + 1],
                                                          None, ALU.is_equal)),
              reads=[("ixf", tt), ("small", "iota8r")], writes=[("oh", tt)],
              conflicts=([("b1c",)] if tt < 4 else []))
            b_ = tt // 2
            E("tensor", (lambda e, tt=tt, b_=b_:
                         e.matmul(pb[6 + b_ // 4][32 * (b_ % 4):32 * (b_ % 4) + 1, 0:8],
                                  onesc_b[:, 0:1], ohsb[:, tt * 8:(tt + 1) * 8],
                                  start=False, stop=(tt % 2 == 1), skip_group_check=True,
                                  tile_position=(0, 32 * (b_ % 4)))),
              reads=[("oh", tt)] + KC(onesc_b), writes=[PSK[6 + b_ // 4]], mm=True)
        cnt_sb = sb("cnt_sb", (128, 16), F32)
        for bk in (6, 7):
            E("scalar", (lambda e, bk=bk: e.copy(cnt_sb[:, (bk - 6) * 8:(bk - 5) * 8],
                                                 pb[bk][:, 0:8])),
              reads=[PSK[bk]], writes=[("cnt_sb", bk)])
        for b_ in range(8):
            E("gpsimd", (lambda e, b_=b_: e.dma_start(
                dt["ag_cnt_in"][b_:b_ + 1, :],
                cnt_sb[32 * (b_ % 4):32 * (b_ % 4) + 1, (b_ // 4) * 8:(b_ // 4) * 8 + 8])),
              reads=[("cnt_sb", 6 + b_ // 4)], writes=[("cntin", b_)], dma_sem="dcnt")
        E("gpsimd", (lambda e: e.collective_compute(
            "AllGather", ALU.bypass, replica_groups=[list(range(NCORES))],
            ins=[dt["ag_cnt_in"].opt()], outs=[dt["ag_cnt_out"].opt()])),
          reads=[("cntin", b_) for b_ in range(8)], writes=[("ago_cnt",)], cc=True,
          dma_sem="ccc")
        WER, MER = 16384, 28672
        sscrold = claim(0, 4096, [("sselscr",)])
        ssout = claim(4096, 4608, [("ssres",)])
        for i in range(NCH):
            for q4 in range(4):
                E("gpsimd", (lambda e, i=i, q4=q4: e.dma_start(
                    arena[:, 0:4096].rearrange("p (b t) -> p b t", b=2),
                    dt["ag_s_out"][q4 * 2:(q4 + 1) * 2, i * 128:(i + 1) * 128, :]
                        .transpose([1, 0, 2]))),
                  reads=[("ago_s",), ("sselscr",)], writes=[("sselscr",)],
                  conflicts=(sscrold if (i == 0 and q4 == 0) else ()), dma_sem="dsel2")
                for w in range(NCORES):
                    mkd = (lambda e: arena[:, 4096:4608]
                           .rearrange("p (b t) -> p b t", b=2))
                    mks = (lambda e, w=w: arena[:, 0:4096]
                           .rearrange("p (b t) -> p b t", b=2)[:, :, w * 256:(w + 1) * 256])
                    if w == 0:
                        E("vector", (lambda e, w=w, mkd=mkd, mks=mks:
                                     e.tensor_scalar(mkd(e), mks(e), selc[:, w:w + 1],
                                                     None, ALU.mult)),
                          reads=[("sselscr",), ("small", "selcol")], writes=[("ssres",)],
                          conflicts=(ssout if (i == 0 and q4 == 0) else ()))
                    else:
                        E("vector", (lambda e, w=w, mkd=mkd, mks=mks:
                                     e.scalar_tensor_tensor(mkd(e), mks(e), selc[:, w:w + 1],
                                                            mkd(e), ALU.mult, ALU.add)),
                          reads=[("sselscr",), ("small", "selcol"), ("ssres",)],
                          writes=[("ssres",)])
                E("gpsimd", (lambda e, i=i, q4=q4: e.dma_start(
                    dt["hb_st"][i * 128:(i + 1) * 128, q4 * 512:(q4 + 1) * 512],
                    arena[:, 4096:4608])),
                  reads=[("ssres",)], writes=[("hbst", i, q4)], dma_sem="dst2")

        y1rold = claim(0, 16384, [("y1res", a) for a in range(8)])
        for a in range(8):
            E("sync", (lambda e, a=a: e.dma_start(
                arena[:, a * 2048:(a + 1) * 2048],
                dt["hb_y1"][a * 128:(a + 1) * 128, :])),
              reads=[("hby1", s_, i_) for s_ in range(4) for i_ in range(NCH)],
              writes=[("y1res", a)],
              conflicts=(y1rold if a == 0 else ()), dma_sem="dyr")

        E("sync", (lambda e: e.dma_start(cntall[:, :],
                                         dt["ag_cnt_out"].rearrange("c b e -> (c b) e"))),
          reads=[("ago_cnt",)], writes=[("cntall",)], dma_sem="dcnl")
        for bk in (0, 1):
            E("tensor", (lambda e, bk=bk: e.matmul(pb[bk][:, 0:512], zrow_b[0:1, 0:128],
                                                   zrow_b[0:1, 0:512], start=True, stop=True)),
              reads=KC(zrow_b), writes=[PSK[bk]], mm=True)
        for b_ in range(8):
            E("tensor", (lambda e, b_=b_: e.matmul(
                pb[b_ // 4][32 * (b_ % 4):32 * (b_ % 4) + 1, 0:8],
                m64sb[:, b_:b_ + 1], cntall[:, :], start=False, stop=True,
                skip_group_check=True, tile_position=(0, 32 * (b_ % 4)))),
              reads=[("cntall",), K_SM("m64")], writes=[PSK[b_ // 4]], mm=True)
        basesr = recb[:, 128:144]
        for bk in (0, 1):
            E("scalar", (lambda e, bk=bk: e.copy(basesr[:, bk * 8:(bk + 1) * 8],
                                                 pb[bk][:, 0:8])),
              reads=[PSK[bk]], writes=[("basesr", bk)])
        E("gpsimd", (lambda e: e.memset(priors[:, :], 0.0)), writes=[("priors",)])
        for tt in range(16):
            bank = 2 + tt % 2
            E("tensor", (lambda e, tt=tt, bank=bank:
                         e.matmul(pb[bank][:, 0:8], smallsb["tri_all"][:, :],
                                  ohsb[:, tt * 8:(tt + 1) * 8], start=True, stop=True)),
              reads=[("oh", tt), K_SM("tri_all")], writes=[PSK[bank]], mm=True)
            E("tensor", (lambda e, tt=tt, bank=bank:
                         e.matmul(pb[bank][:, 0:8], onesr_f[0:1, :], priors[0:1, :],
                                  start=False, stop=True, skip_group_check=True)),
              reads=[("priors",)] + KC(onesr_f), writes=[PSK[bank]], mm=True)
            b_ = tt // 2
            E("tensor", (lambda e, tt=tt, bank=bank, b_=b_:
                         e.matmul(pb[bank][:, 0:8],
                                  onesr_f[32 * (b_ % 4):32 * (b_ % 4) + 1, :],
                                  basesr[32 * (b_ % 4):32 * (b_ % 4) + 1,
                                         (b_ // 4) * 8:(b_ // 4) * 8 + 8],
                                  start=False, stop=True, skip_group_check=True,
                                  tile_position=(32 * (b_ % 4), 0))),
              reads=[("basesr", b_ // 4)] + KC(onesr_f), writes=[PSK[bank]], mm=True)
            E("vector", (lambda e, tt=tt, bank=bank:
                         e.tensor_mul(vsc[0][:, 0:8], pb[bank][:, 0:8],
                                      ohsb[:, tt * 8:(tt + 1) * 8])),
              reads=[PSK[bank], ("oh", tt)], writes=[("vsc", id(vsc[0]))])
            E("vector", (lambda e: e.reduce_sum(vsc[1][:, 3:4], vsc[0][:, 0:8],
                                                mybir.AxisListType.X)),
              reads=[("vsc", id(vsc[0]))], writes=[("locv",)])
            E("vector", (lambda e: e.tensor_scalar(vsc[1][:, 4:5], vsc[1][:, 3:4],
                                                   float(CAPG) - 0.5, None, ALU.is_lt)),
              reads=[("locv",)], writes=[("keep",)])
            E("vector", (lambda e, tt=tt: e.tensor_mul(sgk[:, 32 + tt:32 + tt + 1],
                                                       sgk[:, 32 + tt:32 + tt + 1],
                                                       vsc[1][:, 4:5])),
              reads=[("keep",), ("gv", tt)], writes=[("gv", tt)])
            E("tensor", (lambda e, tt=tt: e.matmul(pb[1][0:1, 0:8], onesc_b[:, 0:1],
                                                   ohsb[:, tt * 8:(tt + 1) * 8],
                                                   start=True, stop=True)),
              reads=[("oh", tt)] + KC(onesc_b), writes=[PSK[1]], mm=True)
            E("vector", (lambda e: e.tensor_add(priors[0:1, :], priors[0:1, :],
                                                pb[1][0:1, 0:8])),
              reads=[PSK[1], ("priors",)], writes=[("priors",)])
        for tt in range(16):
            E("tensor", (lambda e, tt=tt: e.transpose(pb[0][0:1, 0:128],
                                                      sgk[:, 16 + tt:16 + tt + 1],
                                                      smalls["identf"][:, :])),
              reads=[("ixf", tt), K_SM("identf")], writes=[PSK[0]], mm=True)
            E("scalar", (lambda e: e.copy(slrow[0:1, :], pb[0][0:1, 0:128])),
              reads=[PSK[0]], writes=[("slrow",)])
            E("tensor", (lambda e: e.matmul(pb[1][:, 0:128], onesr_f[0:1, :],
                                            slrow[0:1, :], start=True, stop=True)),
              reads=[("slrow",)] + KC(onesr_f), writes=[PSK[1]], mm=True)
            E("vector", (lambda e, tt=tt: e.tensor_copy(sbst[:, tt * 128:(tt + 1) * 128],
                                                        pb[1][:, 0:128])),
              reads=[PSK[1]], writes=[("sbst", tt // 4)],
              conflicts=([("x1", 2), ("x1", 3)] if tt == 0 else ()))
            E("tensor", (lambda e, tt=tt: e.transpose(pb[0][0:1, 0:128],
                                                      sgk[:, 32 + tt:32 + tt + 1],
                                                      smalls["identf"][:, :])),
              reads=[("gv", tt), K_SM("identf")], writes=[PSK[0]], mm=True)
            E("scalar", (lambda e: e.copy(slrow[0:1, :], pb[0][0:1, 0:128])),
              reads=[PSK[0]], writes=[("slrow",)])
            E("tensor", (lambda e: e.matmul(pb[1][:, 0:128], onesr_f[0:1, :],
                                            slrow[0:1, :], start=True, stop=True)),
              reads=[("slrow",)] + KC(onesr_f), writes=[PSK[1]], mm=True)
            E("scalar", (lambda e, tt=tt: e.copy(y1tb[:, tt * 128:(tt + 1) * 128],
                                                 pb[1][:, 0:128])),
              reads=[PSK[1]], writes=[("gkb", tt // 4)],
              conflicts=[("y1g", 0), ("y1g", 1)])

        if "y1" in DEBUG:
            for a in range(8):
                E("gpsimd", (lambda e, a=a: e.dma_start(
                    dt["dbg_y1"][a * 128:(a + 1) * 128, :],
                    arena[:, a * 2048:(a + 1) * 2048])),
                  reads=[("y1res", a)], dma_sem="dbg")
        wfold = claim(WER, WER + 8192, [("wfe", 0, a) for a in range(8)])
        meold = claim(MER, MER + 4096, [("mer", 0), ("mer", 1)])
        biasr = x1[0:1, 6144:6656].bitcast(BF16)
        macc = x1[:, 8192:16384].bitcast(BF16)
        E("gpsimd", (lambda e: e.memset(x1[:, 8192:16384], 0.0)),
          writes=[("macc", b_) for b_ in range(8)],
          conflicts=[("fw1", 0), ("fw1", 1), ("fw2", 0), ("fw2", 1), ("fstg",),
                     ("x1", 4), ("x1", 5), ("x1", 6), ("x1", 7)]
                    + [("ypsl", j_) for j_ in range(8)]
                    + [("y1sl", j_) for j_ in range(8)])
        wslot_ap = [lambda a: arena[:, WER + a * 1024: WER + (a + 1) * 1024],
                    lambda a: xn[:, 4096 + a * 1024: 4096 + (a + 1) * 1024]]
        for e_ in range(NEXP):
            sl4 = e_ % 2
            mer0 = MER + sl4 * 2048
            for a in range(8):
                E("sync", (lambda e, e_=e_, a=a, sl4=sl4: e.dma_start(
                    wslot_ap[sl4](a),
                    dt["ag_wf_out"][e_, a * 128:(a + 1) * 128, :])),
                  reads=[("ago_wf",)], writes=[("wfe", sl4, a)],
                  conflicts=[("wfe", sl4, a)]
                            + (list(wfold) if (e_ == 0 and a == 0) else [])
                            + ([("y1m", a_) for a_ in range(8)]
                               if (e_ == 1 and a == 0) else [])
                            + ([("xn1t", 2 + a // 2, 0), ("xn1t", 2 + a // 2, 1)]
                               if e_ == 1 else []),
                  dma_sem="dwf" + str(sl4))
            E("sync", (lambda e, e_=e_: e.dma_start(
                biasr[0:1, :], dt["ag_wf_out"][e_, 1024:1025, :])),
              reads=[("ago_wf",)], writes=[("biasr",)],
              conflicts=[("biasr",), ("x1", 3)], dma_sem="dwfb")
            E("vector", (lambda e, e_=e_, mer0=mer0: e.tensor_scalar(
                arena[:, mer0:mer0 + 2048], sbst[:, :], float(e_), None, ALU.is_equal)),
              reads=[("sbst", q_) for q_ in range(4)], writes=[("mer", sl4)],
              conflicts=(meold if e_ < 2 else [("mer", sl4)]))
            E("vector", (lambda e, mer0=mer0: e.tensor_mul(arena[:, mer0:mer0 + 2048],
                                                arena[:, mer0:mer0 + 2048],
                                                y1tb[:, 0:2048])),
              reads=[("mer", sl4)] + [("gkb", q_) for q_ in range(4)],
              writes=[("mer", sl4)])
            for tokc in range(4):
                for a in range(8):
                    E("vector", (lambda e, a=a, tokc=tokc, mer0=mer0: e.tensor_mul(
                        xn[:, a * 512:(a + 1) * 512],
                        arena[:, a * 2048 + tokc * 512: a * 2048 + (tokc + 1) * 512],
                        arena[:, mer0 + tokc * 512: mer0 + (tokc + 1) * 512])),
                      reads=[("y1res", a), ("mer", sl4)], writes=[("y1m", a)],
                      conflicts=([("xn1t", a // 4, 0), ("xn1t", a // 4, 1)]
                                 if (e_ == 0 and tokc == 0) else [("y1m", a)]))
                for b_ in range(8):
                    bank = (b_ + tokc) % 2
                    for a in range(8):
                        E("tensor", (lambda e, b_=b_, a=a, bank=bank, sl4=sl4:
                                     e.matmul(pb[bank][:, 0:512],
                                              wslot_ap[sl4](a)[:, b_ * 128:(b_ + 1) * 128],
                                              xn[:, a * 512:(a + 1) * 512],
                                              start=(a == 0), stop=False,
                                              skip_group_check=True)),
                          reads=[("wfe", sl4, a), ("y1m", a)], writes=[PSK[bank]],
                          mm=True)
                    E("tensor", (lambda e, b_=b_, tokc=tokc, bank=bank, mer0=mer0:
                                 e.matmul(pb[bank][:, 0:512],
                                          biasr[0:1, b_ * 128:(b_ + 1) * 128],
                                          arena[0:1, mer0 + tokc * 512:
                                                mer0 + (tokc + 1) * 512],
                                          start=False, stop=True,
                                          skip_group_check=True)),
                      reads=[("biasr",), ("mer", sl4)], writes=[PSK[bank]], mm=True)
                    E("vector", (lambda e, b_=b_, tokc=tokc, bank=bank:
                                 e.tensor_add(
                                     macc[:, b_ * 2048 + tokc * 512:
                                          b_ * 2048 + (tokc + 1) * 512],
                                     macc[:, b_ * 2048 + tokc * 512:
                                          b_ * 2048 + (tokc + 1) * 512],
                                     pb[bank][:, 0:512])),
                      reads=[PSK[bank], ("macc", b_)], writes=[("macc", b_)])

        ypsl2 = arena[:, 24576:28672]
        y1sl2 = arena[:, WER:WER + 4096]
        ypsl2c = claim(24576, 28672, [("ypsl2", i) for i in range(NCH)])
        y1sl2c = claim(WER, WER + 4096, [("y1sl2", i) for i in range(NCH)])
        outsb = x1
        for tc in range(4):
            for m in range(NCH):
                E("vector", (lambda e, tc=tc, m=m:
                             e.tensor_add(ypsl2[:, m * 512:(m + 1) * 512],
                                          arena[:, m * 2048 + tc * 512:
                                                m * 2048 + (tc + 1) * 512],
                                          macc[:, m * 2048 + tc * 512:
                                               m * 2048 + (tc + 1) * 512])),
                  reads=[("y1res", m), ("macc", m)], writes=[("ypsl2", m)],
                  conflicts=(list(ypsl2c) if (tc == 0 and m == 0) else ()))
            DRAIN(emit_ln(ypsl2, lambda i, qc: ("ypsl2", i), y1sl2,
                          lambda i, qc: ("y1sl2", i),
                          512, "ln2g_col", "ln2b_col", onescol=onesc_b,
                          dst_conflicts=(list(y1sl2c) if tc == 0 else ())))
            for m in range(NCH):
                ws = m % 2
                E("gpsimd", (lambda e, tc=tc, m=m, ws=ws: e.dma_start(
                    pt[ws][:, 0:512],
                    dt["hb_st"][m * 128:(m + 1) * 128, tc * 512:(tc + 1) * 512])),
                  reads=[("hbst", m, tc)], writes=[("y1f", ws)],
                  conflicts=[("y1f", ws)] + ([("y1stg", ws), ("b2r",)]
                                             if (tc == 0 and m < 2) else []),
                  dma_sem="dy1f" + str(ws))
                E("vector", (lambda e, tc=tc, m=m, ws=ws:
                             e.tensor_add(x1[:, ws * 512:(ws + 1) * 512],
                                          y1sl2[:, m * 512:(m + 1) * 512],
                                          pt[ws][:, 0:512])),
                  reads=[("y1sl2", m), ("y1f", ws)], writes=[("outsb", ws)],
                  conflicts=[("outdma", ws)] + ([("x1", 0)] if (tc == 0 and m < 2)
                                                else []))
                for hb in range(2):
                    E("sync", (lambda e, tc=tc, m=m, ws=ws, hb=hb: e.dma_start(
                        dt["out_fm"][2 * tc + hb, m * 128:(m + 1) * 128, :],
                        x1[:, ws * 512 + hb * 256: ws * 512 + hb * 256 + 256])),
                      reads=[("outsb", ws)], writes=[("outdma", ws)], dma_sem="dout")
    if "xn1" in DEBUG:
        for i in range(NCH):
            E("gpsimd", (lambda e, i=i: e.dma_start(dt["dbg_xn1"][i * 128:(i + 1) * 128, :],
                                                    xn[:, i * T:(i + 1) * T])),
              reads=[("xn1", i, 0), ("xn1", i, 1), ("xn1", i, 2), ("xn1", i, 3)], dma_sem="dbg")
    if "states" in DEBUG and STAGES["l1a"]:
        for j in range(2):
            E("gpsimd", (lambda e, j=j: e.dma_start(dt["dbg_states"][j * 128:(j + 1) * 128, :],
                                                    arena[:, 28672 + j * T: 28672 + (j + 1) * T])),
              reads=[("st", T - 1, j)], dma_sem="dbg")
    if "ypre" in DEBUG and STAGES["l1a"]:
        for i in range(NCH):
            E("gpsimd", (lambda e, i=i: e.dma_start(dt["dbg_ypre"][i * 128:(i + 1) * 128, :],
                                                    arena[:, 34816 + i * 256: 34816 + (i + 1) * 256])),
              reads=[("ypre", 7, i)], dma_sem="dbg")
    if "y1a" in DEBUG and STAGES["l1a"]:
        for i in range(NCH):
            E("gpsimd", (lambda e, i=i: e.dma_start(dt["dbg_y1a"][i * 128:(i + 1) * 128, :],
                                                    arena[:, 32768 + i * 256: 32768 + (i + 1) * 256])),
              reads=[("y1a", 7, i)], dma_sem="dbg")
    if "h" in DEBUG and STAGES["l1a"]:
        for sub in range(4):
            E("gpsimd", (lambda e, sub=sub: e.dma_start(dt["dbg_h"][sub * 128:(sub + 1) * 128, :],
                                                        h_sb[:, sub * 256:(sub + 1) * 256])),
              reads=[("h", sub)], dma_sem="dbg")
    if "x1" in DEBUG:
        for i in range(NCH):
            E("sync", (lambda e, i=i: e.dma_start(dt["dbg_x1"][i * 128:(i + 1) * 128, :],
                                                  x1[:, i * T:(i + 1) * T])),
              reads=[("x1", i)], dma_sem="dout")

    if not STAGES["l1f"]:
        for bb in range(B):
            for i in range(NCH):
                E("sync", (lambda e, bb=bb, i=i: e.dma_start(
                    dt["out_fm"][bb, i * 128:(i + 1) * 128, :],
                    x1[:, i * T: i * T + TW])),
                  reads=[("x1", i)], dma_sem="dout")

    P.resolve()

    for name, cntv in P.final_counts.items():
        if name.startswith("D:dout") or name.startswith("D:dbg"):
            P.final_waits.append(("gpsimd", name, cntv))

    semnames = sorted(set(P.semnames) | {n for _, n, _ in P.final_waits})
    sems = {}
    for n in semnames:
        sems[n] = es.enter_context(nc.semaphore(n.replace(":", "_")))
    with nc.Block() as block:
        bodies = P.make_bodies(sems)
        block.sync(bodies["sync"])
        block.gpsimd(bodies["gpsimd"])
        block.tensor(bodies["tensor"])
        block.scalar(bodies["scalar"])
        block.vector(bodies["vector"])
    es.close()
    build_program.last_P = P
    return nc, len(P.recs)


def prep_inputs(inputs):
    f = np.float32
    g = {k: np.asarray(v) for k, v in inputs.items()}

    def col(v):
        return np.ascontiguousarray(v.reshape(-1, 128).T).astype(f)

    com = {}
    com["wq"] = np.ascontiguousarray(g["L0_attn_Wq"])
    com["wk"] = np.ascontiguousarray(g["L0_attn_Wk"])
    com["wv"] = np.ascontiguousarray(g["L0_attn_Wv"])
    com["bq_col"] = col(g["L0_attn_bq"]); com["bk_col"] = col(g["L0_attn_bk"])
    com["bv_row"] = g["L0_attn_bv"].reshape(1, -1).astype(f)
    com["w0r"] = np.ascontiguousarray(g["L0_ffn_W_r"].T)
    com["w0k"] = np.ascontiguousarray(g["L0_ffn_W_k"].T)
    com["w0v"] = np.ascontiguousarray(g["L0_ffn_W_v"].T)
    com["mur_col"] = col(g["L0_ffn_mu_r"]); com["muk_col"] = col(g["L0_ffn_mu_k"])
    com["bmT"] = np.ascontiguousarray(g["L1_attn_Bm"].T)
    com["dT"] = np.ascontiguousarray(g["L1_attn_D"].T)
    com["aT"] = np.ascontiguousarray(g["L1_attn_A"].T)
    com["cT"] = np.ascontiguousarray(g["L1_attn_C"].T)
    com["lnw_col"] = col(g["L1_attn_ln_w"]); com["lnb_col"] = col(g["L1_attn_ln_b"])
    com["w1T"] = np.ascontiguousarray(g["L1_attn_W1"].T)
    com["b1_col"] = col(g["L1_attn_b1"])
    com["w2T"] = np.ascontiguousarray(g["L1_attn_W2"].T)
    com["b2_col"] = col(g["L1_attn_b2"])
    com["ipw"] = np.ascontiguousarray(g["L1_ffn_in_proj_w"].T)
    com["ipb_row"] = g["L1_ffn_in_proj_b"][2048:].reshape(1, -1).astype(f)
    com["ipb_col"] = col(g["L1_ffn_in_proj_b"])
    com["opw"] = np.ascontiguousarray(g["L1_ffn_out_proj_w"].T)
    com["opb_col"] = col(g["L1_ffn_out_proj_b"])
    com["ln1g_col"] = col(g["L1_ffn_ln1_g"]); com["ln1b_col"] = col(g["L1_ffn_ln1_b"])
    com["ln2g_col"] = col(g["L1_ffn_ln2_g"]); com["ln2b_col"] = col(g["L1_ffn_ln2_b"])
    com["wg"] = np.ascontiguousarray(g["L1_ffn_wg"])
    jj = np.arange(128)
    uu_ = jj // 8
    com["mha_mask"] = np.where(uu_[:, None] == uu_[None, :], 0.0, -480.0).astype(f)
    com["tri_all"] = (jj[:, None] < jj[None, :]).astype(f)

    com["ident"] = np.eye(128, dtype=f)
    com["identf"] = np.eye(128, dtype=f)
    com["iota8r"] = np.broadcast_to(np.arange(8, dtype=f), (128, 8)).copy()

    x = np.asarray(g["L0_attn_hidden_states"])
    in_maps = []
    for c in range(NCORES):
        m = dict(com)
        m["xT"] = np.ascontiguousarray(x[c].T).astype(f)
        m["myw1T"] = np.ascontiguousarray(np.asarray(g["L1_ffn_w1"])[c].T).astype(f)
        m["myw2"] = np.ascontiguousarray(np.asarray(g["L1_ffn_w2"])[c]).astype(f)
        m["myb1c"] = np.ascontiguousarray(
            np.asarray(g["L1_ffn_b1"])[c].reshape(-1, 128).T).astype(f)
        m["myb2r"] = np.asarray(g["L1_ffn_b2"])[c].reshape(1, -1).astype(f)
        m["selcol"] = np.broadcast_to((np.arange(NCORES) == c).astype(f), (128, NCORES)).copy()
        m64 = np.zeros((64, NEXP), f)
        for cp in range(NCORES):
            for bp in range(NEXP):
                for b_ in range(NEXP):
                    m64[cp * 8 + bp, b_] = (float(bp < b_ and cp != c)
                                            + float(cp < c and bp == b_))
        m["m64"] = m64
        in_maps.append(m)
    return in_maps


_CACHED = {}


def kernel(**inputs):
    key = "prog"
    if key not in _CACHED:
        _CACHED[key] = build_program()
    nc, _ = _CACHED[key]
    in_maps = prep_inputs(inputs)
    res = run_bass_kernel_spmd(nc, in_maps, core_ids=list(range(NCORES)))
    out = np.concatenate([r["out_fm"].transpose(0, 2, 1) for r in res.results], axis=1)
    return out.astype(np.float32)


# version 09
# speedup vs baseline: 1.0049x; 1.0049x over previous
import contextlib
import numpy as np
import ml_dtypes
import concourse.bass as bass
import concourse.mybir as mybir
from concourse.bass_utils import run_bass_kernel_spmd

F32 = mybir.dt.float32
BF16 = mybir.dt.bfloat16
AF = mybir.ActivationFunctionType
ALU = mybir.AluOpType

NCORES = 8
B, T, D = 8, 2048, 1024
NCH = D // 128
TW = T // NCORES
S = 256
FF1 = 4096
NEXP, CAPG = 8, 2048
T_WIN2 = 2048
PAD = 384
MFF = 4096

STAGES = dict(l0=True, l1a=True, l1f=True, out=True)
DEBUG = ()


class Rec:
    __slots__ = ("idx", "eng", "fn", "reads", "writes", "kind", "dma_sem",
                 "flag", "waits", "drain", "seq", "_deps")

    def __init__(self, idx, eng, fn, reads, writes, kind, dma_sem):
        self.idx, self.eng, self.fn = idx, eng, fn
        self.reads, self.writes = tuple(reads), tuple(writes)
        self.kind, self.dma_sem = kind, dma_sem
        self.flag = False
        self.waits = []
        self.drain = False
        self.seq = 0
        self._deps = []


class Prog:
    ENGINES = ("tensor", "scalar", "vector", "sync", "gpsimd")

    def __init__(self):
        self.recs = []
        self.last_w = {}
        self.readers = {}
        self.final_waits = []

    def emit(self, eng, fn, reads=(), writes=(), mm=False, dma_sem=None, cc=False,
             conflicts=()):
        kind = 'cc' if cc else ('d' if dma_sem else ('mm' if mm else 'c'))
        r = Rec(len(self.recs), eng, fn, reads, writes, kind, dma_sem)
        self.recs.append(r)
        deps, seen = [], set()

        def add(d):
            if d is not None and d is not r and d.idx not in seen:
                seen.add(d.idx)
                deps.append(d)
        for k in r.reads:
            add(self.last_w.get(k))
        for k in tuple(writes) + tuple(conflicts):
            add(self.last_w.get(k))
            for d in self.readers.get(k, ()):
                add(d)
        r._deps = deps
        for k in r.reads:
            self.readers.setdefault(k, []).append(r)
        for k in r.writes:
            self.last_w[k] = r
            self.readers[k] = []
        return r

    def resolve(self):
        for r in self.recs:
            for d in r._deps:
                if d.kind in ('d', 'cc'):
                    d.flag = True
                elif d.eng == r.eng and r.kind not in ('d', 'cc'):
                    if d.kind == 'mm' and r.kind == 'mm':
                        continue
                    r.drain = True
                else:
                    d.flag = True
        cnt = {}
        for r in self.recs:
            if r.kind == 'd':
                r.flag = True
                key = 'D:' + r.dma_sem
                cnt[key] = cnt.get(key, 0) + 16
                r.seq = cnt[key]
            elif r.kind == 'cc':
                r.flag = True
                key = 'D:' + r.dma_sem
                cnt[key] = cnt.get(key, 0) + 1
                r.seq = cnt[key]
            elif r.flag:
                key = 'E:' + r.eng
                cnt[key] = cnt.get(key, 0) + 1
                r.seq = cnt[key]
        import bisect
        semlist = {}
        for r in self.recs:
            if r.kind in ('d', 'cc'):
                semlist.setdefault('D:' + r.dma_sem, []).append((r.idx, r.seq))
        marks = {}
        for r in self.recs:
            for d in r._deps:
                if d.kind in ('d', 'cc'):
                    sem = 'D:' + d.dma_sem
                    lst = semlist[sem]
                    pos = bisect.bisect_left(lst, (r.idx, -1)) - 1
                    val = lst[pos][1]
                    assert val >= d.seq
                elif d.eng == r.eng:
                    continue
                else:
                    sem = 'E:' + d.eng
                    val = d.seq
                wm = marks.get((r.eng, sem), 0)
                if val > wm:
                    marks[(r.eng, sem)] = val
                    r.waits.append((sem, val))
        self.semnames = sorted(cnt.keys())
        self.final_counts = dict(cnt)

    def make_bodies(self, sems):
        per = {e: [] for e in self.ENGINES}
        for r in self.recs:
            per[r.eng].append(r)

        def body(eng_name):
            def go(e):
                for r in per[eng_name]:
                    if r.drain:
                        e.drain()
                    for sem, v in r.waits:
                        e.wait_ge(sems[sem], v)
                    instr = r.fn(e)
                    if r.flag:
                        if r.kind == 'd':
                            instr.then_inc(sems['D:' + r.dma_sem], 16)
                        elif r.kind == 'cc':
                            instr.then_inc(sems['D:' + r.dma_sem], 1)
                        else:
                            instr.then_inc(sems['E:' + r.eng], 1)
                for eng2, sem, v in self.final_waits:
                    if eng2 == eng_name:
                        e.wait_ge(sems[sem], v)
            return go
        return {e: body(e) for e in self.ENGINES}


def build_program():
    nc = bass.Bass("TRN2", target_bir_lowering=False)
    es = contextlib.ExitStack()
    P = Prog()
    dt = {}

    def din(name, shape, dtype=F32):
        dt[name] = nc.dram_tensor(name, list(shape), dtype, kind="ExternalInput").ap()

    def dout(name, shape, dtype=F32):
        dt[name] = nc.dram_tensor(name, list(shape), dtype, kind="ExternalOutput").ap()

    def dint(name, shape, dtype=F32):
        dt[name] = nc.dram_tensor(name, list(shape), dtype, kind="Internal").ap()

    def sb(name, shape, dtype):
        return es.enter_context(nc.sbuf_tensor(name, list(shape), dtype))

    def psalloc(name, shape=(128, 512)):
        return es.enter_context(nc.psum_tensor(name, list(shape), F32))

    din("xT", (D, T))
    din("wq", (D, D)); din("wk", (D, D)); din("wv", (D, D))
    din("bq_col", (128, NCH)); din("bk_col", (128, NCH)); din("bv_row", (1, D))
    din("w0r", (D, D)); din("w0k", (D, D)); din("w0v", (D, D))
    din("mur_col", (128, NCH)); din("muk_col", (128, NCH))
    din("bmT", (D, S)); din("dT", (D, D)); din("aT", (S, S)); din("cT", (S, D))
    din("lnw_col", (128, NCH)); din("lnb_col", (128, NCH))
    din("w1T", (D, FF1)); din("b1_col", (128, FF1 // 128))
    din("w2T", (FF1, D)); din("b2_col", (128, NCH))
    din("ipw", (D, 3 * D)); din("ipb_row", (1, D)); din("ipb_col", (128, 3 * NCH))
    din("opw", (D, D)); din("opb_col", (128, NCH))
    din("ln1g_col", (128, NCH)); din("ln1b_col", (128, NCH))
    din("ln2g_col", (128, NCH)); din("ln2b_col", (128, NCH))
    din("wg", (D, NEXP))
    din("myw1T", (MFF, D)); din("myw2", (MFF, D))
    din("myb1c", (128, 32)); din("myb2r", (1, D))
    din("mha_mask", (128, 128))
    din("tri_all", (128, 128))
    din("selcol", (128, NCORES)); din("m64", (64, NEXP))
    din("iota8r", (128, NEXP))
    din("ident", (128, 128)); din("identf", (128, 128))
    dout("out_fm", (B, D, TW))
    for d in DEBUG:
        if d == "y1":
            dout("dbg_y1", (D, T))
        elif d == "states":
            dout("dbg_states", (256, T))
        elif d in ("ypre", "y1a"):
            dout("dbg_" + d, (D, 256))
        elif d == "h":
            dout("dbg_h", (512, 256))
        else:
            dout("dbg_" + d, (D, T))

    dint("ag_xn1_in", (D, T), BF16)
    dint("ag_xn1_out", (NCORES, D, T), BF16)
    dint("ag_s_in", (D, T), F32)
    dint("ag_s_out", (NCORES, D, T), F32)
    dint("ag_cnt_in", (B, NEXP), F32)
    dint("ag_cnt_out", (NCORES, B, NEXP), F32)
    dint("hb_w1b", (D, FF1), BF16)
    dint("hb_opb", (D, D), BF16)
    dint("hb_w2b", (FF1, D), BF16)
    dint("hb_y1", (D, T), BF16)
    dint("hb_st", (D, T_WIN2), BF16)
    dint("ag_wf_in", (1025, D), BF16)
    dint("ag_wf_out", (NCORES, 1025, D), BF16)

    x1 = sb("x1", (128, NCH * T), F32)
    xn = sb("xn", (128, NCH * T), BF16)
    onesr_f = sb("onesr_f", (128, 128), F32)
    onesr_b = sb("onesr_b", (1, 128), BF16)
    onesc_f = sb("onesc_f", (128, 1), F32)
    onesc_b = sb("onesc_b", (128, 1), BF16)
    zrow_b = sb("zrow_b", (1, 512), BF16)
    eps_col = sb("eps_col", (128, 1), F32)
    pb = [psalloc(f"pb{i}") for i in range(8)]
    PSK = [("ps", i) for i in range(8)]

    smalls = {}
    for nm, shp in [("bq_col", (128, NCH)), ("bk_col", (128, NCH)),
                    ("mur_col", (128, NCH)), ("muk_col", (128, NCH)),
                    ("lnw_col", (128, NCH)), ("lnb_col", (128, NCH)),
                    ("b1_col", (128, FF1 // 128)), ("b2_col", (128, NCH)),
                    ("opb_col", (128, NCH)),
                    ("ln1g_col", (128, NCH)), ("ln1b_col", (128, NCH)),
                    ("ln2g_col", (128, NCH)), ("ln2b_col", (128, NCH)),
                    ("identf", (128, 128))]:
        smalls[nm] = sb("s_" + nm, shp, F32)
    smallsb = {}
    for nm, shp in [("bv_row", (1, D)), ("ipb_row", (1, D)),
                    ("mha_mask", (128, 128)),
                    ("tri_all", (128, 128)),
                    ("ident", (128, 128))]:
        smallsb[nm] = sb("b_" + nm, shp, BF16)

    sqt = [sb("sqt0", (128, 512), F32)] * 2
    stat_m = sb("stat_m", (128, 512), F32)
    stat_r = sb("stat_r", (128, 512), F32)
    vsc = [sb(f"vsc{i}", (128, 512), F32) for i in range(2)]
    stat_t = vsc[0]

    E = P.emit
    K_SM = lambda nm: ("small", nm)

    for i in range(NCH):
        E("sync", (lambda e, i=i: e.dma_start(x1[:, i * T:(i + 1) * T],
                                              dt["xT"][i * 128:(i + 1) * 128, :])),
          writes=[("x1", i)], dma_sem="dx")
    for nm in smalls:
        E("sync", (lambda e, nm=nm: e.dma_start(smalls[nm][:, :], dt[nm][:, :])),
          writes=[K_SM(nm)], dma_sem="dsm")
    for nm in smallsb:
        E("gpsimd", (lambda e, nm=nm: e.dma_start(smallsb[nm][:, :], dt[nm][:, :])),
          writes=[K_SM(nm)], dma_sem="dsb")
    for buf, val in [(onesr_f, 1.0), (onesr_b, 1.0), (onesc_f, 1.0), (onesc_b, 1.0),
                     (zrow_b, 0.0), (eps_col, 1e-5)]:
        E("gpsimd", (lambda e, buf=buf, val=val: e.memset(buf[:, :], val)),
          writes=[("c", id(buf))])

    def KC(*bufs):
        return [("c", id(b)) for b in bufs]

    def emit_ln(src, src_key, dst, dst_key, tokens, gamma_nm=None, beta_nm=None, onescol=None, dst_extra=(), dst_conflicts=()):
        onescol = onescol if onescol is not None else onesc_f
        qw = min(512, tokens)
        nqc = tokens // qw
        for bank in (6, 7):
            E("tensor", (lambda e, bank=bank: e.matmul(pb[bank][:, 0:512], zrow_b[0:1, 0:128],
                                                       zrow_b[0:1, 0:512], start=True, stop=True)),
              reads=KC(zrow_b), writes=[PSK[bank]], mm=True)
        for qc in range(nqc):
            for i in range(NCH):
                sq = sqt[(qc * NCH + i) % 2]
                sl = (i * tokens + qc * qw, i * tokens + (qc + 1) * qw)
                E("scalar", (lambda e, sq=sq, sl=sl, qw=qw:
                             e.activation(sq[:, 0:qw], src[:, sl[0]:sl[1]], AF.Square)),
                  reads=[src_key(i, qc)], writes=[("sq", id(sq))])
                E("tensor", (lambda e, i=i, qc=qc, sl=sl, qw=qw:
                             e.matmul(pb[6][32 * qc:32 * qc + 1, 0:qw], onescol[:, 0:1],
                                      src[:, sl[0]:sl[1]], start=False, stop=(i == NCH - 1),
                                      skip_group_check=True, tile_position=(0, 32 * qc))),
                  reads=[src_key(i, qc)] + KC(onescol), writes=[PSK[6]], mm=True)
                E("tensor", (lambda e, i=i, qc=qc, sq=sq, qw=qw:
                             e.matmul(pb[7][32 * qc:32 * qc + 1, 0:qw], onesc_f[:, 0:1],
                                      sq[:, 0:qw], start=False, stop=(i == NCH - 1),
                                      skip_group_check=True, tile_position=(0, 32 * qc))),
                  reads=[("sq", id(sq))] + KC(onesc_f), writes=[PSK[7]], mm=True)
                yield
        E("vector", (lambda e: e.tensor_scalar(stat_m[:, :], pb[6][:, :], 1.0 / D, None, ALU.mult)),
          reads=[PSK[6]], writes=[("ln", "m")])
        E("vector", (lambda e: e.tensor_mul(stat_t[:, :], stat_m[:, :], stat_m[:, :])),
          reads=[("ln", "m")], writes=[("vsc", id(vsc[0]))])
        E("vector", (lambda e: e.scalar_tensor_tensor(stat_r[:, :], pb[7][:, :], 1.0 / D,
                                                      stat_t[:, :], ALU.mult, ALU.subtract)),
          reads=[PSK[7], ("vsc", id(vsc[0]))], writes=[("ln", "v")])
        E("scalar", (lambda e: e.activation(stat_t[:, :], stat_r[:, :], AF.Ln, bias=eps_col[:, 0:1])),
          reads=[("ln", "v")] + KC(eps_col), writes=[("vsc", id(vsc[0]))])
        E("scalar", (lambda e: e.activation(stat_r[:, :], stat_t[:, :], AF.Exp, scale=-0.5)),
          reads=[("vsc", id(vsc[0]))], writes=[("ln", "r")])
        yield
        for qc in range(nqc):
            E("tensor", (lambda e, qc=qc, qw=qw: e.matmul(pb[4][:, 0:qw], onesr_f[32 * qc:32 * qc + 1, :],
                                                   stat_m[32 * qc:32 * qc + 1, 0:qw],
                                                   start=True, stop=True,
                                                   tile_position=(32 * qc, 0))),
              reads=[("ln", "m")] + KC(onesr_f), writes=[PSK[4]], mm=True)
            E("tensor", (lambda e, qc=qc, qw=qw: e.matmul(pb[5][:, 0:qw], onesr_f[32 * qc:32 * qc + 1, :],
                                                   stat_r[32 * qc:32 * qc + 1, 0:qw],
                                                   start=True, stop=True,
                                                   tile_position=(32 * qc, 0))),
              reads=[("ln", "r")] + KC(onesr_f), writes=[PSK[5]], mm=True)
            for i in range(NCH):
                v = vsc[i % 2]
                sl = (i * tokens + qc * qw, i * tokens + (qc + 1) * qw)
                E("vector", (lambda e, v=v, sl=sl, qw=qw:
                             e.tensor_sub(v[:, 0:qw], src[:, sl[0]:sl[1]], pb[4][:, 0:qw])),
                  reads=[src_key(i, qc), PSK[4]], writes=[("vsc", id(v))])
                if gamma_nm is None:
                    E("vector", (lambda e, v=v, sl=sl, qw=qw:
                                 e.tensor_mul(dst[:, sl[0]:sl[1]], v[:, 0:qw], pb[5][:, 0:qw])),
                      reads=[("vsc", id(v)), PSK[5]] + list(dst_extra), writes=[dst_key(i, qc)],
                      conflicts=list(dst_conflicts))
                    yield
                else:
                    g, bcol = smalls[gamma_nm], smalls[beta_nm]
                    E("vector", (lambda e, v=v, qw=qw:
                                 e.tensor_mul(v[:, 0:qw], v[:, 0:qw], pb[5][:, 0:qw])),
                      reads=[("vsc", id(v)), PSK[5]], writes=[("vsc", id(v))])
                    E("vector", (lambda e, v=v, sl=sl, g=g, bcol=bcol, i=i, qw=qw:
                                 e.tensor_scalar(dst[:, sl[0]:sl[1]], v[:, 0:qw],
                                                 g[:, i:i + 1], bcol[:, i:i + 1],
                                                 ALU.mult, ALU.add)),
                      reads=[("vsc", id(v)), K_SM(gamma_nm), K_SM(beta_nm)] + list(dst_extra),
                      writes=[dst_key(i, qc)], conflicts=list(dst_conflicts))
                    yield

    def DRAIN(gen):
        for _ in gen:
            pass

    arena = sb("arena", (128, 36864), BF16)
    tenants = []

    def claim(c0, c1, keys):
        old = []
        keep = []
        for ent in tenants:
            if ent[0] < c1 and c0 < ent[1]:
                old.extend(ent[2])
                if ent[0] < c0:
                    keep.append([ent[0], c0, ent[2]])
                if ent[1] > c1:
                    keep.append([c1, ent[1], ent[2]])
            else:
                keep.append(ent)
        tenants[:] = keep + [[c0, c1, list(keys)]]
        return old

    WA, WB, WC = 0, 8192, 16384
    XRC, RC, KC2 = 24576, 28672, 32768
    VTC = 16384
    QHC, KHC = 32768, 34816

    def load_wslot(nm, base, region, keyname):
        keys = [(keyname, i) for i in range(NCH)]
        old = claim(base, base + NCH * 1024, keys)
        for i in range(NCH):
            E("gpsimd", (lambda e, nm=nm, base=base, i=i:
                         e.dma_start(arena[:, base + i * 1024: base + (i + 1) * 1024],
                                     dt[nm][i * 128:(i + 1) * 128, :])),
              writes=[(keyname, i)], conflicts=old, dma_sem="dw")

    pt = [sb(f"pt{i}", (128, 512), BF16) for i in range(2)]
    attq = sb("attq", (128, 2048), BF16)
    y1tb = sb("y1tb", (128, 2 * 1024), BF16)
    attp = y1tb
    recip_sb = sb("recip_sb", (128, 128), F32)
    recb = sb("recb", (128, 160), F32)
    ctmp = sb("ctmp", (128, 512), F32)

    if STAGES["l0"]:
        DRAIN(emit_ln(x1, lambda i, qc: ("x1", i), xn, lambda i, qc: ("xn", i, qc), T))

        load_wslot("w0r", WA, "WA", "w0r")
        load_wslot("w0k", WB, "WB", "w0k")
        load_wslot("w0v", WC, "WC", "w0v")

        def emit_mix(qc, mu_nm, tag):
            mu = smalls[mu_nm]
            keys = [(tag, qc, i) for i in range(NCH)]
            old = claim(XRC, XRC + 4096, keys)
            for i in range(NCH):
                lo = i * T + qc * 512
                d0 = XRC + i * 512
                if qc == 0:
                    E("vector", (lambda e, lo=lo:
                                 e.tensor_sub(vsc[0][:, 1:512], xn[:, lo + 1:lo + 512],
                                              xn[:, lo:lo + 511])),
                      reads=[("xn", i, 0)], writes=[("vsc", 0)])
                    E("vector", (lambda e, mu=mu, i=i:
                                 e.tensor_scalar(vsc[1][:, 1:512], vsc[0][:, 1:512],
                                                 mu[:, i:i + 1], None, ALU.mult)),
                      reads=[("vsc", 0), K_SM(mu_nm)], writes=[("vsc", 1)])
                    E("vector", (lambda e, lo=lo, d0=d0:
                                 e.tensor_add(arena[:, d0 + 1:d0 + 512], vsc[1][:, 1:512],
                                              xn[:, lo:lo + 511])),
                      reads=[("vsc", 1), ("xn", i, 0)], writes=[(tag, qc, i)], conflicts=old)
                    E("vector", (lambda e, lo=lo, d0=d0, mu=mu, i=i:
                                 e.tensor_scalar(arena[:, d0:d0 + 1], xn[:, lo:lo + 1],
                                                 mu[:, i:i + 1], None, ALU.mult)),
                      reads=[("xn", i, 0), K_SM(mu_nm)], writes=[(tag, qc, i)])
                else:
                    E("vector", (lambda e, lo=lo, i=i, qc=qc:
                                 e.tensor_sub(vsc[0][:, 0:512], xn[:, lo:lo + 512],
                                              xn[:, lo - 1:lo + 511])),
                      reads=[("xn", i, qc), ("xn", i, qc - 1)], writes=[("vsc", 0)])
                    E("vector", (lambda e, mu=mu, i=i:
                                 e.tensor_scalar(vsc[1][:, 0:512], vsc[0][:, 0:512],
                                                 mu[:, i:i + 1], None, ALU.mult)),
                      reads=[("vsc", 0), K_SM(mu_nm)], writes=[("vsc", 1)])
                    E("vector", (lambda e, lo=lo, d0=d0, i=i, qc=qc:
                                 e.tensor_add(arena[:, d0:d0 + 512], vsc[1][:, 0:512],
                                              xn[:, lo - 1:lo + 511])),
                      reads=[("vsc", 1), ("xn", i, qc - 1), ("xn", i, qc)],
                      writes=[(tag, qc, i)], conflicts=old)

        for qc in range(4):
            emit_mix(qc, "mur_col", "xr")
            rkeys = [("rblk", qc, oc) for oc in range(NCH)]
            rold = claim(RC, RC + 4096, rkeys)
            for oc in range(NCH):
                bank = 2 + oc % 2
                for i in range(NCH):
                    E("tensor", (lambda e, oc=oc, i=i, bank=bank, qc=qc:
                                 e.matmul(pb[bank][:, 0:512],
                                          arena[:, WA + i * 1024 + oc * 128: WA + i * 1024 + (oc + 1) * 128],
                                          arena[:, XRC + i * 512: XRC + (i + 1) * 512],
                                          start=(i == 0), stop=(i == NCH - 1),
                                          skip_group_check=True)),
                      reads=[("xr", qc, i), ("w0r", i)], writes=[PSK[bank]], mm=True)
                E("scalar", (lambda e, oc=oc, bank=bank:
                             e.activation(arena[:, RC + oc * 512: RC + (oc + 1) * 512],
                                          pb[bank][:, 0:512], AF.Sigmoid)),
                  reads=[PSK[bank]], writes=[("rblk", qc, oc)], conflicts=rold)
            emit_mix(qc, "muk_col", "xk")
            kkeys = [("kablk", qc, oc) for oc in range(NCH)]
            kold = claim(KC2, KC2 + 4096, kkeys)
            for oc in range(NCH):
                bank = 2 + oc % 2
                for i in range(NCH):
                    E("tensor", (lambda e, oc=oc, i=i, bank=bank, qc=qc:
                                 e.matmul(pb[bank][:, 0:512],
                                          arena[:, WB + i * 1024 + oc * 128: WB + i * 1024 + (oc + 1) * 128],
                                          arena[:, XRC + i * 512: XRC + (i + 1) * 512],
                                          start=(i == 0), stop=(i == NCH - 1),
                                          skip_group_check=True)),
                      reads=[("xk", qc, i), ("w0k", i)], writes=[PSK[bank]], mm=True)
                E("scalar", (lambda e, oc=oc, bank=bank:
                             e.activation(arena[:, KC2 + oc * 512: KC2 + (oc + 1) * 512],
                                          pb[bank][:, 0:512], AF.Relu)),
                  reads=[PSK[bank]], writes=[("kablk", qc, oc)], conflicts=kold)
                E("vector", (lambda e, oc=oc:
                             e.tensor_mul(arena[:, KC2 + oc * 512: KC2 + (oc + 1) * 512],
                                          arena[:, KC2 + oc * 512: KC2 + (oc + 1) * 512],
                                          arena[:, KC2 + oc * 512: KC2 + (oc + 1) * 512])),
                  reads=[("kablk", qc, oc)], writes=[("kablk", qc, oc)])
            for oc in range(NCH):
                bank = 2 + oc % 2
                for i in range(NCH):
                    E("tensor", (lambda e, oc=oc, i=i, bank=bank, qc=qc:
                                 e.matmul(pb[bank][:, 0:512],
                                          arena[:, WC + i * 1024 + oc * 128: WC + i * 1024 + (oc + 1) * 128],
                                          arena[:, KC2 + i * 512: KC2 + (i + 1) * 512],
                                          start=(i == 0), stop=(i == NCH - 1),
                                          skip_group_check=True)),
                      reads=[("kablk", qc, i), ("w0v", i)], writes=[PSK[bank]], mm=True)
                E("vector", (lambda e, oc=oc, bank=bank:
                             e.tensor_mul(ctmp[:, :], pb[bank][:, 0:512],
                                          arena[:, RC + oc * 512: RC + (oc + 1) * 512])),
                  reads=[PSK[bank], ("rblk", qc, oc)], writes=[("ctmp",)])
                E("vector", (lambda e, oc=oc, qc=qc:
                             e.tensor_add(x1[:, oc * T + qc * 512: oc * T + (qc + 1) * 512],
                                          x1[:, oc * T + qc * 512: oc * T + (qc + 1) * 512],
                                          ctmp[:, :])),
                  reads=[("ctmp",), ("x1", oc)], writes=[("x1", oc)])

        WTMP, KHF, VHF, WQR = 0, 8192, 16384, 24576
        load_wslot("wq", WQR, "WQR", "wq")
        den = sqt[0]
        for mhalf in range(2):
            mb0 = mhalf * 1024
            wvold = claim(WTMP, WTMP + 8192, [("wv", mhalf, i) for i in range(NCH)])
            for i in range(NCH):
                E("gpsimd", (lambda e, i=i: e.dma_start(
                    arena[:, WTMP + i * 1024: WTMP + (i + 1) * 1024],
                    dt["wv"][i * 128:(i + 1) * 128, :])),
                  writes=[("wv", mhalf, i)], conflicts=wvold, dma_sem="dw")
            vkeys = [("vtok", mhalf, tl, half) for tl in range(8) for half in range(2)]
            vold = claim(VHF, VHF + 8192, vkeys)
            for tl in range(8):
                for half in range(2):
                    bank = (tl * 2 + half) % 2
                    for i in range(NCH):
                        E("tensor", (lambda e, tl=tl, half=half, i=i, bank=bank, mb0=mb0:
                                     e.matmul(pb[bank][:, 0:512],
                                              xn[:, i * T + mb0 + tl * 128:
                                                 i * T + mb0 + (tl + 1) * 128],
                                              arena[:, WTMP + i * 1024 + half * 512:
                                                    WTMP + i * 1024 + (half + 1) * 512],
                                              start=(i == 0), stop=False,
                                              skip_group_check=True)),
                          reads=[("xn", i, (mb0 + tl * 128) // 512), ("wv", mhalf, i)],
                          writes=[PSK[bank]], mm=True)
                    E("tensor", (lambda e, half=half, bank=bank:
                                 e.matmul(pb[bank][:, 0:512], onesr_b[0:1, :],
                                          smallsb["bv_row"][0:1, half * 512:(half + 1) * 512],
                                          start=False, stop=True, skip_group_check=True)),
                      reads=[K_SM("bv_row")] + KC(onesr_b), writes=[PSK[bank]], mm=True)
                    E("scalar", (lambda e, tl=tl, half=half, bank=bank:
                                 e.copy(arena[:, VHF + tl * 1024 + half * 512:
                                              VHF + tl * 1024 + (half + 1) * 512],
                                        pb[bank][:, 0:512])),
                      reads=[PSK[bank]], writes=[("vtok", mhalf, tl, half)],
                      conflicts=vold)
            wkold = claim(WTMP, WTMP + 8192, [("wk", mhalf, i) for i in range(NCH)])
            for i in range(NCH):
                E("gpsimd", (lambda e, i=i: e.dma_start(
                    arena[:, WTMP + i * 1024: WTMP + (i + 1) * 1024],
                    dt["wk"][i * 128:(i + 1) * 128, :])),
                  writes=[("wk", mhalf, i)], conflicts=wkold, dma_sem="dw")
            kkeys = [("khf", mhalf, g, h2) for g in range(8) for h2 in range(2)]
            kold = claim(KHF, KHF + 8192, kkeys)
            for g in range(8):
                for h2 in range(2):
                    bank = (g * 2 + h2) % 2
                    for i in range(NCH):
                        E("tensor", (lambda e, g=g, h2=h2, i=i, bank=bank, mb0=mb0:
                                     e.matmul(pb[bank][:, 0:512],
                                              arena[:, WTMP + i * 1024 + g * 128:
                                                    WTMP + i * 1024 + (g + 1) * 128],
                                              xn[:, i * T + mb0 + h2 * 512:
                                                 i * T + mb0 + (h2 + 1) * 512],
                                              start=(i == 0), stop=(i == NCH - 1),
                                              skip_group_check=True)),
                          reads=[("xn", i, (mb0 + h2 * 512) // 512), ("wk", mhalf, i)],
                          writes=[PSK[bank]], mm=True)
                    E("scalar", (lambda e, g=g, h2=h2, bank=bank:
                                 e.activation(arena[:, KHF + g * 1024 + h2 * 512:
                                                    KHF + g * 1024 + (h2 + 1) * 512],
                                              pb[bank][:, 0:512], AF.Identity,
                                              bias=smalls["bk_col"][:, g:g + 1])),
                      reads=[PSK[bank], K_SM("bk_col")], writes=[("khf", mhalf, g, h2)],
                      conflicts=kold)
            ap2old = claim(WTMP, WTMP + 2048, [("attp2", p_) for p_ in range(4)])
            attp2 = arena[:, WTMP:WTMP + 2048]

            def emit_ctx(tl, mhalf, buf, bkey):
                for g in range(8):
                    E("tensor", (lambda e, g=g, tl=tl, mhalf=mhalf, buf=buf:
                                 e.matmul(pb[4 + g // 2][:, (g % 2) * 256:
                                                        (g % 2) * 256 + 256],
                                          arena[:, VHF + tl * 1024 + g * 128:
                                                VHF + tl * 1024 + (g + 1) * 128],
                                          buf[:, g * 256:(g + 1) * 256],
                                          start=False, stop=(tl == 7),
                                          skip_group_check=True)),
                      reads=[(bkey, g // 2), ("vtok", mhalf, tl, g // 4)],
                      writes=[PSK[4 + g // 2]], mm=True)

            for qc in range(8):
                for g in range(8):
                    bank = g % 2
                    for i in range(NCH):
                        E("tensor", (lambda e, g=g, i=i, bank=bank, qc=qc:
                                     e.matmul(pb[bank][:, 0:256],
                                              arena[:, WQR + i * 1024 + g * 128:
                                                    WQR + i * 1024 + (g + 1) * 128],
                                              xn[:, i * T + qc * 256: i * T + (qc + 1) * 256],
                                              start=(i == 0), stop=(i == NCH - 1),
                                              skip_group_check=True)),
                          reads=[("xn", i, qc // 2), ("wq", i)], writes=[PSK[bank]],
                          mm=True)
                    E("scalar", (lambda e, g=g, bank=bank:
                                 e.activation(attq[:, g * 256:(g + 1) * 256],
                                              pb[bank][:, 0:256], AF.Identity,
                                              bias=smalls["bq_col"][:, g:g + 1])),
                      reads=[PSK[bank], K_SM("bq_col")], writes=[("attq", g)],
                      conflicts=[("attq", g)])
                for bank in (4, 5, 6, 7):
                    E("tensor", (lambda e, bank=bank: e.matmul(
                        pb[bank][:, 0:512], zrow_b[0:1, 0:128], zrow_b[0:1, 0:512],
                        start=True, stop=True)),
                      reads=KC(zrow_b), writes=[PSK[bank]], mm=True)
                for tl in range(8):
                    buf = attp if tl % 2 == 0 else attp2
                    bkey = "attp" if tl % 2 == 0 else "attp2"
                    for pair in range(4):
                        bank = 2 + pair % 2
                        for sub in range(2):
                            g = pair * 2 + sub
                            E("tensor", (lambda e, g=g, sub=sub, bank=bank, tl=tl,
                                         mhalf=mhalf:
                                         e.matmul(pb[bank][:, sub * 256:(sub + 1) * 256],
                                                  arena[:, KHF + g * 1024 + tl * 128:
                                                        KHF + g * 1024 + (tl + 1) * 128],
                                                  attq[:, g * 256:(g + 1) * 256],
                                                  start=True, stop=True,
                                                  skip_group_check=True)),
                              reads=[("khf", mhalf, g, tl // 4), ("attq", g)],
                              writes=[PSK[bank]], mm=True)
                        E("scalar", (lambda e, pair=pair, bank=bank, buf=buf:
                                     e.activation(buf[:, pair * 512:(pair + 1) * 512],
                                                  pb[bank][:, 0:512], AF.Exp,
                                                  scale=1.0 / 32.0)),
                          reads=[PSK[bank]], writes=[(bkey, pair)],
                          conflicts=(list(ap2old) if (qc == 0 and tl == 1 and pair == 0)
                                     else ()))
                    E("vector", (lambda e, buf=buf: e.tensor_add(
                        pt[1][:, 0:256], buf[:, 0:256], buf[:, 256:512])),
                      reads=[(bkey, 0)], writes=[("d01",)])
                    E("vector", (lambda e, buf=buf: e.tensor_add(
                        pt[1][:, 256:512], buf[:, 512:768], buf[:, 768:1024])),
                      reads=[(bkey, 1)], writes=[("d23",)])
                    E("vector", (lambda e, buf=buf: e.tensor_add(
                        pt[0][:, 256:512], buf[:, 1024:1280], buf[:, 1280:1536])),
                      reads=[(bkey, 2)], writes=[("d45",)])
                    E("vector", (lambda e, buf=buf: e.tensor_add(
                        den[:, 256:512], buf[:, 1536:1792], buf[:, 1792:2048])),
                      reads=[(bkey, 3)], writes=[("d67",)])
                    E("vector", (lambda e: e.tensor_add(pt[1][:, 0:256], pt[1][:, 0:256],
                                                        pt[1][:, 256:512])),
                      reads=[("d01",), ("d23",)], writes=[("d01",)])
                    E("vector", (lambda e: e.tensor_add(den[:, 0:256], pt[0][:, 256:512],
                                                        den[:, 256:512])),
                      reads=[("d45",), ("d67",)], writes=[("den",)])
                    E("vector", (lambda e: e.tensor_add(den[:, 0:256], den[:, 0:256],
                                                        pt[1][:, 0:256])),
                      reads=[("den",), ("d01",)], writes=[("den",)])
                    E("vector", (lambda e: e.reciprocal(stat_m[:, 0:256], den[:, 0:256])),
                      reads=[("den",)], writes=[("recm",)])
                    E("vector", (lambda e: e.tensor_copy(pt[0][:, 0:256],
                                                         stat_m[:, 0:256])),
                      reads=[("recm",)], writes=[("rec16",)])
                    for g in range(8):
                        E("vector", (lambda e, g=g, buf=buf: e.tensor_mul(
                            buf[:, g * 256:(g + 1) * 256],
                            buf[:, g * 256:(g + 1) * 256], pt[0][:, 0:256])),
                          reads=[(bkey, g // 2), ("rec16",)], writes=[(bkey, g // 2)])
                    if tl >= 1:
                        pbuf = attp if (tl - 1) % 2 == 0 else attp2
                        pkey = "attp" if (tl - 1) % 2 == 0 else "attp2"
                        emit_ctx(tl - 1, mhalf, pbuf, pkey)
                emit_ctx(7, mhalf, attp2, "attp2")
                for g in range(8):
                    E("vector", (lambda e, g=g, qc=qc:
                                 e.tensor_add(x1[:, g * T + qc * 256:
                                                 g * T + (qc + 1) * 256],
                                              x1[:, g * T + qc * 256:
                                                 g * T + (qc + 1) * 256],
                                              pb[4 + g // 2][:, (g % 2) * 256:
                                                             (g % 2) * 256 + 256])),
                      reads=[PSK[4 + g // 2], ("x1", g)], writes=[("x1", g)])

    if STAGES["l1a"]:
        at_sb = sb("at_sb", (128, 512), BF16)
        h_sb = sb("h_sb", (128, 4 * 256), BF16)
        zero2 = sb("zero2", (128, 2), BF16)
        DTC = 0
        BMC = 8192
        CTC = 10240
        XBF = 12288
        W1S = [16384, 20480]
        W2S = [24576, 25600]
        STB = 28672
        Y1A = 32768
        YPR = 34816

        DRAIN(emit_ln(x1, lambda i, qc: ("x1", i), xn, lambda i, qc: ("xn1", i, qc), T))

        E("gpsimd", (lambda e: e.memset(zero2[:, :], 0.0)), writes=[("zero2",)])
        dold = claim(DTC, DTC + 8192, [("dT", i) for i in range(NCH)])
        for i in range(NCH):
            E("gpsimd", (lambda e, i=i: e.dma_start(arena[:, DTC + i * 1024: DTC + (i + 1) * 1024],
                                                    dt["dT"][i * 128:(i + 1) * 128, :])),
              writes=[("dT", i)], conflicts=dold, dma_sem="dw")
        bold = claim(BMC, BMC + 2048, [("bmT", i) for i in range(NCH)])
        for i in range(NCH):
            E("gpsimd", (lambda e, i=i: e.dma_start(arena[:, BMC + i * 256: BMC + (i + 1) * 256],
                                                    dt["bmT"][i * 128:(i + 1) * 128, :])),
              writes=[("bmT", i)], conflicts=bold, dma_sem="dw")
        cold = claim(CTC, CTC + 2048, [("cT", j) for j in range(2)])
        for j in range(2):
            E("gpsimd", (lambda e, j=j: e.dma_start(arena[:, CTC + j * 1024: CTC + (j + 1) * 1024],
                                                    dt["cT"][j * 128:(j + 1) * 128, :])),
              writes=[("cT", j)], conflicts=cold, dma_sem="dw")
        for idx, (i, j) in enumerate(((0, 0), (0, 1), (1, 0), (1, 1))):
            E("gpsimd", (lambda e, idx=idx, i=i, j=j:
                         e.dma_start(at_sb[:, (2 * i + j) * 128:(2 * i + j + 1) * 128],
                                     dt["aT"][i * 128:(i + 1) * 128, j * 128:(j + 1) * 128])),
              writes=[("at", i, j)], dma_sem="dw")

        stold = claim(STB, STB + 4096, [("streg",)])
        E("gpsimd", (lambda e: e.memset(arena[:, STB:STB + 4096], 0.0)),
          writes=[("streg",)], conflicts=stold)
        y1old = claim(Y1A, Y1A + 2048, [("y1reg",)])
        E("gpsimd", (lambda e: e.memset(arena[:, Y1A:Y1A + 2048], 0.0)),
          writes=[("y1reg",)], conflicts=y1old)
        ypold = claim(YPR, YPR + 2048, [("ypreg",)])
        E("gpsimd", (lambda e: e.memset(arena[:, YPR:YPR + 2048], 0.0)),
          writes=[("ypreg",)], conflicts=ypold)

        xbold = claim(XBF, XBF + 4096, [("xB", sc, tc) for sc in range(2) for tc in range(4)])
        for sc in range(2):
            for tc in range(4):
                bank = 4 + tc % 2
                for i in range(NCH):
                    E("tensor", (lambda e, sc=sc, tc=tc, i=i, bank=bank:
                                 e.matmul(pb[bank][:, 0:512],
                                          arena[:, BMC + i * 256 + sc * 128: BMC + i * 256 + (sc + 1) * 128],
                                          xn[:, i * T + tc * 512: i * T + (tc + 1) * 512],
                                          start=(i == 0), stop=(i == NCH - 1),
                                          skip_group_check=True)),
                      reads=[("xn1", i, tc), ("bmT", i)], writes=[PSK[bank]], mm=True)
                E("vector", (lambda e, sc=sc, tc=tc, bank=bank:
                             e.tensor_copy(arena[:, XBF + sc * T + tc * 512: XBF + sc * T + (tc + 1) * 512],
                                           pb[bank][:, 0:512])),
                  reads=[PSK[bank]], writes=[("xB", sc, tc)], conflicts=xbold)

        RUN = 256
        n_runs = T // RUN

        def emit_scan_bank_prep(r):
            bank = 2 + r % 2
            E("tensor", (lambda e, bank=bank: e.matmul(pb[bank][:, 0:512], zrow_b[0:1, 0:128],
                                                       zrow_b[0:1, 0:512], start=True, stop=True)),
              reads=KC(zrow_b), writes=[PSK[bank]], mm=True)
            for j in range(2):
                E("vector", (lambda e, r=r, j=j, bank=bank:
                             e.tensor_copy(pb[bank][:, j:512:2],
                                           arena[:, XBF + j * T + r * RUN: XBF + j * T + (r + 1) * RUN])),
                  reads=[("xB", j, (r * RUN) // 512)], writes=[PSK[bank]])

        def cast_gen():
            slot = [0]

            def pair(src_ap_fn, dst_ap_fn, wkey):
                sl2 = slot[0] % 4
                slot[0] += 1
                E("gpsimd", (lambda e, sl2=sl2, src_ap_fn=src_ap_fn: e.dma_start(
                    attq[:, sl2 * 512:(sl2 + 1) * 512], src_ap_fn())),
                  writes=[("cst", sl2)],
                  conflicts=[("cst", sl2)] + ([("attq", 2 * sl2), ("attq", 2 * sl2 + 1)]
                                              if slot[0] <= 4 else []),
                  dma_sem="dci" + str(sl2))
                E("sync", (lambda e, sl2=sl2, dst_ap_fn=dst_ap_fn: e.dma_start(
                    dst_ap_fn(), attq[:, sl2 * 512:(sl2 + 1) * 512])),
                  reads=[("cst", sl2)], writes=[wkey], dma_sem="dco" + str(sl2))
            for f in range(8):
                for i in range(8):
                    pair((lambda f=f, i=i: dt["w1T"][i * 128:(i + 1) * 128,
                                                     f * 512:(f + 1) * 512]),
                         (lambda f=f, i=i: dt["hb_w1b"][i * 128:(i + 1) * 128,
                                                        f * 512:(f + 1) * 512]),
                         ("hw1c", f, i))
                    yield
            for rc in range(32):
                for hf in range(2):
                    pair((lambda rc=rc, hf=hf: dt["w2T"][rc * 128:(rc + 1) * 128,
                                                         hf * 512:(hf + 1) * 512]),
                         (lambda rc=rc, hf=hf: dt["hb_w2b"][rc * 128:(rc + 1) * 128,
                                                            hf * 512:(hf + 1) * 512]),
                         ("hw2c", rc, hf))
                    yield
            for rc in range(8):
                for hf in range(2):
                    pair((lambda rc=rc, hf=hf: dt["opw"][rc * 128:(rc + 1) * 128,
                                                         hf * 512:(hf + 1) * 512]),
                         (lambda rc=rc, hf=hf: dt["hb_opb"][rc * 128:(rc + 1) * 128,
                                                            hf * 512:(hf + 1) * 512]),
                         ("hopc", rc, hf))
                    yield

        def win_outs_gen(w):
            t0 = w * 256
            for oc in range(NCH):
                bank = oc % 2
                for i in range(NCH):
                    E("tensor", (lambda e, oc=oc, i=i, bank=bank, t0=t0:
                                 e.matmul(pb[bank][:, 0:256],
                                          arena[:, DTC + i * 1024 + oc * 128: DTC + i * 1024 + (oc + 1) * 128],
                                          xn[:, i * T + t0: i * T + t0 + 256],
                                          start=(i == 0), stop=False, skip_group_check=True)),
                      reads=[("xn1", i, t0 // 512), ("dT", i)], writes=[PSK[bank]], mm=True)
                    if i % 3 == 2:
                        yield
                for j in range(2):
                    E("tensor", (lambda e, oc=oc, j=j, bank=bank, t0=t0:
                                 e.matmul(pb[bank][:, 0:256],
                                          arena[:, CTC + j * 1024 + oc * 128: CTC + j * 1024 + (oc + 1) * 128],
                                          arena[:, STB + j * T + t0: STB + j * T + t0 + 256],
                                          start=False, stop=(j == 1), skip_group_check=True)),
                      reads=[("st", t0 + 255, j), ("cT", j)], writes=[PSK[bank]], mm=True)
                E("vector", (lambda e, oc=oc, bank=bank, t0=t0:
                             e.tensor_add(arena[:, YPR + oc * 256: YPR + (oc + 1) * 256],
                                          pb[bank][:, 0:256],
                                          xn[:, oc * T + t0: oc * T + t0 + 256])),
                  reads=[PSK[bank], ("xn1", oc, t0 // 512), ("ypreg",)],
                  writes=[("ypre", w, oc)],
                  conflicts=([("ypre", w - 1, oc)] if w >= 1 else []))
                yield
            yield from emit_ln(arena[:, YPR:YPR + 2048], lambda i, qc, w=w: ("ypre", w, i),
                               arena[:, Y1A:Y1A + 2048], lambda i, qc, w=w: ("y1a", w, i),
                               256, "lnw_col", "lnb_col", onescol=onesc_b,
                               dst_extra=[("y1reg",)],
                               dst_conflicts=([("y1a", w - 1, i_) for i_ in range(NCH)]
                                              if w >= 1 else []))

        def win_ffn_gen(w):
            t0 = w * 256
            for bk in range(4, 8):
                E("tensor", (lambda e, bk=bk: e.matmul(pb[bk][:, 0:512], zrow_b[0:1, 0:128],
                                                       zrow_b[0:1, 0:512], start=True, stop=True)),
                  reads=KC(zrow_b), writes=[PSK[bk]], mm=True)
            yield
            for f in range(NCH):
                ws = f % 2
                w1keys = [("w1b", ws, i) for i in range(NCH)]
                w1old = claim(W1S[ws], W1S[ws] + 4096, w1keys)
                for i in range(NCH):
                    E("sync", (lambda e, f=f, i=i, ws=ws:
                                 e.dma_start(arena[:, W1S[ws] + i * 512: W1S[ws] + (i + 1) * 512],
                                             dt["hb_w1b"][i * 128:(i + 1) * 128, f * 512:(f + 1) * 512])),
                      reads=[("hw1c", f, i)],
                      writes=[("w1b", ws, i)], conflicts=w1old, dma_sem="dw1" + str(ws))
                yield
                for sub in range(4):
                    ws2 = (f * 4 + sub) % 2
                    w2old = claim(W2S[ws2], W2S[ws2] + 1024, [("w2b", ws2)])
                    E("sync", (lambda e, f=f, sub=sub, ws2=ws2:
                                 e.dma_start(arena[:, W2S[ws2]: W2S[ws2] + 1024],
                                             dt["hb_w2b"][f * 512 + sub * 128: f * 512 + (sub + 1) * 128, :])),
                      reads=[("hw2c", f * 4 + sub, 0), ("hw2c", f * 4 + sub, 1)],
                      writes=[("w2b", ws2)], conflicts=[("w2b", ws2)] + list(w2old),
                      dma_sem="dw2" + str(ws2))
                    bank = sub % 2
                    for i in range(NCH):
                        E("tensor", (lambda e, sub=sub, i=i, bank=bank, ws=ws:
                                     e.matmul(pb[bank][:, 0:256],
                                              arena[:, W1S[ws] + i * 512 + sub * 128: W1S[ws] + i * 512 + (sub + 1) * 128],
                                              arena[:, Y1A + i * 256: Y1A + (i + 1) * 256],
                                              start=(i == 0), stop=(i == NCH - 1),
                                              skip_group_check=True)),
                          reads=[("y1a", w, i), ("w1b", ws, i)], writes=[PSK[bank]], mm=True)
                        if i % 3 == 2:
                            yield
                    E("scalar", (lambda e, f=f, sub=sub, bank=bank:
                                 e.activation(h_sb[:, sub * 256:(sub + 1) * 256], pb[bank][:, 0:256],
                                              AF.Gelu, bias=smalls["b1_col"][:, f * 4 + sub: f * 4 + sub + 1])),
                      reads=[PSK[bank], K_SM("b1_col")], writes=[("h", sub)])
                    yield
                    for oc in range(NCH):
                        E("tensor", (lambda e, f=f, sub=sub, oc=oc, ws2=ws2:
                                     e.matmul(pb[4 + oc // 2][:, (oc % 2) * 256:(oc % 2) * 256 + 256],
                                              arena[:, W2S[ws2] + oc * 128: W2S[ws2] + (oc + 1) * 128],
                                              h_sb[:, sub * 256:(sub + 1) * 256],
                                              start=False, stop=(f == NCH - 1 and sub == 3),
                                              skip_group_check=True)),
                          reads=[("h", sub), ("w2b", ws2)], writes=[PSK[4 + oc // 2]], mm=True)
                        if oc % 3 == 2:
                            yield
                    yield
            for oc in range(NCH):
                E("vector", (lambda e, oc=oc:
                             e.tensor_scalar(vsc[0][:, 0:256],
                                             pb[4 + oc // 2][:, (oc % 2) * 256:(oc % 2) * 256 + 256],
                                             smalls["b2_col"][:, oc:oc + 1], None, ALU.add)),
                  reads=[PSK[4 + oc // 2], K_SM("b2_col")], writes=[("vsc", id(vsc[0]))])
                E("vector", (lambda e, oc=oc, w=w:
                             e.tensor_add(vsc[1][:, 0:256], vsc[0][:, 0:256],
                                          arena[:, Y1A + oc * 256: Y1A + (oc + 1) * 256])),
                  reads=[("vsc", id(vsc[0])), ("y1a", w, oc)], writes=[("vsc", id(vsc[1]))])
                E("vector", (lambda e, oc=oc, t0=t0:
                             e.tensor_add(x1[:, oc * T + t0: oc * T + t0 + 256],
                                          x1[:, oc * T + t0: oc * T + t0 + 256],
                                          vsc[1][:, 0:256])),
                  reads=[("vsc", id(vsc[1])), ("x1", oc)], writes=[("x1", oc)])
                yield

        import collections as _coll
        fillers = _coll.deque()

        def pump(k=1):
            done = 0
            while fillers and done < k:
                try:
                    next(fillers[0])
                    done += 1
                except StopIteration:
                    fillers.popleft()

        emit_scan_bank_prep(0)
        emit_scan_bank_prep(1)
        for step in range(T):
            r, u = divmod(step, RUN)
            if u == 0:
                if r == 0:
                    fillers.append(cast_gen())
                if r >= 2:
                    fillers.append(win_ffn_gen(r - 2))
                if r >= 1:
                    fillers.append(win_outs_gen(r - 1))
                if r == 7:
                    fillers.append(win_ffn_gen(6))
            bank = 2 + r % 2
            for (jj, oj) in ((0, 0), (1, 0), (0, 1), (1, 1)):
                if step == 0:
                    rhs_fn = (lambda e, jj=jj: zero2[:, jj:jj + 1])
                    rkey = ("zero2",)
                else:
                    rhs_fn = (lambda e, jj=jj, step=step: arena[:, STB + jj * T + step - 1: STB + jj * T + step])
                    rkey = ("st", step - 1, jj)
                E("tensor", (lambda e, u=u, bank=bank, jj=jj, oj=oj, rhs_fn=rhs_fn:
                             e.matmul(pb[bank][:, 2 * u + oj:2 * u + oj + 1],
                                      at_sb[:, (2 * jj + oj) * 128:(2 * jj + oj + 1) * 128],
                                      rhs_fn(e),
                                      start=False, stop=(jj == 1), skip_group_check=True)),
                  reads=[rkey, ("at", jj, oj)], writes=[PSK[bank]], mm=True)
            E("scalar", (lambda e, step=step, u=u, bank=bank:
                         e.activation(arena[:, STB + step: STB + 2 * T: T],
                                      pb[bank][:, 2 * u:2 * u + 2], AF.Tanh)),
              reads=[PSK[bank], ("streg",)], writes=[("st", step, 0), ("st", step, 1)])
            if u == 0 and 2 <= r + 1 < n_runs:
                emit_scan_bank_prep(r + 1)
            pump(1)
        fillers.append(win_outs_gen(7))
        fillers.append(win_ffn_gen(7))
        while fillers:
            pump(1000)

    if STAGES["l1f"]:
        selc = sb("sb_selc", (128, NCORES), F32)
        ipbc = sb("sb_ipbc", (128, 3 * NCH), F32)
        opwb0 = attq[:, 0:1024]
        ptile2 = sb("ptile2", (128, 6 * 128), BF16)
        mrec = recip_sb[:, 0:128]
        ohsb = sb("ohsb", (128, 16 * 8), BF16)
        sgk = sb("sgk", (128, 16 * 4), F32)
        mx8 = sb("mx8", (128, 8), F32)
        ix8 = sb("ix8", (128, 8), mybir.dt.uint32)
        gsc = sb("gsc", (128, 8), F32)
        priors = sb("priors", (1, 8), F32)
        cntall = sb("cntall", (64, 8), F32)
        m64sb = sb("sb_m64", (64, 8), F32)
        iota8r = sb("sb_iota8r", (128, 8), F32)
        sbst = x1[:, 4096:6144]

        for nm, buf in [("selcol", selc), ("ipb_col", ipbc), ("m64", m64sb),
                        ("iota8r", iota8r)]:
            if buf.dtype == F32:
                E("sync", (lambda e, nm=nm, buf=buf: e.dma_start(buf[:, :], dt[nm][:, :])),
                  writes=[("small", nm)], dma_sem="dsm2")
            else:
                E("gpsimd", (lambda e, nm=nm, buf=buf: e.dma_start(buf[:, :], dt[nm][:, :])),
                  writes=[("small", nm)], dma_sem="dsb2")

        for i in range(NCH):
            E("sync", (lambda e, i=i: e.dma_start(dt["ag_xn1_in"][i * 128:(i + 1) * 128, :],
                                                  xn[:, i * T:(i + 1) * T])),
              reads=[("xn1", i, 0), ("xn1", i, 1), ("xn1", i, 2), ("xn1", i, 3)],
              writes=[("agin", i)], dma_sem="dagi")
        E("gpsimd", (lambda e: e.collective_compute(
            "AllGather", ALU.bypass, replica_groups=[list(range(NCORES))],
            ins=[dt["ag_xn1_in"].opt()], outs=[dt["ag_xn1_out"].opt()])),
          reads=[("agin", i) for i in range(NCH)], writes=[("ago_xn1",)], cc=True,
          dma_sem="ccx")
        for i in range(NCH):
            E("gpsimd", (lambda e, i=i: e.dma_start(dt["ag_s_in"][i * 128:(i + 1) * 128, :],
                                                    x1[:, i * T:(i + 1) * T])),
              reads=[("x1", i)], writes=[("agsin", i)], dma_sem="dags")
        E("gpsimd", (lambda e: e.collective_compute(
            "AllGather", ALU.bypass, replica_groups=[list(range(NCORES))],
            ins=[dt["ag_s_in"].opt()], outs=[dt["ag_s_out"].opt()])),
          reads=[("agsin", i) for i in range(NCH)], writes=[("ago_s",)], cc=True,
          dma_sem="ccs")

        fr = x1[:, 8192:9984].bitcast(BF16)
        for bh in range(2):
            for fch in range(32):
                s1 = fch % 2
                E("gpsimd", (lambda e, fch=fch, s1=s1: e.dma_start(
                    fr[:, s1 * 1024:(s1 + 1) * 1024],
                    dt["myw1T"][fch * 128:(fch + 1) * 128, :])),
                  writes=[("fw1", s1)],
                  conflicts=[("fw1", s1)] + ([("x1", 4)] if (bh == 0 and fch < 2) else []),
                  dma_sem="dfw" + str(s1))
                E("gpsimd", (lambda e, fch=fch, s1=s1, bh=bh: e.dma_start(
                    fr[:, 2048 + s1 * 512: 2048 + (s1 + 1) * 512],
                    dt["myw2"][fch * 128:(fch + 1) * 128, bh * 512:(bh + 1) * 512])),
                  writes=[("fw2", s1)], conflicts=[("fw2", s1)], dma_sem="dfv" + str(s1))
                for a in range(8):
                    E("tensor", (lambda e, s1=s1, a=a, fch=fch:
                                 e.matmul(pb[a][:, 0:512],
                                          fr[:, s1 * 1024 + a * 128:
                                             s1 * 1024 + (a + 1) * 128],
                                          fr[:, 2048 + s1 * 512: 2048 + (s1 + 1) * 512],
                                          start=(fch == 0), stop=(fch == 31),
                                          skip_group_check=True)),
                      reads=[("fw1", s1), ("fw2", s1)], writes=[PSK[a]], mm=True)
            for a in range(8):
                E("scalar", (lambda e, a=a: e.copy(fr[:, 3072:3584], pb[a][:, 0:512])),
                  reads=[PSK[a]], writes=[("fstg",)])
                E("sync", (lambda e, a=a, bh=bh: e.dma_start(
                    dt["ag_wf_in"][a * 128:(a + 1) * 128, bh * 512:(bh + 1) * 512],
                    fr[:, 3072:3584])),
                  reads=[("fstg",)], writes=[("wfin", bh, a)], dma_sem="dwfi")
        E("gpsimd", (lambda e: e.dma_start(ohsb[:, 0:32], dt["myb1c"][:, :])),
          writes=[("b1c",)], dma_sem="dsb2")
        for bh in range(2):
            for fch in range(32):
                s1 = fch % 2
                E("gpsimd", (lambda e, fch=fch, s1=s1, bh=bh: e.dma_start(
                    fr[:, 2048 + s1 * 512: 2048 + (s1 + 1) * 512],
                    dt["myw2"][fch * 128:(fch + 1) * 128, bh * 512:(bh + 1) * 512])),
                  writes=[("fw2", s1)], conflicts=[("fw2", s1)], dma_sem="dfv" + str(s1))
                E("tensor", (lambda e, fch=fch, s1=s1:
                             e.matmul(pb[2][0:1, 0:512], ohsb[:, fch:fch + 1],
                                      fr[:, 2048 + s1 * 512: 2048 + (s1 + 1) * 512],
                                      start=(fch == 0), stop=False,
                                      skip_group_check=True)),
                  reads=[("b1c",), ("fw2", s1)], writes=[PSK[2]], mm=True)
            E("gpsimd", (lambda e, bh=bh: e.dma_start(
                pt[0][0:1, 0:512], dt["myb2r"][0:1, bh * 512:(bh + 1) * 512])),
              writes=[("b2r",)], conflicts=[("b2r",), ("rec16",), ("d45",)],
              dma_sem="db2r")
            E("tensor", (lambda e: e.matmul(pb[2][0:1, 0:512], onesr_b[0:1, 0:1],
                                            pt[0][0:1, 0:512], start=False, stop=True,
                                            skip_group_check=True)),
              reads=[("b2r",)] + KC(onesr_b), writes=[PSK[2]], mm=True)
            E("scalar", (lambda e: e.copy(fr[0:1, 3072:3584], pb[2][0:1, 0:512])),
              reads=[PSK[2]], writes=[("fstg",)])
            E("sync", (lambda e, bh=bh: e.dma_start(
                dt["ag_wf_in"][1024:1025, bh * 512:(bh + 1) * 512],
                fr[0:1, 3072:3584])),
              reads=[("fstg",)], writes=[("wfinb", bh)], dma_sem="dwfi")
        E("gpsimd", (lambda e: e.collective_compute(
            "AllGather", ALU.bypass, replica_groups=[list(range(NCORES))],
            ins=[dt["ag_wf_in"].opt()], outs=[dt["ag_wf_out"].opt()])),
          reads=[("wfin", bh, a) for bh in range(2) for a in range(8)]
                + [("wfinb", bh) for bh in range(2)],
          writes=[("ago_wf",)], cc=True, dma_sem="ccw")

        selold = claim(0, 8192, [("cselscr",)])
        for i in range(NCH):
            for hh in range(2):
                E("gpsimd", (lambda e, i=i, hh=hh: e.dma_start(
                    arena[:, 0:8192].rearrange("p (b t) -> p b t", b=4),
                    dt["ag_xn1_out"][hh * 4:(hh + 1) * 4, i * 128:(i + 1) * 128, :]
                        .transpose([1, 0, 2]))),
                  reads=[("ago_xn1",)], writes=[("cselscr",)], dma_sem="dsel",
                  conflicts=(selold if (i == 0 and hh == 0) else ()))
                for w in range(NCORES):
                    mkdst = (lambda e, i=i, hh=hh:
                             xn[:, i * 2048: (i + 1) * 2048]
                             .rearrange("p (t b) -> p t b", b=8)
                             [:, :, hh * 4:(hh + 1) * 4])
                    mksrc = (lambda e, w=w: arena[:, 0:8192]
                             .rearrange("p (b t) -> p t b", b=4)
                             [:, w * 256:(w + 1) * 256, :])
                    if w == 0:
                        E("vector", (lambda e, w=w, mkdst=mkdst, mksrc=mksrc:
                                     e.tensor_scalar(mkdst(e), mksrc(e), selc[:, w:w + 1],
                                                     None, ALU.mult)),
                          reads=[("cselscr",), ("small", "selcol")],
                          writes=[("xn1t", i, hh)],
                          conflicts=[("xn1", i, q_) for q_ in range(4)])
                    else:
                        E("vector", (lambda e, w=w, mkdst=mkdst, mksrc=mksrc:
                                     e.scalar_tensor_tensor(mkdst(e), mksrc(e),
                                                            selc[:, w:w + 1], mkdst(e),
                                                            ALU.mult, ALU.add)),
                          reads=[("cselscr",), ("small", "selcol"), ("xn1t", i, hh)],
                          writes=[("xn1t", i, hh)])
        XNT = lambda i: [("xn1t", i, 0), ("xn1t", i, 1)]

        def xslab(e, i, t0, n=64):
            sl = t0 // 64
            return xn[:, i * 2048 + sl * 512: i * 2048 + (sl + 1) * 512]

        ipold = claim(0, 24576, [("ipw", i) for i in range(NCH)])
        for i in range(NCH):
            E("gpsimd", (lambda e, i=i: e.dma_start(arena[:, i * 3072:(i + 1) * 3072],
                                                    dt["ipw"][i * 128:(i + 1) * 128, :])),
              writes=[("ipw", i)], conflicts=ipold, dma_sem="dw")
        QS, KS, VS = 24576, 28672, 32768
        opw_last = {}
        ypsl = arena[:, 32768:36864]
        y1sl = arena[:, 28672:32768]
        qold = claim(QS, QS + 4096, [("qsb", j) for j in range(8)])
        kold = claim(KS, KS + 4096, [("ksb", j) for j in range(8)])
        vold2 = claim(VS, VS + 4096, [("vsb", j, hf) for j in range(4) for hf in range(2)])

        ypsl = x1[:, 10240:12288].bitcast(BF16)
        y1sl = x1[:, 12288:14336].bitcast(BF16)

        def c1_qkv_gen(sl_):
            for och in range(16):
                bank = och % 2
                dstoff = QS + och * 512 if och < 8 else KS + (och - 8) * 512
                for i in range(NCH):
                    E("tensor", (lambda e, och=och, i=i, bank=bank, sl_=sl_:
                                 e.matmul(pb[bank][:, 0:512],
                                          arena[:, i * 3072 + och * 128: i * 3072 + (och + 1) * 128],
                                          xn[:, i * 2048 + sl_ * 512:
                                             i * 2048 + (sl_ + 1) * 512],
                                          start=(i == 0), stop=(i == NCH - 1),
                                          skip_group_check=True)),
                      reads=XNT(i) + [("ipw", i)], writes=[PSK[bank]], mm=True)
                    if i % 3 == 2:
                        yield
                E("scalar", (lambda e, och=och, bank=bank, dstoff=dstoff:
                             e.activation(arena[:, dstoff:dstoff + 512], pb[bank][:, 0:512],
                                          AF.Identity, bias=ipbc[:, och:och + 1])),
                  reads=[PSK[bank], ("small", "ipb_col")],
                  writes=[("qsb", och) if och < 8 else ("ksb", och - 8)],
                  conflicts=((qold if sl_ == 0 else []) if och == 0 else
                             ((kold if sl_ == 0 else []) if och == 8 else []))
                            + ([("ctx", och, nb_) for nb_ in range(4)] if och < 8 else []))
                yield
            for vt in range(4):
                for half in range(2):
                    bank = half
                    for i in range(NCH):
                        E("tensor", (lambda e, vt=vt, half=half, i=i, bank=bank, sl_=sl_:
                                     e.matmul(pb[bank][:, 0:512],
                                              xn[:, i * 2048 + sl_ * 512 + vt * 128:
                                                 i * 2048 + sl_ * 512 + (vt + 1) * 128],
                                              arena[:, i * 3072 + 2048 + half * 512:
                                                    i * 3072 + 2048 + (half + 1) * 512],
                                              start=(i == 0), stop=False,
                                              skip_group_check=True)),
                          reads=XNT(i) + [("ipw", i)], writes=[PSK[bank]], mm=True)
                        if i % 3 == 2:
                            yield
                    E("tensor", (lambda e, half=half, bank=bank:
                                 e.matmul(pb[bank][:, 0:512], onesr_b[0:1, :],
                                          smallsb["ipb_row"][0:1, half * 512:
                                                             (half + 1) * 512],
                                          start=False, stop=True, skip_group_check=True)),
                      reads=[K_SM("ipb_row")] + KC(onesr_b), writes=[PSK[bank]], mm=True)
                    E("scalar", (lambda e, vt=vt, half=half, bank=bank:
                                 e.copy(arena[:, VS + vt * 1024 + half * 512:
                                              VS + vt * 1024 + (half + 1) * 512],
                                        pb[bank][:, 0:512])),
                      reads=[PSK[bank]], writes=[("vsb", vt, half)],
                      conflicts=(vold2 if (sl_ == 0 and vt == 0 and half == 0) else ()))
                    yield

        def c1_tail_gen(sl_):
            t0 = sl_ * 64
            for th in range(2):
                for bk in range(4, 8):
                    E("tensor", (lambda e, bk=bk: e.matmul(
                        pb[bk][:, 0:512], zrow_b[0:1, 0:128], zrow_b[0:1, 0:512],
                        start=True, stop=True)),
                      reads=KC(zrow_b), writes=[PSK[bk]], mm=True)
                for i in range(NCH):
                    ws3 = i % 2
                    wb = opwb0 if ws3 == 0 else h_sb
                    prevk = opw_last.get(ws3)
                    E("sync", (lambda e, i=i, wb=wb, th=th:
                                 e.dma_start(wb[:, 0:1024], dt["hb_opb"][i * 128:(i + 1) * 128, :])),
                      reads=[("hopc", i, 0), ("hopc", i, 1)],
                      writes=[("opwb", sl_, th, i)],
                      conflicts=([prevk] if prevk else
                                 ([("cst", s_) for s_ in range(4)] if ws3 == 0 else [])),
                      dma_sem="dop" + str(ws3))
                    opw_last[ws3] = ("opwb", sl_, th, i)
                    for oc in range(NCH):
                        E("tensor", (lambda e, oc=oc, i=i, wb=wb, th=th:
                                     e.matmul(pb[4 + oc // 2][:, (oc % 2) * 256:
                                                             (oc % 2) * 256 + 256],
                                              wb[:, oc * 128:(oc + 1) * 128],
                                              arena[:, QS + i * 512 + th * 256:
                                                    QS + i * 512 + th * 256 + 256],
                                              start=False, stop=(i == NCH - 1),
                                              skip_group_check=True)),
                          reads=[("ctx", i, 0), ("ctx", i, 1), ("ctx", i, 2), ("ctx", i, 3),
                                 ("opwb", sl_, th, i)],
                          writes=[PSK[4 + oc // 2]], mm=True)
                for oc in range(NCH):
                    E("vector", (lambda e, oc=oc, t0=t0, th=th, sl_=sl_:
                                 e.tensor_add(vsc[0][:, 0:256],
                                              pb[4 + oc // 2][:, (oc % 2) * 256:
                                                              (oc % 2) * 256 + 256],
                                              xn[:, oc * 2048 + sl_ * 512 + th * 256:
                                                 oc * 2048 + sl_ * 512 + th * 256 + 256])),
                      reads=[PSK[4 + oc // 2]] + XNT(oc), writes=[("vsc", id(vsc[0]))])
                    E("vector", (lambda e, oc=oc, th=th:
                                 e.tensor_scalar(ypsl[:, oc * 512 + th * 256:
                                                     oc * 512 + th * 256 + 256],
                                                 vsc[0][:, 0:256],
                                                 smalls["opb_col"][:, oc:oc + 1],
                                                 None, ALU.add)),
                      reads=[("vsc", id(vsc[0])), K_SM("opb_col")], writes=[("ypsl", oc)],
                      conflicts=([("x1", 5)] if (sl_ == 0 and th == 0 and oc == 0) else ()))
                    if th == 1:
                        yield
            yield from emit_ln(ypsl, lambda i, qc: ("ypsl", i), y1sl,
                               lambda i, qc: ("y1sl", i), 512,
                               "ln1g_col", "ln1b_col", onescol=onesc_b,
                               dst_conflicts=([("x1", 6)] if sl_ == 0 else ()))
            for i in range(NCH):
                E("vector", (lambda e, i=i: e.tensor_copy(
                    pt[i % 2][:, 0:512].rearrange("p (b t) -> p b t", b=8),
                    y1sl[:, i * 512:(i + 1) * 512].rearrange("p (t b) -> p b t", b=8))),
                  reads=[("y1sl", i)], writes=[("y1stg", i % 2)],
                  conflicts=([("rec16",), ("d01",), ("d23",), ("d45",)]
                             if (sl_ == 0 and i < 2) else ()))
                E("gpsimd", (lambda e, i=i, t0=t0: e.dma_start(
                    dt["hb_y1"][i * 128:(i + 1) * 128, :]
                        .rearrange("p (b t) -> p b t", b=8)[:, :, t0:t0 + 64],
                    pt[i % 2][:, 0:512].rearrange("p (b t) -> p b t", b=8))),
                  reads=[("y1stg", i % 2)], writes=[("hby1", sl_, i)], dma_sem="dy1")
                yield

        def rr(g1, g2):
            gens = [g for g in (g1, g2) if g is not None]
            while gens:
                nxt = []
                for g in gens:
                    try:
                        next(g)
                        nxt.append(g)
                    except StopIteration:
                        pass
                gens = nxt

        DRAIN(c1_qkv_gen(0))
        pend_qkv = None
        for sl_ in range(4):
            t0 = sl_ * 64
            for nb in range(4):
                E("tensor", (lambda e: e.matmul(pb[1][:, 0:512], zrow_b[0:1, 0:128],
                                                zrow_b[0:1, 0:512], start=True, stop=True)),
                  reads=KC(zrow_b), writes=[PSK[1]], mm=True)
                for h in range(4):
                    bank = 2 + h % 2
                    for ch in range(2):
                        E("tensor", (lambda e, nb=nb, h=h, ch=ch, bank=bank:
                                     e.matmul(pb[bank][:, 0:128],
                                              arena[:, KS + (2 * h + ch) * 512 + nb * 128:
                                                    KS + (2 * h + ch) * 512 + (nb + 1) * 128],
                                              arena[:, QS + (2 * h + ch) * 512 + nb * 128:
                                                    QS + (2 * h + ch) * 512 + (nb + 1) * 128],
                                              start=(ch == 0), stop=(ch == 1),
                                              skip_group_check=True)),
                          reads=[("qsb", 2 * h + ch), ("ksb", 2 * h + ch)],
                          writes=[PSK[bank]], mm=True)
                    E("vector", (lambda e, bank=bank:
                                 e.tensor_add(vsc[0][:, 0:128], pb[bank][:, 0:128],
                                              smallsb["mha_mask"][:, :])),
                      reads=[PSK[bank], K_SM("mha_mask")], writes=[("vsc", id(vsc[0]))])
                    E("scalar", (lambda e, h=h:
                                 e.activation(ptile2[:, 256 + h * 128: 256 + (h + 1) * 128],
                                              vsc[0][:, 0:128], AF.Exp, scale=1.0 / 16.0)),
                      reads=[("vsc", id(vsc[0]))], writes=[("pt2s", h)])
                    E("tensor", (lambda e, h=h:
                                 e.matmul(pb[1][32 * h:32 * h + 1, 0:128], onesc_b[:, 0:1],
                                          ptile2[:, 256 + h * 128: 256 + (h + 1) * 128],
                                          start=False, stop=True, skip_group_check=True,
                                          tile_position=(0, 32 * h))),
                      reads=[("pt2s", h)] + KC(onesc_b), writes=[PSK[1]], mm=True)
                E("vector", (lambda e: e.reciprocal(mrec[:, :], pb[1][:, 0:128])),
                  reads=[PSK[1]], writes=[("mrec",)])
                for h in range(4):
                    E("tensor", (lambda e, h=h: e.matmul(pb[0][:, 0:128],
                                                         onesr_f[32 * h:32 * h + 1, :],
                                                         mrec[32 * h:32 * h + 1, 0:128],
                                                         start=True, stop=True,
                                                         tile_position=(32 * h, 0))),
                      reads=[("mrec",)] + KC(onesr_f), writes=[PSK[0]], mm=True)
                    E("vector", (lambda e, h=h:
                                 e.tensor_mul(ptile2[:, (h % 2) * 128:(h % 2) * 128 + 128],
                                              ptile2[:, 256 + h * 128: 256 + (h + 1) * 128],
                                              pb[0][:, 0:128])),
                      reads=[("pt2s", h), PSK[0]], writes=[("pt2", h % 2)])
                    for ch in range(2):
                        bank = 4 + (2 * h + ch) % 4
                        E("tensor", (lambda e, nb=nb, h=h, ch=ch, bank=bank:
                                     e.matmul(pb[bank][:, 0:128],
                                              arena[:, VS + nb * 1024 + h * 256 + ch * 128:
                                                    VS + nb * 1024 + h * 256 + (ch + 1) * 128],
                                              ptile2[:, (h % 2) * 128:(h % 2) * 128 + 128],
                                              start=True, stop=True)),
                          reads=[("pt2", h % 2), ("vsb", nb, (h * 2 + ch) // 4)],
                          writes=[PSK[bank]], mm=True)
                        E("scalar", (lambda e, nb=nb, h=h, ch=ch, bank=bank:
                                     e.copy(arena[:, QS + (2 * h + ch) * 512 + nb * 128:
                                                  QS + (2 * h + ch) * 512 + (nb + 1) * 128],
                                            pb[bank][:, 0:128])),
                          reads=[PSK[bank]], writes=[("ctx", 2 * h + ch, nb)],
                          conflicts=[("qsb", 2 * h + ch)])
            tail = c1_tail_gen(sl_)
            rr(tail, c1_qkv_gen(sl_ + 1) if sl_ < 3 else None)
        wgb = sb("wgb", (128, NCH * NEXP), BF16)
        slrow = recb[0:1, 0:128]
        for i in range(NCH):
            E("gpsimd", (lambda e, i=i: e.dma_start(wgb[:, i * 8:(i + 1) * 8],
                                                    dt["wg"][i * 128:(i + 1) * 128, :])),
              writes=[("wgb", i)], dma_sem="dwgb")
        for bk in (6, 7):
            E("tensor", (lambda e, bk=bk: e.matmul(pb[bk][:, 0:512], zrow_b[0:1, 0:128],
                                                   zrow_b[0:1, 0:512], start=True, stop=True)),
              reads=KC(zrow_b), writes=[PSK[bk]], mm=True)
        for tt in range(16):
            ws = tt % 2
            E("gpsimd", (lambda e, tt=tt, ws=ws: e.dma_start(
                y1tb[:, ws * 1024:(ws + 1) * 1024].rearrange("p (i c) -> p i c", i=8),
                dt["hb_y1"][:, tt * 128:(tt + 1) * 128].rearrange("(i p) c -> p i c", p=128))),
              reads=[("hby1", s_, i_) for s_ in range(4) for i_ in range(NCH)],
              writes=[("y1g", ws)],
              conflicts=[("y1g", ws)] + ([("attp", p_) for p_ in range(4)]
                                         if tt < 2 else []),
              dma_sem="dyg" + str(ws))
            bank = 2 + ws
            for i in range(NCH):
                E("tensor", (lambda e, tt=tt, ws=ws, i=i, bank=bank:
                             e.matmul(pb[bank][:, 0:8],
                                      y1tb[:, ws * 1024 + i * 128: ws * 1024 + (i + 1) * 128],
                                      wgb[:, i * 8:(i + 1) * 8],
                                      start=(i == 0), stop=(i == NCH - 1),
                                      skip_group_check=True)),
                  reads=[("y1g", ws), ("wgb", i)], writes=[PSK[bank]], mm=True)
            E("vector", (lambda e, bank=bank: e.tensor_copy(gsc[:, :], pb[bank][:, 0:8])),
              reads=[PSK[bank]], writes=[("gsc",)])
            E("vector", (lambda e: e.max(mx8[:, :], gsc[:, :])),
              reads=[("gsc",)], writes=[("mx8",)])
            E("vector", (lambda e: e.max_index(ix8[:, :], mx8[:, :], gsc[:, :])),
              reads=[("gsc",), ("mx8",)], writes=[("ix8",)])
            E("vector", (lambda e: e.tensor_scalar(vsc[0][:, 0:1], mx8[:, 0:1], -1.0,
                                                   None, ALU.mult)),
              reads=[("mx8",)], writes=[("vsc", id(vsc[0]))])
            E("scalar", (lambda e: e.activation(vsc[1][:, 0:8], gsc[:, :], AF.Exp,
                                                bias=vsc[0][:, 0:1])),
              reads=[("gsc",), ("vsc", id(vsc[0]))], writes=[("vsc", id(vsc[1]))])
            E("vector", (lambda e: e.reduce_sum(vsc[0][:, 1:2], vsc[1][:, 0:8],
                                                mybir.AxisListType.X)),
              reads=[("vsc", id(vsc[1]))], writes=[("vsc", id(vsc[0]))])
            E("vector", (lambda e, tt=tt: e.reciprocal(sgk[:, 32 + tt:32 + tt + 1],
                                                       vsc[0][:, 1:2])),
              reads=[("vsc", id(vsc[0]))], writes=[("gv", tt)])
            E("vector", (lambda e, tt=tt: e.tensor_copy(sgk[:, 16 + tt:16 + tt + 1],
                                                        ix8[:, 0:1])),
              reads=[("ix8",)], writes=[("ixf", tt)])
            E("vector", (lambda e, tt=tt: e.tensor_scalar(ohsb[:, tt * 8:(tt + 1) * 8],
                                                          iota8r[:, :],
                                                          sgk[:, 16 + tt:16 + tt + 1],
                                                          None, ALU.is_equal)),
              reads=[("ixf", tt), ("small", "iota8r")], writes=[("oh", tt)],
              conflicts=([("b1c",)] if tt < 4 else []))
            b_ = tt // 2
            E("tensor", (lambda e, tt=tt, b_=b_:
                         e.matmul(pb[6 + b_ // 4][32 * (b_ % 4):32 * (b_ % 4) + 1, 0:8],
                                  onesc_b[:, 0:1], ohsb[:, tt * 8:(tt + 1) * 8],
                                  start=False, stop=(tt % 2 == 1), skip_group_check=True,
                                  tile_position=(0, 32 * (b_ % 4)))),
              reads=[("oh", tt)] + KC(onesc_b), writes=[PSK[6 + b_ // 4]], mm=True)
        cnt_sb = sb("cnt_sb", (128, 16), F32)
        for bk in (6, 7):
            E("scalar", (lambda e, bk=bk: e.copy(cnt_sb[:, (bk - 6) * 8:(bk - 5) * 8],
                                                 pb[bk][:, 0:8])),
              reads=[PSK[bk]], writes=[("cnt_sb", bk)])
        for b_ in range(8):
            E("gpsimd", (lambda e, b_=b_: e.dma_start(
                dt["ag_cnt_in"][b_:b_ + 1, :],
                cnt_sb[32 * (b_ % 4):32 * (b_ % 4) + 1, (b_ // 4) * 8:(b_ // 4) * 8 + 8])),
              reads=[("cnt_sb", 6 + b_ // 4)], writes=[("cntin", b_)], dma_sem="dcnt")
        E("gpsimd", (lambda e: e.collective_compute(
            "AllGather", ALU.bypass, replica_groups=[list(range(NCORES))],
            ins=[dt["ag_cnt_in"].opt()], outs=[dt["ag_cnt_out"].opt()])),
          reads=[("cntin", b_) for b_ in range(8)], writes=[("ago_cnt",)], cc=True,
          dma_sem="ccc")
        E("sync", (lambda e: e.dma_start(cntall[:, :],
                                         dt["ag_cnt_out"].rearrange("c b e -> (c b) e"))),
          reads=[("ago_cnt",)], writes=[("cntall",)], dma_sem="dcnl")
        for bk in (0, 1):
            E("tensor", (lambda e, bk=bk: e.matmul(pb[bk][:, 0:512], zrow_b[0:1, 0:128],
                                                   zrow_b[0:1, 0:512], start=True, stop=True)),
              reads=KC(zrow_b), writes=[PSK[bk]], mm=True)
        for b_ in range(8):
            E("tensor", (lambda e, b_=b_: e.matmul(
                pb[b_ // 4][32 * (b_ % 4):32 * (b_ % 4) + 1, 0:8],
                m64sb[:, b_:b_ + 1], cntall[:, :], start=False, stop=True,
                skip_group_check=True, tile_position=(0, 32 * (b_ % 4)))),
              reads=[("cntall",), K_SM("m64")], writes=[PSK[b_ // 4]], mm=True)
        basesr = recb[:, 128:144]
        for bk in (0, 1):
            E("scalar", (lambda e, bk=bk: e.copy(basesr[:, bk * 8:(bk + 1) * 8],
                                                 pb[bk][:, 0:8])),
              reads=[PSK[bk]], writes=[("basesr", bk)])
        E("gpsimd", (lambda e: e.memset(priors[:, :], 0.0)), writes=[("priors",)])
        for tt in range(16):
            bank = 2 + tt % 2
            E("tensor", (lambda e, tt=tt, bank=bank:
                         e.matmul(pb[bank][:, 0:8], smallsb["tri_all"][:, :],
                                  ohsb[:, tt * 8:(tt + 1) * 8], start=True, stop=True)),
              reads=[("oh", tt), K_SM("tri_all")], writes=[PSK[bank]], mm=True)
            E("tensor", (lambda e, tt=tt, bank=bank:
                         e.matmul(pb[bank][:, 0:8], onesr_f[0:1, :], priors[0:1, :],
                                  start=False, stop=True, skip_group_check=True)),
              reads=[("priors",)] + KC(onesr_f), writes=[PSK[bank]], mm=True)
            b_ = tt // 2
            E("tensor", (lambda e, tt=tt, bank=bank, b_=b_:
                         e.matmul(pb[bank][:, 0:8],
                                  onesr_f[32 * (b_ % 4):32 * (b_ % 4) + 1, :],
                                  basesr[32 * (b_ % 4):32 * (b_ % 4) + 1,
                                         (b_ // 4) * 8:(b_ // 4) * 8 + 8],
                                  start=False, stop=True, skip_group_check=True,
                                  tile_position=(32 * (b_ % 4), 0))),
              reads=[("basesr", b_ // 4)] + KC(onesr_f), writes=[PSK[bank]], mm=True)
            E("vector", (lambda e, tt=tt, bank=bank:
                         e.tensor_mul(vsc[0][:, 0:8], pb[bank][:, 0:8],
                                      ohsb[:, tt * 8:(tt + 1) * 8])),
              reads=[PSK[bank], ("oh", tt)], writes=[("vsc", id(vsc[0]))])
            E("vector", (lambda e: e.reduce_sum(vsc[1][:, 3:4], vsc[0][:, 0:8],
                                                mybir.AxisListType.X)),
              reads=[("vsc", id(vsc[0]))], writes=[("locv",)])
            E("vector", (lambda e: e.tensor_scalar(vsc[1][:, 4:5], vsc[1][:, 3:4],
                                                   float(CAPG) - 0.5, None, ALU.is_lt)),
              reads=[("locv",)], writes=[("keep",)])
            E("vector", (lambda e, tt=tt: e.tensor_mul(sgk[:, 32 + tt:32 + tt + 1],
                                                       sgk[:, 32 + tt:32 + tt + 1],
                                                       vsc[1][:, 4:5])),
              reads=[("keep",), ("gv", tt)], writes=[("gv", tt)])
            E("tensor", (lambda e, tt=tt: e.matmul(pb[1][0:1, 0:8], onesc_b[:, 0:1],
                                                   ohsb[:, tt * 8:(tt + 1) * 8],
                                                   start=True, stop=True)),
              reads=[("oh", tt)] + KC(onesc_b), writes=[PSK[1]], mm=True)
            E("vector", (lambda e: e.tensor_add(priors[0:1, :], priors[0:1, :],
                                                pb[1][0:1, 0:8])),
              reads=[PSK[1], ("priors",)], writes=[("priors",)])
        for tt in range(16):
            E("tensor", (lambda e, tt=tt: e.transpose(pb[0][0:1, 0:128],
                                                      sgk[:, 16 + tt:16 + tt + 1],
                                                      smalls["identf"][:, :])),
              reads=[("ixf", tt), K_SM("identf")], writes=[PSK[0]], mm=True)
            E("scalar", (lambda e: e.copy(slrow[0:1, :], pb[0][0:1, 0:128])),
              reads=[PSK[0]], writes=[("slrow",)])
            E("tensor", (lambda e: e.matmul(pb[1][:, 0:128], onesr_f[0:1, :],
                                            slrow[0:1, :], start=True, stop=True)),
              reads=[("slrow",)] + KC(onesr_f), writes=[PSK[1]], mm=True)
            E("vector", (lambda e, tt=tt: e.tensor_copy(sbst[:, tt * 128:(tt + 1) * 128],
                                                        pb[1][:, 0:128])),
              reads=[PSK[1]], writes=[("sbst", tt // 4)],
              conflicts=([("x1", 2), ("x1", 3)] if tt == 0 else ()))
            E("tensor", (lambda e, tt=tt: e.transpose(pb[0][0:1, 0:128],
                                                      sgk[:, 32 + tt:32 + tt + 1],
                                                      smalls["identf"][:, :])),
              reads=[("gv", tt), K_SM("identf")], writes=[PSK[0]], mm=True)
            E("scalar", (lambda e: e.copy(slrow[0:1, :], pb[0][0:1, 0:128])),
              reads=[PSK[0]], writes=[("slrow",)])
            E("tensor", (lambda e: e.matmul(pb[1][:, 0:128], onesr_f[0:1, :],
                                            slrow[0:1, :], start=True, stop=True)),
              reads=[("slrow",)] + KC(onesr_f), writes=[PSK[1]], mm=True)
            E("scalar", (lambda e, tt=tt: e.copy(y1tb[:, tt * 128:(tt + 1) * 128],
                                                 pb[1][:, 0:128])),
              reads=[PSK[1]], writes=[("gkb", tt // 4)],
              conflicts=[("y1g", 0), ("y1g", 1)])

        WER, MER = 16384, 28672
        sscrold = claim(0, 4096, [("sselscr",)])
        ssout = claim(4096, 4608, [("ssres",)])
        for i in range(NCH):
            for q4 in range(4):
                E("gpsimd", (lambda e, i=i, q4=q4: e.dma_start(
                    arena[:, 0:4096].rearrange("p (b t) -> p b t", b=2),
                    dt["ag_s_out"][q4 * 2:(q4 + 1) * 2, i * 128:(i + 1) * 128, :]
                        .transpose([1, 0, 2]))),
                  reads=[("ago_s",), ("sselscr",)], writes=[("sselscr",)],
                  conflicts=(sscrold if (i == 0 and q4 == 0) else ()), dma_sem="dsel2")
                for w in range(NCORES):
                    mkd = (lambda e: arena[:, 4096:4608]
                           .rearrange("p (b t) -> p b t", b=2))
                    mks = (lambda e, w=w: arena[:, 0:4096]
                           .rearrange("p (b t) -> p b t", b=2)[:, :, w * 256:(w + 1) * 256])
                    if w == 0:
                        E("vector", (lambda e, w=w, mkd=mkd, mks=mks:
                                     e.tensor_scalar(mkd(e), mks(e), selc[:, w:w + 1],
                                                     None, ALU.mult)),
                          reads=[("sselscr",), ("small", "selcol")], writes=[("ssres",)],
                          conflicts=(ssout if (i == 0 and q4 == 0) else ()))
                    else:
                        E("vector", (lambda e, w=w, mkd=mkd, mks=mks:
                                     e.scalar_tensor_tensor(mkd(e), mks(e), selc[:, w:w + 1],
                                                            mkd(e), ALU.mult, ALU.add)),
                          reads=[("sselscr",), ("small", "selcol"), ("ssres",)],
                          writes=[("ssres",)])
                E("gpsimd", (lambda e, i=i, q4=q4: e.dma_start(
                    dt["hb_st"][i * 128:(i + 1) * 128, q4 * 512:(q4 + 1) * 512],
                    arena[:, 4096:4608])),
                  reads=[("ssres",)], writes=[("hbst", i, q4)], dma_sem="dst2")

        y1rold = claim(0, 16384, [("y1res", a) for a in range(8)])
        for a in range(8):
            E("sync", (lambda e, a=a: e.dma_start(
                arena[:, a * 2048:(a + 1) * 2048],
                dt["hb_y1"][a * 128:(a + 1) * 128, :])),
              reads=[("hby1", s_, i_) for s_ in range(4) for i_ in range(NCH)],
              writes=[("y1res", a)],
              conflicts=(y1rold if a == 0 else ()), dma_sem="dyr")

        if "y1" in DEBUG:
            for a in range(8):
                E("gpsimd", (lambda e, a=a: e.dma_start(
                    dt["dbg_y1"][a * 128:(a + 1) * 128, :],
                    arena[:, a * 2048:(a + 1) * 2048])),
                  reads=[("y1res", a)], dma_sem="dbg")
        wfold = claim(WER, WER + 8192, [("wfe", a) for a in range(8)])
        meold = claim(MER, MER + 2048, [("mer",)])
        biasr = x1[0:1, 6144:6656].bitcast(BF16)
        macc = x1[:, 8192:16384].bitcast(BF16)
        E("gpsimd", (lambda e: e.memset(x1[:, 8192:16384], 0.0)),
          writes=[("macc", b_) for b_ in range(8)],
          conflicts=[("fw1", 0), ("fw1", 1), ("fw2", 0), ("fw2", 1), ("fstg",),
                     ("x1", 4), ("x1", 5), ("x1", 6), ("x1", 7)]
                    + [("ypsl", j_) for j_ in range(8)]
                    + [("y1sl", j_) for j_ in range(8)])
        for e_ in range(NEXP):
            for a in range(8):
                E("sync", (lambda e, e_=e_, a=a: e.dma_start(
                    arena[:, WER + a * 1024: WER + (a + 1) * 1024],
                    dt["ag_wf_out"][e_, a * 128:(a + 1) * 128, :])),
                  reads=[("ago_wf",)], writes=[("wfe", a)],
                  conflicts=[("wfe", a)] + (list(wfold) if (e_ == 0 and a == 0) else []),
                  dma_sem="dwf")
            E("sync", (lambda e, e_=e_: e.dma_start(
                biasr[0:1, :], dt["ag_wf_out"][e_, 1024:1025, :])),
              reads=[("ago_wf",)], writes=[("biasr",)],
              conflicts=[("biasr",), ("x1", 3)], dma_sem="dwfb")
            E("vector", (lambda e, e_=e_: e.tensor_scalar(
                arena[:, MER:MER + 2048], sbst[:, :], float(e_), None, ALU.is_equal)),
              reads=[("sbst", q_) for q_ in range(4)], writes=[("mer",)],
              conflicts=(meold if e_ == 0 else [("mer",)]))
            E("vector", (lambda e: e.tensor_mul(arena[:, MER:MER + 2048],
                                                arena[:, MER:MER + 2048],
                                                y1tb[:, 0:2048])),
              reads=[("mer",)] + [("gkb", q_) for q_ in range(4)], writes=[("mer",)])
            for a in range(8):
                E("vector", (lambda e, a=a: e.tensor_mul(
                    xn[:, a * 2048:(a + 1) * 2048],
                    arena[:, a * 2048:(a + 1) * 2048],
                    arena[:, MER:MER + 2048])),
                  reads=[("y1res", a), ("mer",)], writes=[("y1m", a)],
                  conflicts=([("xn1t", a, 0), ("xn1t", a, 1)] if e_ == 0
                             else [("y1m", a)]))
            for b_ in range(8):
                for tokc in range(4):
                    bank = (b_ * 4 + tokc) % 2
                    for a in range(8):
                        E("tensor", (lambda e, b_=b_, tokc=tokc, a=a, bank=bank:
                                     e.matmul(pb[bank][:, 0:512],
                                              arena[:, WER + a * 1024 + b_ * 128:
                                                    WER + a * 1024 + (b_ + 1) * 128],
                                              xn[:, a * 2048 + tokc * 512:
                                                 a * 2048 + (tokc + 1) * 512],
                                              start=(a == 0), stop=False,
                                              skip_group_check=True)),
                          reads=[("wfe", a), ("y1m", a)], writes=[PSK[bank]], mm=True)
                    E("tensor", (lambda e, b_=b_, tokc=tokc, bank=bank:
                                 e.matmul(pb[bank][:, 0:512],
                                          biasr[0:1, b_ * 128:(b_ + 1) * 128],
                                          arena[0:1, MER + tokc * 512:
                                                MER + (tokc + 1) * 512],
                                          start=False, stop=True,
                                          skip_group_check=True)),
                      reads=[("biasr",), ("mer",)], writes=[PSK[bank]], mm=True)
                    E("vector", (lambda e, b_=b_, tokc=tokc, bank=bank:
                                 e.tensor_add(
                                     macc[:, b_ * 2048 + tokc * 512:
                                          b_ * 2048 + (tokc + 1) * 512],
                                     macc[:, b_ * 2048 + tokc * 512:
                                          b_ * 2048 + (tokc + 1) * 512],
                                     pb[bank][:, 0:512])),
                      reads=[PSK[bank], ("macc", b_)], writes=[("macc", b_)])

        ypsl2 = arena[:, 24576:28672]
        y1sl2 = arena[:, WER:WER + 4096]
        ypsl2c = claim(24576, 28672, [("ypsl2", i) for i in range(NCH)])
        y1sl2c = claim(WER, WER + 4096, [("y1sl2", i) for i in range(NCH)])
        outsb = x1
        for tc in range(4):
            for m in range(NCH):
                E("vector", (lambda e, tc=tc, m=m:
                             e.tensor_add(ypsl2[:, m * 512:(m + 1) * 512],
                                          arena[:, m * 2048 + tc * 512:
                                                m * 2048 + (tc + 1) * 512],
                                          macc[:, m * 2048 + tc * 512:
                                               m * 2048 + (tc + 1) * 512])),
                  reads=[("y1res", m), ("macc", m)], writes=[("ypsl2", m)],
                  conflicts=(list(ypsl2c) if (tc == 0 and m == 0) else ()))
            DRAIN(emit_ln(ypsl2, lambda i, qc: ("ypsl2", i), y1sl2,
                          lambda i, qc: ("y1sl2", i),
                          512, "ln2g_col", "ln2b_col", onescol=onesc_b,
                          dst_conflicts=(list(y1sl2c) if tc == 0 else ())))
            for m in range(NCH):
                ws = m % 2
                E("gpsimd", (lambda e, tc=tc, m=m, ws=ws: e.dma_start(
                    pt[ws][:, 0:512],
                    dt["hb_st"][m * 128:(m + 1) * 128, tc * 512:(tc + 1) * 512])),
                  reads=[("hbst", m, tc)], writes=[("y1f", ws)],
                  conflicts=[("y1f", ws)] + ([("y1stg", ws), ("b2r",)]
                                             if (tc == 0 and m < 2) else []),
                  dma_sem="dy1f" + str(ws))
                E("vector", (lambda e, tc=tc, m=m, ws=ws:
                             e.tensor_add(x1[:, ws * 512:(ws + 1) * 512],
                                          y1sl2[:, m * 512:(m + 1) * 512],
                                          pt[ws][:, 0:512])),
                  reads=[("y1sl2", m), ("y1f", ws)], writes=[("outsb", ws)],
                  conflicts=[("outdma", ws)] + ([("x1", 0)] if (tc == 0 and m < 2)
                                                else []))
                for hb in range(2):
                    E("sync", (lambda e, tc=tc, m=m, ws=ws, hb=hb: e.dma_start(
                        dt["out_fm"][2 * tc + hb, m * 128:(m + 1) * 128, :],
                        x1[:, ws * 512 + hb * 256: ws * 512 + hb * 256 + 256])),
                      reads=[("outsb", ws)], writes=[("outdma", ws)], dma_sem="dout")
    if "xn1" in DEBUG:
        for i in range(NCH):
            E("gpsimd", (lambda e, i=i: e.dma_start(dt["dbg_xn1"][i * 128:(i + 1) * 128, :],
                                                    xn[:, i * T:(i + 1) * T])),
              reads=[("xn1", i, 0), ("xn1", i, 1), ("xn1", i, 2), ("xn1", i, 3)], dma_sem="dbg")
    if "states" in DEBUG and STAGES["l1a"]:
        for j in range(2):
            E("gpsimd", (lambda e, j=j: e.dma_start(dt["dbg_states"][j * 128:(j + 1) * 128, :],
                                                    arena[:, 28672 + j * T: 28672 + (j + 1) * T])),
              reads=[("st", T - 1, j)], dma_sem="dbg")
    if "ypre" in DEBUG and STAGES["l1a"]:
        for i in range(NCH):
            E("gpsimd", (lambda e, i=i: e.dma_start(dt["dbg_ypre"][i * 128:(i + 1) * 128, :],
                                                    arena[:, 34816 + i * 256: 34816 + (i + 1) * 256])),
              reads=[("ypre", 7, i)], dma_sem="dbg")
    if "y1a" in DEBUG and STAGES["l1a"]:
        for i in range(NCH):
            E("gpsimd", (lambda e, i=i: e.dma_start(dt["dbg_y1a"][i * 128:(i + 1) * 128, :],
                                                    arena[:, 32768 + i * 256: 32768 + (i + 1) * 256])),
              reads=[("y1a", 7, i)], dma_sem="dbg")
    if "h" in DEBUG and STAGES["l1a"]:
        for sub in range(4):
            E("gpsimd", (lambda e, sub=sub: e.dma_start(dt["dbg_h"][sub * 128:(sub + 1) * 128, :],
                                                        h_sb[:, sub * 256:(sub + 1) * 256])),
              reads=[("h", sub)], dma_sem="dbg")
    if "x1" in DEBUG:
        for i in range(NCH):
            E("sync", (lambda e, i=i: e.dma_start(dt["dbg_x1"][i * 128:(i + 1) * 128, :],
                                                  x1[:, i * T:(i + 1) * T])),
              reads=[("x1", i)], dma_sem="dout")

    if not STAGES["l1f"]:
        for bb in range(B):
            for i in range(NCH):
                E("sync", (lambda e, bb=bb, i=i: e.dma_start(
                    dt["out_fm"][bb, i * 128:(i + 1) * 128, :],
                    x1[:, i * T: i * T + TW])),
                  reads=[("x1", i)], dma_sem="dout")

    P.resolve()

    for name, cntv in P.final_counts.items():
        if name.startswith("D:dout") or name.startswith("D:dbg"):
            P.final_waits.append(("gpsimd", name, cntv))

    semnames = sorted(set(P.semnames) | {n for _, n, _ in P.final_waits})
    sems = {}
    for n in semnames:
        sems[n] = es.enter_context(nc.semaphore(n.replace(":", "_")))
    with nc.Block() as block:
        bodies = P.make_bodies(sems)
        block.sync(bodies["sync"])
        block.gpsimd(bodies["gpsimd"])
        block.tensor(bodies["tensor"])
        block.scalar(bodies["scalar"])
        block.vector(bodies["vector"])
    es.close()
    build_program.last_P = P
    return nc, len(P.recs)


def prep_inputs(inputs):
    f = np.float32
    g = {k: np.asarray(v) for k, v in inputs.items()}

    def col(v):
        return np.ascontiguousarray(v.reshape(-1, 128).T).astype(f)

    com = {}
    com["wq"] = np.ascontiguousarray(g["L0_attn_Wq"])
    com["wk"] = np.ascontiguousarray(g["L0_attn_Wk"])
    com["wv"] = np.ascontiguousarray(g["L0_attn_Wv"])
    com["bq_col"] = col(g["L0_attn_bq"]); com["bk_col"] = col(g["L0_attn_bk"])
    com["bv_row"] = g["L0_attn_bv"].reshape(1, -1).astype(f)
    com["w0r"] = np.ascontiguousarray(g["L0_ffn_W_r"].T)
    com["w0k"] = np.ascontiguousarray(g["L0_ffn_W_k"].T)
    com["w0v"] = np.ascontiguousarray(g["L0_ffn_W_v"].T)
    com["mur_col"] = col(g["L0_ffn_mu_r"]); com["muk_col"] = col(g["L0_ffn_mu_k"])
    com["bmT"] = np.ascontiguousarray(g["L1_attn_Bm"].T)
    com["dT"] = np.ascontiguousarray(g["L1_attn_D"].T)
    com["aT"] = np.ascontiguousarray(g["L1_attn_A"].T)
    com["cT"] = np.ascontiguousarray(g["L1_attn_C"].T)
    com["lnw_col"] = col(g["L1_attn_ln_w"]); com["lnb_col"] = col(g["L1_attn_ln_b"])
    com["w1T"] = np.ascontiguousarray(g["L1_attn_W1"].T)
    com["b1_col"] = col(g["L1_attn_b1"])
    com["w2T"] = np.ascontiguousarray(g["L1_attn_W2"].T)
    com["b2_col"] = col(g["L1_attn_b2"])
    com["ipw"] = np.ascontiguousarray(g["L1_ffn_in_proj_w"].T)
    com["ipb_row"] = g["L1_ffn_in_proj_b"][2048:].reshape(1, -1).astype(f)
    com["ipb_col"] = col(g["L1_ffn_in_proj_b"])
    com["opw"] = np.ascontiguousarray(g["L1_ffn_out_proj_w"].T)
    com["opb_col"] = col(g["L1_ffn_out_proj_b"])
    com["ln1g_col"] = col(g["L1_ffn_ln1_g"]); com["ln1b_col"] = col(g["L1_ffn_ln1_b"])
    com["ln2g_col"] = col(g["L1_ffn_ln2_g"]); com["ln2b_col"] = col(g["L1_ffn_ln2_b"])
    com["wg"] = np.ascontiguousarray(g["L1_ffn_wg"])
    jj = np.arange(128)
    uu_ = jj // 8
    com["mha_mask"] = np.where(uu_[:, None] == uu_[None, :], 0.0, -480.0).astype(f)
    com["tri_all"] = (jj[:, None] < jj[None, :]).astype(f)

    com["ident"] = np.eye(128, dtype=f)
    com["identf"] = np.eye(128, dtype=f)
    com["iota8r"] = np.broadcast_to(np.arange(8, dtype=f), (128, 8)).copy()

    x = np.asarray(g["L0_attn_hidden_states"])
    in_maps = []
    for c in range(NCORES):
        m = dict(com)
        m["xT"] = np.ascontiguousarray(x[c].T).astype(f)
        m["myw1T"] = np.ascontiguousarray(np.asarray(g["L1_ffn_w1"])[c].T).astype(f)
        m["myw2"] = np.ascontiguousarray(np.asarray(g["L1_ffn_w2"])[c]).astype(f)
        m["myb1c"] = np.ascontiguousarray(
            np.asarray(g["L1_ffn_b1"])[c].reshape(-1, 128).T).astype(f)
        m["myb2r"] = np.asarray(g["L1_ffn_b2"])[c].reshape(1, -1).astype(f)
        m["selcol"] = np.broadcast_to((np.arange(NCORES) == c).astype(f), (128, NCORES)).copy()
        m64 = np.zeros((64, NEXP), f)
        for cp in range(NCORES):
            for bp in range(NEXP):
                for b_ in range(NEXP):
                    m64[cp * 8 + bp, b_] = (float(bp < b_ and cp != c)
                                            + float(cp < c and bp == b_))
        m["m64"] = m64
        in_maps.append(m)
    return in_maps


_CACHED = {}


def kernel(**inputs):
    key = "prog"
    if key not in _CACHED:
        _CACHED[key] = build_program()
    nc, _ = _CACHED[key]
    in_maps = prep_inputs(inputs)
    res = run_bass_kernel_spmd(nc, in_maps, core_ids=list(range(NCORES)))
    out = np.concatenate([r["out_fm"].transpose(0, 2, 1) for r in res.results], axis=1)
    return out.astype(np.float32)


# version 10
# speedup vs baseline: 1.0077x; 1.0029x over previous
import contextlib
import numpy as np
import ml_dtypes
import concourse.bass as bass
import concourse.mybir as mybir
from concourse.bass_utils import run_bass_kernel_spmd

F32 = mybir.dt.float32
BF16 = mybir.dt.bfloat16
AF = mybir.ActivationFunctionType
ALU = mybir.AluOpType

NCORES = 8
B, T, D = 8, 2048, 1024
NCH = D // 128
TW = T // NCORES
S = 256
FF1 = 4096
NEXP, CAPG = 8, 2048
T_WIN2 = 2048
PAD = 384
MFF = 4096

STAGES = dict(l0=True, l1a=True, l1f=True, out=True)
DEBUG = ()


class Rec:
    __slots__ = ("idx", "eng", "fn", "reads", "writes", "kind", "dma_sem",
                 "flag", "waits", "drain", "seq", "_deps")

    def __init__(self, idx, eng, fn, reads, writes, kind, dma_sem):
        self.idx, self.eng, self.fn = idx, eng, fn
        self.reads, self.writes = tuple(reads), tuple(writes)
        self.kind, self.dma_sem = kind, dma_sem
        self.flag = False
        self.waits = []
        self.drain = False
        self.seq = 0
        self._deps = []


class Prog:
    ENGINES = ("tensor", "scalar", "vector", "sync", "gpsimd")

    def __init__(self):
        self.recs = []
        self.last_w = {}
        self.readers = {}
        self.final_waits = []

    def emit(self, eng, fn, reads=(), writes=(), mm=False, dma_sem=None, cc=False,
             conflicts=()):
        kind = 'cc' if cc else ('d' if dma_sem else ('mm' if mm else 'c'))
        r = Rec(len(self.recs), eng, fn, reads, writes, kind, dma_sem)
        self.recs.append(r)
        deps, seen = [], set()

        def add(d):
            if d is not None and d is not r and d.idx not in seen:
                seen.add(d.idx)
                deps.append(d)
        for k in r.reads:
            add(self.last_w.get(k))
        for k in tuple(writes) + tuple(conflicts):
            add(self.last_w.get(k))
            for d in self.readers.get(k, ()):
                add(d)
        r._deps = deps
        for k in r.reads:
            self.readers.setdefault(k, []).append(r)
        for k in r.writes:
            self.last_w[k] = r
            self.readers[k] = []
        return r

    def resolve(self):
        for r in self.recs:
            for d in r._deps:
                if d.kind in ('d', 'cc'):
                    d.flag = True
                elif d.eng == r.eng and r.kind not in ('d', 'cc'):
                    if d.kind == 'mm' and r.kind == 'mm':
                        continue
                    r.drain = True
                else:
                    d.flag = True
        cnt = {}
        for r in self.recs:
            if r.kind == 'd':
                r.flag = True
                key = 'D:' + r.dma_sem
                cnt[key] = cnt.get(key, 0) + 16
                r.seq = cnt[key]
            elif r.kind == 'cc':
                r.flag = True
                key = 'D:' + r.dma_sem
                cnt[key] = cnt.get(key, 0) + 1
                r.seq = cnt[key]
            elif r.flag:
                key = 'E:' + r.eng
                cnt[key] = cnt.get(key, 0) + 1
                r.seq = cnt[key]
        import bisect
        semlist = {}
        for r in self.recs:
            if r.kind in ('d', 'cc'):
                semlist.setdefault('D:' + r.dma_sem, []).append((r.idx, r.seq))
        marks = {}
        for r in self.recs:
            for d in r._deps:
                if d.kind in ('d', 'cc'):
                    sem = 'D:' + d.dma_sem
                    lst = semlist[sem]
                    pos = bisect.bisect_left(lst, (r.idx, -1)) - 1
                    val = lst[pos][1]
                    assert val >= d.seq
                elif d.eng == r.eng:
                    continue
                else:
                    sem = 'E:' + d.eng
                    val = d.seq
                wm = marks.get((r.eng, sem), 0)
                if val > wm:
                    marks[(r.eng, sem)] = val
                    r.waits.append((sem, val))
        self.semnames = sorted(cnt.keys())
        self.final_counts = dict(cnt)

    def make_bodies(self, sems):
        per = {e: [] for e in self.ENGINES}
        for r in self.recs:
            per[r.eng].append(r)

        def body(eng_name):
            def go(e):
                for r in per[eng_name]:
                    if r.drain:
                        e.drain()
                    for sem, v in r.waits:
                        e.wait_ge(sems[sem], v)
                    instr = r.fn(e)
                    if r.flag:
                        if r.kind == 'd':
                            instr.then_inc(sems['D:' + r.dma_sem], 16)
                        elif r.kind == 'cc':
                            instr.then_inc(sems['D:' + r.dma_sem], 1)
                        else:
                            instr.then_inc(sems['E:' + r.eng], 1)
                for eng2, sem, v in self.final_waits:
                    if eng2 == eng_name:
                        e.wait_ge(sems[sem], v)
            return go
        return {e: body(e) for e in self.ENGINES}


def build_program():
    nc = bass.Bass("TRN2", target_bir_lowering=False)
    es = contextlib.ExitStack()
    P = Prog()
    dt = {}

    def din(name, shape, dtype=F32):
        dt[name] = nc.dram_tensor(name, list(shape), dtype, kind="ExternalInput").ap()

    def dout(name, shape, dtype=F32):
        dt[name] = nc.dram_tensor(name, list(shape), dtype, kind="ExternalOutput").ap()

    def dint(name, shape, dtype=F32):
        dt[name] = nc.dram_tensor(name, list(shape), dtype, kind="Internal").ap()

    def sb(name, shape, dtype):
        return es.enter_context(nc.sbuf_tensor(name, list(shape), dtype))

    def psalloc(name, shape=(128, 512)):
        return es.enter_context(nc.psum_tensor(name, list(shape), F32))

    din("xT", (D, T))
    din("wq", (D, D)); din("wk", (D, D)); din("wv", (D, D))
    din("bq_col", (128, NCH)); din("bk_col", (128, NCH)); din("bv_row", (1, D))
    din("w0r", (D, D)); din("w0k", (D, D)); din("w0v", (D, D))
    din("mur_col", (128, NCH)); din("muk_col", (128, NCH))
    din("bmT", (D, S)); din("dT", (D, D)); din("aT", (S, S)); din("cT", (S, D))
    din("lnw_col", (128, NCH)); din("lnb_col", (128, NCH))
    din("w1T", (D, FF1)); din("b1_col", (128, FF1 // 128))
    din("w2T", (FF1, D)); din("b2_col", (128, NCH))
    din("ipw", (D, 3 * D)); din("ipb_row", (1, D)); din("ipb_col", (128, 3 * NCH))
    din("opw", (D, D)); din("opb_col", (128, NCH))
    din("ln1g_col", (128, NCH)); din("ln1b_col", (128, NCH))
    din("ln2g_col", (128, NCH)); din("ln2b_col", (128, NCH))
    din("wg", (D, NEXP))
    din("myw1T", (MFF, D)); din("myw2", (MFF, D))
    din("myb1c", (128, 32)); din("myb2r", (1, D))
    din("mha_mask", (128, 128))
    din("tri_all", (128, 128))
    din("selcol", (128, NCORES)); din("m64", (64, NEXP))
    din("iota8r", (128, NEXP))
    din("ident", (128, 128)); din("identf", (128, 128))
    dout("out_fm", (B, D, TW))
    for d in DEBUG:
        if d == "y1":
            dout("dbg_y1", (D, T))
        elif d == "states":
            dout("dbg_states", (256, T))
        elif d in ("ypre", "y1a"):
            dout("dbg_" + d, (D, 256))
        elif d == "h":
            dout("dbg_h", (512, 256))
        else:
            dout("dbg_" + d, (D, T))

    dint("ag_xn1_in", (D, T), BF16)
    dint("ag_xn1_out", (NCORES, D, T), BF16)
    dint("ag_s_in", (D, T), F32)
    dint("ag_s_out", (NCORES, D, T), F32)
    dint("ag_cnt_in", (B, NEXP), F32)
    dint("ag_cnt_out", (NCORES, B, NEXP), F32)
    dint("hb_w1b", (D, FF1), BF16)
    dint("hb_opb", (D, D), BF16)
    dint("hb_ipb", (D, 3 * D), BF16)
    dint("hb_w2b", (FF1, D), BF16)
    dint("hb_y1", (D, T), BF16)
    dint("hb_st", (D, T_WIN2), BF16)
    dint("ag_wf_in", (1025, D), BF16)
    dint("ag_wf_out", (NCORES, 1025, D), BF16)

    x1 = sb("x1", (128, NCH * T), F32)
    xn = sb("xn", (128, NCH * T), BF16)
    onesr_f = sb("onesr_f", (128, 128), F32)
    onesr_b = sb("onesr_b", (1, 128), BF16)
    onesc_f = sb("onesc_f", (128, 1), F32)
    onesc_b = sb("onesc_b", (128, 1), BF16)
    zrow_b = sb("zrow_b", (1, 512), BF16)
    eps_col = sb("eps_col", (128, 1), F32)
    pb = [psalloc(f"pb{i}") for i in range(8)]
    PSK = [("ps", i) for i in range(8)]

    smalls = {}
    for nm, shp in [("bq_col", (128, NCH)), ("bk_col", (128, NCH)),
                    ("mur_col", (128, NCH)), ("muk_col", (128, NCH)),
                    ("lnw_col", (128, NCH)), ("lnb_col", (128, NCH)),
                    ("b1_col", (128, FF1 // 128)), ("b2_col", (128, NCH)),
                    ("opb_col", (128, NCH)),
                    ("ln1g_col", (128, NCH)), ("ln1b_col", (128, NCH)),
                    ("ln2g_col", (128, NCH)), ("ln2b_col", (128, NCH)),
                    ("identf", (128, 128))]:
        smalls[nm] = sb("s_" + nm, shp, F32)
    smallsb = {}
    for nm, shp in [("bv_row", (1, D)), ("ipb_row", (1, D)),
                    ("mha_mask", (128, 128)),
                    ("tri_all", (128, 128)),
                    ("ident", (128, 128))]:
        smallsb[nm] = sb("b_" + nm, shp, BF16)

    sqt = [sb("sqt0", (128, 512), F32)] * 2
    stat_m = sb("stat_m", (128, 512), F32)
    stat_r = sb("stat_r", (128, 512), F32)
    vsc = [sb(f"vsc{i}", (128, 512), F32) for i in range(2)]
    stat_t = vsc[0]

    E = P.emit
    K_SM = lambda nm: ("small", nm)

    for i in range(NCH):
        E("sync", (lambda e, i=i: e.dma_start(x1[:, i * T:(i + 1) * T],
                                              dt["xT"][i * 128:(i + 1) * 128, :])),
          writes=[("x1", i)], dma_sem="dx")
    for nm in smalls:
        E("sync", (lambda e, nm=nm: e.dma_start(smalls[nm][:, :], dt[nm][:, :])),
          writes=[K_SM(nm)], dma_sem="dsm")
    for nm in smallsb:
        E("gpsimd", (lambda e, nm=nm: e.dma_start(smallsb[nm][:, :], dt[nm][:, :])),
          writes=[K_SM(nm)], dma_sem="dsb")
    for buf, val in [(onesr_f, 1.0), (onesr_b, 1.0), (onesc_f, 1.0), (onesc_b, 1.0),
                     (zrow_b, 0.0), (eps_col, 1e-5)]:
        E("gpsimd", (lambda e, buf=buf, val=val: e.memset(buf[:, :], val)),
          writes=[("c", id(buf))])

    def KC(*bufs):
        return [("c", id(b)) for b in bufs]

    def emit_ln(src, src_key, dst, dst_key, tokens, gamma_nm=None, beta_nm=None, onescol=None, dst_extra=(), dst_conflicts=()):
        onescol = onescol if onescol is not None else onesc_f
        qw = min(512, tokens)
        nqc = tokens // qw
        for bank in (6, 7):
            E("tensor", (lambda e, bank=bank: e.matmul(pb[bank][:, 0:512], zrow_b[0:1, 0:128],
                                                       zrow_b[0:1, 0:512], start=True, stop=True)),
              reads=KC(zrow_b), writes=[PSK[bank]], mm=True)
        for qc in range(nqc):
            for i in range(NCH):
                sq = sqt[(qc * NCH + i) % 2]
                sl = (i * tokens + qc * qw, i * tokens + (qc + 1) * qw)
                E("scalar", (lambda e, sq=sq, sl=sl, qw=qw:
                             e.activation(sq[:, 0:qw], src[:, sl[0]:sl[1]], AF.Square)),
                  reads=[src_key(i, qc)], writes=[("sq", id(sq))])
                E("tensor", (lambda e, i=i, qc=qc, sl=sl, qw=qw:
                             e.matmul(pb[6][32 * qc:32 * qc + 1, 0:qw], onescol[:, 0:1],
                                      src[:, sl[0]:sl[1]], start=False, stop=(i == NCH - 1),
                                      skip_group_check=True, tile_position=(0, 32 * qc))),
                  reads=[src_key(i, qc)] + KC(onescol), writes=[PSK[6]], mm=True)
                E("tensor", (lambda e, i=i, qc=qc, sq=sq, qw=qw:
                             e.matmul(pb[7][32 * qc:32 * qc + 1, 0:qw], onesc_f[:, 0:1],
                                      sq[:, 0:qw], start=False, stop=(i == NCH - 1),
                                      skip_group_check=True, tile_position=(0, 32 * qc))),
                  reads=[("sq", id(sq))] + KC(onesc_f), writes=[PSK[7]], mm=True)
                yield
        E("vector", (lambda e: e.tensor_scalar(stat_m[:, :], pb[6][:, :], 1.0 / D, None, ALU.mult)),
          reads=[PSK[6]], writes=[("ln", "m")])
        E("vector", (lambda e: e.tensor_mul(stat_t[:, :], stat_m[:, :], stat_m[:, :])),
          reads=[("ln", "m")], writes=[("vsc", id(vsc[0]))])
        E("vector", (lambda e: e.scalar_tensor_tensor(stat_r[:, :], pb[7][:, :], 1.0 / D,
                                                      stat_t[:, :], ALU.mult, ALU.subtract)),
          reads=[PSK[7], ("vsc", id(vsc[0]))], writes=[("ln", "v")])
        E("scalar", (lambda e: e.activation(stat_t[:, :], stat_r[:, :], AF.Ln, bias=eps_col[:, 0:1])),
          reads=[("ln", "v")] + KC(eps_col), writes=[("vsc", id(vsc[0]))])
        E("scalar", (lambda e: e.activation(stat_r[:, :], stat_t[:, :], AF.Exp, scale=-0.5)),
          reads=[("vsc", id(vsc[0]))], writes=[("ln", "r")])
        yield
        for qc in range(nqc):
            E("tensor", (lambda e, qc=qc, qw=qw: e.matmul(pb[4][:, 0:qw], onesr_f[32 * qc:32 * qc + 1, :],
                                                   stat_m[32 * qc:32 * qc + 1, 0:qw],
                                                   start=True, stop=True,
                                                   tile_position=(32 * qc, 0))),
              reads=[("ln", "m")] + KC(onesr_f), writes=[PSK[4]], mm=True)
            E("tensor", (lambda e, qc=qc, qw=qw: e.matmul(pb[5][:, 0:qw], onesr_f[32 * qc:32 * qc + 1, :],
                                                   stat_r[32 * qc:32 * qc + 1, 0:qw],
                                                   start=True, stop=True,
                                                   tile_position=(32 * qc, 0))),
              reads=[("ln", "r")] + KC(onesr_f), writes=[PSK[5]], mm=True)
            for i in range(NCH):
                v = vsc[i % 2]
                sl = (i * tokens + qc * qw, i * tokens + (qc + 1) * qw)
                E("vector", (lambda e, v=v, sl=sl, qw=qw:
                             e.tensor_sub(v[:, 0:qw], src[:, sl[0]:sl[1]], pb[4][:, 0:qw])),
                  reads=[src_key(i, qc), PSK[4]], writes=[("vsc", id(v))])
                if gamma_nm is None:
                    E("vector", (lambda e, v=v, sl=sl, qw=qw:
                                 e.tensor_mul(dst[:, sl[0]:sl[1]], v[:, 0:qw], pb[5][:, 0:qw])),
                      reads=[("vsc", id(v)), PSK[5]] + list(dst_extra), writes=[dst_key(i, qc)],
                      conflicts=list(dst_conflicts))
                    yield
                else:
                    g, bcol = smalls[gamma_nm], smalls[beta_nm]
                    E("vector", (lambda e, v=v, qw=qw:
                                 e.tensor_mul(v[:, 0:qw], v[:, 0:qw], pb[5][:, 0:qw])),
                      reads=[("vsc", id(v)), PSK[5]], writes=[("vsc", id(v))])
                    E("vector", (lambda e, v=v, sl=sl, g=g, bcol=bcol, i=i, qw=qw:
                                 e.tensor_scalar(dst[:, sl[0]:sl[1]], v[:, 0:qw],
                                                 g[:, i:i + 1], bcol[:, i:i + 1],
                                                 ALU.mult, ALU.add)),
                      reads=[("vsc", id(v)), K_SM(gamma_nm), K_SM(beta_nm)] + list(dst_extra),
                      writes=[dst_key(i, qc)], conflicts=list(dst_conflicts))
                    yield

    def DRAIN(gen):
        for _ in gen:
            pass

    arena = sb("arena", (128, 36864), BF16)
    tenants = []

    def claim(c0, c1, keys):
        old = []
        keep = []
        for ent in tenants:
            if ent[0] < c1 and c0 < ent[1]:
                old.extend(ent[2])
                if ent[0] < c0:
                    keep.append([ent[0], c0, ent[2]])
                if ent[1] > c1:
                    keep.append([c1, ent[1], ent[2]])
            else:
                keep.append(ent)
        tenants[:] = keep + [[c0, c1, list(keys)]]
        return old

    WA, WB, WC = 0, 8192, 16384
    XRC, RC, KC2 = 24576, 28672, 32768
    VTC = 16384
    QHC, KHC = 32768, 34816

    def load_wslot(nm, base, region, keyname):
        keys = [(keyname, i) for i in range(NCH)]
        old = claim(base, base + NCH * 1024, keys)
        for i in range(NCH):
            E("gpsimd", (lambda e, nm=nm, base=base, i=i:
                         e.dma_start(arena[:, base + i * 1024: base + (i + 1) * 1024],
                                     dt[nm][i * 128:(i + 1) * 128, :])),
              writes=[(keyname, i)], conflicts=old, dma_sem="dw")

    pt = [sb(f"pt{i}", (128, 512), BF16) for i in range(2)]
    attq = sb("attq", (128, 2048), BF16)
    y1tb = sb("y1tb", (128, 2 * 1024), BF16)
    attp = y1tb
    recip_sb = sb("recip_sb", (128, 128), F32)
    recb = sb("recb", (128, 160), F32)
    ctmp = sb("ctmp", (128, 512), F32)

    if STAGES["l0"]:
        DRAIN(emit_ln(x1, lambda i, qc: ("x1", i), xn, lambda i, qc: ("xn", i, qc), T))

        load_wslot("w0r", WA, "WA", "w0r")
        load_wslot("w0k", WB, "WB", "w0k")
        load_wslot("w0v", WC, "WC", "w0v")

        def emit_mix(qc, mu_nm, tag):
            mu = smalls[mu_nm]
            keys = [(tag, qc, i) for i in range(NCH)]
            old = claim(XRC, XRC + 4096, keys)
            for i in range(NCH):
                lo = i * T + qc * 512
                d0 = XRC + i * 512
                if qc == 0:
                    E("vector", (lambda e, lo=lo:
                                 e.tensor_sub(vsc[0][:, 1:512], xn[:, lo + 1:lo + 512],
                                              xn[:, lo:lo + 511])),
                      reads=[("xn", i, 0)], writes=[("vsc", 0)])
                    E("vector", (lambda e, mu=mu, i=i:
                                 e.tensor_scalar(vsc[1][:, 1:512], vsc[0][:, 1:512],
                                                 mu[:, i:i + 1], None, ALU.mult)),
                      reads=[("vsc", 0), K_SM(mu_nm)], writes=[("vsc", 1)])
                    E("vector", (lambda e, lo=lo, d0=d0:
                                 e.tensor_add(arena[:, d0 + 1:d0 + 512], vsc[1][:, 1:512],
                                              xn[:, lo:lo + 511])),
                      reads=[("vsc", 1), ("xn", i, 0)], writes=[(tag, qc, i)], conflicts=old)
                    E("vector", (lambda e, lo=lo, d0=d0, mu=mu, i=i:
                                 e.tensor_scalar(arena[:, d0:d0 + 1], xn[:, lo:lo + 1],
                                                 mu[:, i:i + 1], None, ALU.mult)),
                      reads=[("xn", i, 0), K_SM(mu_nm)], writes=[(tag, qc, i)])
                else:
                    E("vector", (lambda e, lo=lo, i=i, qc=qc:
                                 e.tensor_sub(vsc[0][:, 0:512], xn[:, lo:lo + 512],
                                              xn[:, lo - 1:lo + 511])),
                      reads=[("xn", i, qc), ("xn", i, qc - 1)], writes=[("vsc", 0)])
                    E("vector", (lambda e, mu=mu, i=i:
                                 e.tensor_scalar(vsc[1][:, 0:512], vsc[0][:, 0:512],
                                                 mu[:, i:i + 1], None, ALU.mult)),
                      reads=[("vsc", 0), K_SM(mu_nm)], writes=[("vsc", 1)])
                    E("vector", (lambda e, lo=lo, d0=d0, i=i, qc=qc:
                                 e.tensor_add(arena[:, d0:d0 + 512], vsc[1][:, 0:512],
                                              xn[:, lo - 1:lo + 511])),
                      reads=[("vsc", 1), ("xn", i, qc - 1), ("xn", i, qc)],
                      writes=[(tag, qc, i)], conflicts=old)

        for qc in range(4):
            emit_mix(qc, "mur_col", "xr")
            rkeys = [("rblk", qc, oc) for oc in range(NCH)]
            rold = claim(RC, RC + 4096, rkeys)
            for oc in range(NCH):
                bank = 2 + oc % 2
                for i in range(NCH):
                    E("tensor", (lambda e, oc=oc, i=i, bank=bank, qc=qc:
                                 e.matmul(pb[bank][:, 0:512],
                                          arena[:, WA + i * 1024 + oc * 128: WA + i * 1024 + (oc + 1) * 128],
                                          arena[:, XRC + i * 512: XRC + (i + 1) * 512],
                                          start=(i == 0), stop=(i == NCH - 1),
                                          skip_group_check=True)),
                      reads=[("xr", qc, i), ("w0r", i)], writes=[PSK[bank]], mm=True)
                E("scalar", (lambda e, oc=oc, bank=bank:
                             e.activation(arena[:, RC + oc * 512: RC + (oc + 1) * 512],
                                          pb[bank][:, 0:512], AF.Sigmoid)),
                  reads=[PSK[bank]], writes=[("rblk", qc, oc)], conflicts=rold)
            emit_mix(qc, "muk_col", "xk")
            kkeys = [("kablk", qc, oc) for oc in range(NCH)]
            kold = claim(KC2, KC2 + 4096, kkeys)
            for oc in range(NCH):
                bank = 2 + oc % 2
                for i in range(NCH):
                    E("tensor", (lambda e, oc=oc, i=i, bank=bank, qc=qc:
                                 e.matmul(pb[bank][:, 0:512],
                                          arena[:, WB + i * 1024 + oc * 128: WB + i * 1024 + (oc + 1) * 128],
                                          arena[:, XRC + i * 512: XRC + (i + 1) * 512],
                                          start=(i == 0), stop=(i == NCH - 1),
                                          skip_group_check=True)),
                      reads=[("xk", qc, i), ("w0k", i)], writes=[PSK[bank]], mm=True)
                E("scalar", (lambda e, oc=oc, bank=bank:
                             e.activation(arena[:, KC2 + oc * 512: KC2 + (oc + 1) * 512],
                                          pb[bank][:, 0:512], AF.Relu)),
                  reads=[PSK[bank]], writes=[("kablk", qc, oc)], conflicts=kold)
                E("vector", (lambda e, oc=oc:
                             e.tensor_mul(arena[:, KC2 + oc * 512: KC2 + (oc + 1) * 512],
                                          arena[:, KC2 + oc * 512: KC2 + (oc + 1) * 512],
                                          arena[:, KC2 + oc * 512: KC2 + (oc + 1) * 512])),
                  reads=[("kablk", qc, oc)], writes=[("kablk", qc, oc)])
            for oc in range(NCH):
                bank = 2 + oc % 2
                for i in range(NCH):
                    E("tensor", (lambda e, oc=oc, i=i, bank=bank, qc=qc:
                                 e.matmul(pb[bank][:, 0:512],
                                          arena[:, WC + i * 1024 + oc * 128: WC + i * 1024 + (oc + 1) * 128],
                                          arena[:, KC2 + i * 512: KC2 + (i + 1) * 512],
                                          start=(i == 0), stop=(i == NCH - 1),
                                          skip_group_check=True)),
                      reads=[("kablk", qc, i), ("w0v", i)], writes=[PSK[bank]], mm=True)
                E("vector", (lambda e, oc=oc, bank=bank:
                             e.tensor_mul(ctmp[:, :], pb[bank][:, 0:512],
                                          arena[:, RC + oc * 512: RC + (oc + 1) * 512])),
                  reads=[PSK[bank], ("rblk", qc, oc)], writes=[("ctmp",)])
                E("vector", (lambda e, oc=oc, qc=qc:
                             e.tensor_add(x1[:, oc * T + qc * 512: oc * T + (qc + 1) * 512],
                                          x1[:, oc * T + qc * 512: oc * T + (qc + 1) * 512],
                                          ctmp[:, :])),
                  reads=[("ctmp",), ("x1", oc)], writes=[("x1", oc)])

        WTMP, KHF, VHF, WQR = 0, 8192, 16384, 24576
        load_wslot("wq", WQR, "WQR", "wq")
        den = sqt[0]
        for mhalf in range(2):
            mb0 = mhalf * 1024
            wvold = claim(WTMP, WTMP + 8192, [("wv", mhalf, i) for i in range(NCH)])
            for i in range(NCH):
                E("gpsimd", (lambda e, i=i: e.dma_start(
                    arena[:, WTMP + i * 1024: WTMP + (i + 1) * 1024],
                    dt["wv"][i * 128:(i + 1) * 128, :])),
                  writes=[("wv", mhalf, i)], conflicts=wvold, dma_sem="dw")
            vkeys = [("vtok", mhalf, tl, half) for tl in range(8) for half in range(2)]
            vold = claim(VHF, VHF + 8192, vkeys)
            for tl in range(8):
                for half in range(2):
                    bank = (tl * 2 + half) % 2
                    for i in range(NCH):
                        E("tensor", (lambda e, tl=tl, half=half, i=i, bank=bank, mb0=mb0:
                                     e.matmul(pb[bank][:, 0:512],
                                              xn[:, i * T + mb0 + tl * 128:
                                                 i * T + mb0 + (tl + 1) * 128],
                                              arena[:, WTMP + i * 1024 + half * 512:
                                                    WTMP + i * 1024 + (half + 1) * 512],
                                              start=(i == 0), stop=False,
                                              skip_group_check=True)),
                          reads=[("xn", i, (mb0 + tl * 128) // 512), ("wv", mhalf, i)],
                          writes=[PSK[bank]], mm=True)
                    E("tensor", (lambda e, half=half, bank=bank:
                                 e.matmul(pb[bank][:, 0:512], onesr_b[0:1, :],
                                          smallsb["bv_row"][0:1, half * 512:(half + 1) * 512],
                                          start=False, stop=True, skip_group_check=True)),
                      reads=[K_SM("bv_row")] + KC(onesr_b), writes=[PSK[bank]], mm=True)
                    E("scalar", (lambda e, tl=tl, half=half, bank=bank:
                                 e.copy(arena[:, VHF + tl * 1024 + half * 512:
                                              VHF + tl * 1024 + (half + 1) * 512],
                                        pb[bank][:, 0:512])),
                      reads=[PSK[bank]], writes=[("vtok", mhalf, tl, half)],
                      conflicts=vold)
            wkold = claim(WTMP, WTMP + 8192, [("wk", mhalf, i) for i in range(NCH)])
            for i in range(NCH):
                E("gpsimd", (lambda e, i=i: e.dma_start(
                    arena[:, WTMP + i * 1024: WTMP + (i + 1) * 1024],
                    dt["wk"][i * 128:(i + 1) * 128, :])),
                  writes=[("wk", mhalf, i)], conflicts=wkold, dma_sem="dw")
            kkeys = [("khf", mhalf, g, h2) for g in range(8) for h2 in range(2)]
            kold = claim(KHF, KHF + 8192, kkeys)
            for g in range(8):
                for h2 in range(2):
                    bank = (g * 2 + h2) % 2
                    for i in range(NCH):
                        E("tensor", (lambda e, g=g, h2=h2, i=i, bank=bank, mb0=mb0:
                                     e.matmul(pb[bank][:, 0:512],
                                              arena[:, WTMP + i * 1024 + g * 128:
                                                    WTMP + i * 1024 + (g + 1) * 128],
                                              xn[:, i * T + mb0 + h2 * 512:
                                                 i * T + mb0 + (h2 + 1) * 512],
                                              start=(i == 0), stop=(i == NCH - 1),
                                              skip_group_check=True)),
                          reads=[("xn", i, (mb0 + h2 * 512) // 512), ("wk", mhalf, i)],
                          writes=[PSK[bank]], mm=True)
                    E("scalar", (lambda e, g=g, h2=h2, bank=bank:
                                 e.activation(arena[:, KHF + g * 1024 + h2 * 512:
                                                    KHF + g * 1024 + (h2 + 1) * 512],
                                              pb[bank][:, 0:512], AF.Identity,
                                              bias=smalls["bk_col"][:, g:g + 1])),
                      reads=[PSK[bank], K_SM("bk_col")], writes=[("khf", mhalf, g, h2)],
                      conflicts=kold)
            ap2old = claim(WTMP, WTMP + 2048, [("attp2", p_) for p_ in range(4)])
            attp2 = arena[:, WTMP:WTMP + 2048]

            def emit_ctx(tl, mhalf, buf, bkey):
                for g in range(8):
                    E("tensor", (lambda e, g=g, tl=tl, mhalf=mhalf, buf=buf:
                                 e.matmul(pb[4 + g // 2][:, (g % 2) * 256:
                                                        (g % 2) * 256 + 256],
                                          arena[:, VHF + tl * 1024 + g * 128:
                                                VHF + tl * 1024 + (g + 1) * 128],
                                          buf[:, g * 256:(g + 1) * 256],
                                          start=False, stop=(tl == 7),
                                          skip_group_check=True)),
                      reads=[(bkey, g // 2), ("vtok", mhalf, tl, g // 4)],
                      writes=[PSK[4 + g // 2]], mm=True)

            for qc in range(8):
                for g in range(8):
                    bank = g % 2
                    for i in range(NCH):
                        E("tensor", (lambda e, g=g, i=i, bank=bank, qc=qc:
                                     e.matmul(pb[bank][:, 0:256],
                                              arena[:, WQR + i * 1024 + g * 128:
                                                    WQR + i * 1024 + (g + 1) * 128],
                                              xn[:, i * T + qc * 256: i * T + (qc + 1) * 256],
                                              start=(i == 0), stop=(i == NCH - 1),
                                              skip_group_check=True)),
                          reads=[("xn", i, qc // 2), ("wq", i)], writes=[PSK[bank]],
                          mm=True)
                    E("scalar", (lambda e, g=g, bank=bank:
                                 e.activation(attq[:, g * 256:(g + 1) * 256],
                                              pb[bank][:, 0:256], AF.Identity,
                                              bias=smalls["bq_col"][:, g:g + 1])),
                      reads=[PSK[bank], K_SM("bq_col")], writes=[("attq", g)],
                      conflicts=[("attq", g)])
                for bank in (4, 5, 6, 7):
                    E("tensor", (lambda e, bank=bank: e.matmul(
                        pb[bank][:, 0:512], zrow_b[0:1, 0:128], zrow_b[0:1, 0:512],
                        start=True, stop=True)),
                      reads=KC(zrow_b), writes=[PSK[bank]], mm=True)
                for tl in range(8):
                    buf = attp if tl % 2 == 0 else attp2
                    bkey = "attp" if tl % 2 == 0 else "attp2"
                    for pair in range(4):
                        bank = 2 + pair % 2
                        for sub in range(2):
                            g = pair * 2 + sub
                            E("tensor", (lambda e, g=g, sub=sub, bank=bank, tl=tl,
                                         mhalf=mhalf:
                                         e.matmul(pb[bank][:, sub * 256:(sub + 1) * 256],
                                                  arena[:, KHF + g * 1024 + tl * 128:
                                                        KHF + g * 1024 + (tl + 1) * 128],
                                                  attq[:, g * 256:(g + 1) * 256],
                                                  start=True, stop=True,
                                                  skip_group_check=True)),
                              reads=[("khf", mhalf, g, tl // 4), ("attq", g)],
                              writes=[PSK[bank]], mm=True)
                        E("scalar", (lambda e, pair=pair, bank=bank, buf=buf:
                                     e.activation(buf[:, pair * 512:(pair + 1) * 512],
                                                  pb[bank][:, 0:512], AF.Exp,
                                                  scale=1.0 / 32.0)),
                          reads=[PSK[bank]], writes=[(bkey, pair)],
                          conflicts=(list(ap2old) if (qc == 0 and tl == 1 and pair == 0)
                                     else ()))
                    E("vector", (lambda e, buf=buf: e.tensor_add(
                        pt[1][:, 0:256], buf[:, 0:256], buf[:, 256:512])),
                      reads=[(bkey, 0)], writes=[("d01",)])
                    E("vector", (lambda e, buf=buf: e.tensor_add(
                        pt[1][:, 256:512], buf[:, 512:768], buf[:, 768:1024])),
                      reads=[(bkey, 1)], writes=[("d23",)])
                    E("vector", (lambda e, buf=buf: e.tensor_add(
                        pt[0][:, 256:512], buf[:, 1024:1280], buf[:, 1280:1536])),
                      reads=[(bkey, 2)], writes=[("d45",)])
                    E("vector", (lambda e, buf=buf: e.tensor_add(
                        den[:, 256:512], buf[:, 1536:1792], buf[:, 1792:2048])),
                      reads=[(bkey, 3)], writes=[("d67",)])
                    E("vector", (lambda e: e.tensor_add(pt[1][:, 0:256], pt[1][:, 0:256],
                                                        pt[1][:, 256:512])),
                      reads=[("d01",), ("d23",)], writes=[("d01",)])
                    E("vector", (lambda e: e.tensor_add(den[:, 0:256], pt[0][:, 256:512],
                                                        den[:, 256:512])),
                      reads=[("d45",), ("d67",)], writes=[("den",)])
                    E("vector", (lambda e: e.tensor_add(den[:, 0:256], den[:, 0:256],
                                                        pt[1][:, 0:256])),
                      reads=[("den",), ("d01",)], writes=[("den",)])
                    E("vector", (lambda e: e.reciprocal(stat_m[:, 0:256], den[:, 0:256])),
                      reads=[("den",)], writes=[("recm",)])
                    E("vector", (lambda e: e.tensor_copy(pt[0][:, 0:256],
                                                         stat_m[:, 0:256])),
                      reads=[("recm",)], writes=[("rec16",)])
                    for g in range(8):
                        E("vector", (lambda e, g=g, buf=buf: e.tensor_mul(
                            buf[:, g * 256:(g + 1) * 256],
                            buf[:, g * 256:(g + 1) * 256], pt[0][:, 0:256])),
                          reads=[(bkey, g // 2), ("rec16",)], writes=[(bkey, g // 2)])
                    if tl >= 1:
                        pbuf = attp if (tl - 1) % 2 == 0 else attp2
                        pkey = "attp" if (tl - 1) % 2 == 0 else "attp2"
                        emit_ctx(tl - 1, mhalf, pbuf, pkey)
                emit_ctx(7, mhalf, attp2, "attp2")
                for g in range(8):
                    E("vector", (lambda e, g=g, qc=qc:
                                 e.tensor_add(x1[:, g * T + qc * 256:
                                                 g * T + (qc + 1) * 256],
                                              x1[:, g * T + qc * 256:
                                                 g * T + (qc + 1) * 256],
                                              pb[4 + g // 2][:, (g % 2) * 256:
                                                             (g % 2) * 256 + 256])),
                      reads=[PSK[4 + g // 2], ("x1", g)], writes=[("x1", g)])

    if STAGES["l1a"]:
        at_sb = sb("at_sb", (128, 512), BF16)
        h_sb = sb("h_sb", (128, 4 * 256), BF16)
        zero2 = sb("zero2", (128, 2), BF16)
        DTC = 0
        BMC = 8192
        CTC = 10240
        XBF = 12288
        W1S = [16384, 20480]
        W2S = [24576, 25600]
        STB = 28672
        Y1A = 32768
        YPR = 34816

        DRAIN(emit_ln(x1, lambda i, qc: ("x1", i), xn, lambda i, qc: ("xn1", i, qc), T))

        E("gpsimd", (lambda e: e.memset(zero2[:, :], 0.0)), writes=[("zero2",)])
        dold = claim(DTC, DTC + 8192, [("dT", i) for i in range(NCH)])
        for i in range(NCH):
            E("gpsimd", (lambda e, i=i: e.dma_start(arena[:, DTC + i * 1024: DTC + (i + 1) * 1024],
                                                    dt["dT"][i * 128:(i + 1) * 128, :])),
              writes=[("dT", i)], conflicts=dold, dma_sem="dw")
        bold = claim(BMC, BMC + 2048, [("bmT", i) for i in range(NCH)])
        for i in range(NCH):
            E("gpsimd", (lambda e, i=i: e.dma_start(arena[:, BMC + i * 256: BMC + (i + 1) * 256],
                                                    dt["bmT"][i * 128:(i + 1) * 128, :])),
              writes=[("bmT", i)], conflicts=bold, dma_sem="dw")
        cold = claim(CTC, CTC + 2048, [("cT", j) for j in range(2)])
        for j in range(2):
            E("gpsimd", (lambda e, j=j: e.dma_start(arena[:, CTC + j * 1024: CTC + (j + 1) * 1024],
                                                    dt["cT"][j * 128:(j + 1) * 128, :])),
              writes=[("cT", j)], conflicts=cold, dma_sem="dw")
        for idx, (i, j) in enumerate(((0, 0), (0, 1), (1, 0), (1, 1))):
            E("gpsimd", (lambda e, idx=idx, i=i, j=j:
                         e.dma_start(at_sb[:, (2 * i + j) * 128:(2 * i + j + 1) * 128],
                                     dt["aT"][i * 128:(i + 1) * 128, j * 128:(j + 1) * 128])),
              writes=[("at", i, j)], dma_sem="dw")

        stold = claim(STB, STB + 4096, [("streg",)])
        E("gpsimd", (lambda e: e.memset(arena[:, STB:STB + 4096], 0.0)),
          writes=[("streg",)], conflicts=stold)
        y1old = claim(Y1A, Y1A + 2048, [("y1reg",)])
        E("gpsimd", (lambda e: e.memset(arena[:, Y1A:Y1A + 2048], 0.0)),
          writes=[("y1reg",)], conflicts=y1old)
        ypold = claim(YPR, YPR + 2048, [("ypreg",)])
        E("gpsimd", (lambda e: e.memset(arena[:, YPR:YPR + 2048], 0.0)),
          writes=[("ypreg",)], conflicts=ypold)

        xbold = claim(XBF, XBF + 4096, [("xB", sc, tc) for sc in range(2) for tc in range(4)])
        for sc in range(2):
            for tc in range(4):
                bank = 4 + tc % 2
                for i in range(NCH):
                    E("tensor", (lambda e, sc=sc, tc=tc, i=i, bank=bank:
                                 e.matmul(pb[bank][:, 0:512],
                                          arena[:, BMC + i * 256 + sc * 128: BMC + i * 256 + (sc + 1) * 128],
                                          xn[:, i * T + tc * 512: i * T + (tc + 1) * 512],
                                          start=(i == 0), stop=(i == NCH - 1),
                                          skip_group_check=True)),
                      reads=[("xn1", i, tc), ("bmT", i)], writes=[PSK[bank]], mm=True)
                E("vector", (lambda e, sc=sc, tc=tc, bank=bank:
                             e.tensor_copy(arena[:, XBF + sc * T + tc * 512: XBF + sc * T + (tc + 1) * 512],
                                           pb[bank][:, 0:512])),
                  reads=[PSK[bank]], writes=[("xB", sc, tc)], conflicts=xbold)

        RUN = 256
        n_runs = T // RUN

        def emit_scan_bank_prep(r):
            bank = 2 + r % 2
            E("tensor", (lambda e, bank=bank: e.matmul(pb[bank][:, 0:512], zrow_b[0:1, 0:128],
                                                       zrow_b[0:1, 0:512], start=True, stop=True)),
              reads=KC(zrow_b), writes=[PSK[bank]], mm=True)
            for j in range(2):
                E("vector", (lambda e, r=r, j=j, bank=bank:
                             e.tensor_copy(pb[bank][:, j:512:2],
                                           arena[:, XBF + j * T + r * RUN: XBF + j * T + (r + 1) * RUN])),
                  reads=[("xB", j, (r * RUN) // 512)], writes=[PSK[bank]])

        def cast_gen():
            slot = [0]

            def pair(src_ap_fn, dst_ap_fn, wkey):
                sl2 = slot[0] % 4
                slot[0] += 1
                E("gpsimd", (lambda e, sl2=sl2, src_ap_fn=src_ap_fn: e.dma_start(
                    attq[:, sl2 * 512:(sl2 + 1) * 512], src_ap_fn())),
                  writes=[("cst", sl2)],
                  conflicts=[("cst", sl2)] + ([("attq", 2 * sl2), ("attq", 2 * sl2 + 1)]
                                              if slot[0] <= 4 else []),
                  dma_sem="dci" + str(sl2))
                E("sync", (lambda e, sl2=sl2, dst_ap_fn=dst_ap_fn: e.dma_start(
                    dst_ap_fn(), attq[:, sl2 * 512:(sl2 + 1) * 512])),
                  reads=[("cst", sl2)], writes=[wkey], dma_sem="dco" + str(sl2))
            for f in range(8):
                for i in range(8):
                    pair((lambda f=f, i=i: dt["w1T"][i * 128:(i + 1) * 128,
                                                     f * 512:(f + 1) * 512]),
                         (lambda f=f, i=i: dt["hb_w1b"][i * 128:(i + 1) * 128,
                                                        f * 512:(f + 1) * 512]),
                         ("hw1c", f, i))
                    yield
            for rc in range(32):
                for hf in range(2):
                    pair((lambda rc=rc, hf=hf: dt["w2T"][rc * 128:(rc + 1) * 128,
                                                         hf * 512:(hf + 1) * 512]),
                         (lambda rc=rc, hf=hf: dt["hb_w2b"][rc * 128:(rc + 1) * 128,
                                                            hf * 512:(hf + 1) * 512]),
                         ("hw2c", rc, hf))
                    yield
            for rc in range(8):
                for hf in range(2):
                    pair((lambda rc=rc, hf=hf: dt["opw"][rc * 128:(rc + 1) * 128,
                                                         hf * 512:(hf + 1) * 512]),
                         (lambda rc=rc, hf=hf: dt["hb_opb"][rc * 128:(rc + 1) * 128,
                                                            hf * 512:(hf + 1) * 512]),
                         ("hopc", rc, hf))
                    yield
            for rc in range(8):
                for hf in range(6):
                    pair((lambda rc=rc, hf=hf: dt["ipw"][rc * 128:(rc + 1) * 128,
                                                         hf * 512:(hf + 1) * 512]),
                         (lambda rc=rc, hf=hf: dt["hb_ipb"][rc * 128:(rc + 1) * 128,
                                                            hf * 512:(hf + 1) * 512]),
                         ("hipc", rc, hf))
                    yield

        def win_outs_gen(w):
            t0 = w * 256
            for oc in range(NCH):
                bank = oc % 2
                for i in range(NCH):
                    E("tensor", (lambda e, oc=oc, i=i, bank=bank, t0=t0:
                                 e.matmul(pb[bank][:, 0:256],
                                          arena[:, DTC + i * 1024 + oc * 128: DTC + i * 1024 + (oc + 1) * 128],
                                          xn[:, i * T + t0: i * T + t0 + 256],
                                          start=(i == 0), stop=False, skip_group_check=True)),
                      reads=[("xn1", i, t0 // 512), ("dT", i)], writes=[PSK[bank]], mm=True)
                    if i % 3 == 2:
                        yield
                for j in range(2):
                    E("tensor", (lambda e, oc=oc, j=j, bank=bank, t0=t0:
                                 e.matmul(pb[bank][:, 0:256],
                                          arena[:, CTC + j * 1024 + oc * 128: CTC + j * 1024 + (oc + 1) * 128],
                                          arena[:, STB + j * T + t0: STB + j * T + t0 + 256],
                                          start=False, stop=(j == 1), skip_group_check=True)),
                      reads=[("st", t0 + 255, j), ("cT", j)], writes=[PSK[bank]], mm=True)
                E("vector", (lambda e, oc=oc, bank=bank, t0=t0:
                             e.tensor_add(arena[:, YPR + oc * 256: YPR + (oc + 1) * 256],
                                          pb[bank][:, 0:256],
                                          xn[:, oc * T + t0: oc * T + t0 + 256])),
                  reads=[PSK[bank], ("xn1", oc, t0 // 512), ("ypreg",)],
                  writes=[("ypre", w, oc)],
                  conflicts=([("ypre", w - 1, oc)] if w >= 1 else []))
                yield
            yield from emit_ln(arena[:, YPR:YPR + 2048], lambda i, qc, w=w: ("ypre", w, i),
                               arena[:, Y1A:Y1A + 2048], lambda i, qc, w=w: ("y1a", w, i),
                               256, "lnw_col", "lnb_col", onescol=onesc_b,
                               dst_extra=[("y1reg",)],
                               dst_conflicts=([("y1a", w - 1, i_) for i_ in range(NCH)]
                                              if w >= 1 else []))

        def win_ffn_gen(w):
            t0 = w * 256
            for bk in range(4, 8):
                E("tensor", (lambda e, bk=bk: e.matmul(pb[bk][:, 0:512], zrow_b[0:1, 0:128],
                                                       zrow_b[0:1, 0:512], start=True, stop=True)),
                  reads=KC(zrow_b), writes=[PSK[bk]], mm=True)
            yield
            for f in range(NCH):
                ws = f % 2
                w1keys = [("w1b", ws, i) for i in range(NCH)]
                w1old = claim(W1S[ws], W1S[ws] + 4096, w1keys)
                for i in range(NCH):
                    E("sync", (lambda e, f=f, i=i, ws=ws:
                                 e.dma_start(arena[:, W1S[ws] + i * 512: W1S[ws] + (i + 1) * 512],
                                             dt["hb_w1b"][i * 128:(i + 1) * 128, f * 512:(f + 1) * 512])),
                      reads=[("hw1c", f, i)],
                      writes=[("w1b", ws, i)], conflicts=w1old, dma_sem="dw1" + str(ws))
                yield
                for sub in range(4):
                    ws2 = (f * 4 + sub) % 2
                    w2old = claim(W2S[ws2], W2S[ws2] + 1024, [("w2b", ws2)])
                    E("sync", (lambda e, f=f, sub=sub, ws2=ws2:
                                 e.dma_start(arena[:, W2S[ws2]: W2S[ws2] + 1024],
                                             dt["hb_w2b"][f * 512 + sub * 128: f * 512 + (sub + 1) * 128, :])),
                      reads=[("hw2c", f * 4 + sub, 0), ("hw2c", f * 4 + sub, 1)],
                      writes=[("w2b", ws2)], conflicts=[("w2b", ws2)] + list(w2old),
                      dma_sem="dw2" + str(ws2))
                    bank = sub % 2
                    for i in range(NCH):
                        E("tensor", (lambda e, sub=sub, i=i, bank=bank, ws=ws:
                                     e.matmul(pb[bank][:, 0:256],
                                              arena[:, W1S[ws] + i * 512 + sub * 128: W1S[ws] + i * 512 + (sub + 1) * 128],
                                              arena[:, Y1A + i * 256: Y1A + (i + 1) * 256],
                                              start=(i == 0), stop=(i == NCH - 1),
                                              skip_group_check=True)),
                          reads=[("y1a", w, i), ("w1b", ws, i)], writes=[PSK[bank]], mm=True)
                        if i % 3 == 2:
                            yield
                    E("scalar", (lambda e, f=f, sub=sub, bank=bank:
                                 e.activation(h_sb[:, sub * 256:(sub + 1) * 256], pb[bank][:, 0:256],
                                              AF.Gelu, bias=smalls["b1_col"][:, f * 4 + sub: f * 4 + sub + 1])),
                      reads=[PSK[bank], K_SM("b1_col")], writes=[("h", sub)])
                    yield
                    for oc in range(NCH):
                        E("tensor", (lambda e, f=f, sub=sub, oc=oc, ws2=ws2:
                                     e.matmul(pb[4 + oc // 2][:, (oc % 2) * 256:(oc % 2) * 256 + 256],
                                              arena[:, W2S[ws2] + oc * 128: W2S[ws2] + (oc + 1) * 128],
                                              h_sb[:, sub * 256:(sub + 1) * 256],
                                              start=False, stop=(f == NCH - 1 and sub == 3),
                                              skip_group_check=True)),
                          reads=[("h", sub), ("w2b", ws2)], writes=[PSK[4 + oc // 2]], mm=True)
                        if oc % 3 == 2:
                            yield
                    yield
            for oc in range(NCH):
                E("vector", (lambda e, oc=oc:
                             e.tensor_scalar(vsc[0][:, 0:256],
                                             pb[4 + oc // 2][:, (oc % 2) * 256:(oc % 2) * 256 + 256],
                                             smalls["b2_col"][:, oc:oc + 1], None, ALU.add)),
                  reads=[PSK[4 + oc // 2], K_SM("b2_col")], writes=[("vsc", id(vsc[0]))])
                E("vector", (lambda e, oc=oc, w=w:
                             e.tensor_add(vsc[1][:, 0:256], vsc[0][:, 0:256],
                                          arena[:, Y1A + oc * 256: Y1A + (oc + 1) * 256])),
                  reads=[("vsc", id(vsc[0])), ("y1a", w, oc)], writes=[("vsc", id(vsc[1]))])
                E("vector", (lambda e, oc=oc, t0=t0:
                             e.tensor_add(x1[:, oc * T + t0: oc * T + t0 + 256],
                                          x1[:, oc * T + t0: oc * T + t0 + 256],
                                          vsc[1][:, 0:256])),
                  reads=[("vsc", id(vsc[1])), ("x1", oc)], writes=[("x1", oc)])
                yield

        import collections as _coll
        fillers = _coll.deque()

        def pump(k=1):
            done = 0
            while fillers and done < k:
                try:
                    next(fillers[0])
                    done += 1
                except StopIteration:
                    fillers.popleft()

        emit_scan_bank_prep(0)
        emit_scan_bank_prep(1)
        for step in range(T):
            r, u = divmod(step, RUN)
            if u == 0:
                if r == 0:
                    fillers.append(cast_gen())
                if r >= 2:
                    fillers.append(win_ffn_gen(r - 2))
                if r >= 1:
                    fillers.append(win_outs_gen(r - 1))
            bank = 2 + r % 2
            for (jj, oj) in ((0, 0), (1, 0), (0, 1), (1, 1)):
                if step == 0:
                    rhs_fn = (lambda e, jj=jj: zero2[:, jj:jj + 1])
                    rkey = ("zero2",)
                else:
                    rhs_fn = (lambda e, jj=jj, step=step: arena[:, STB + jj * T + step - 1: STB + jj * T + step])
                    rkey = ("st", step - 1, jj)
                E("tensor", (lambda e, u=u, bank=bank, jj=jj, oj=oj, rhs_fn=rhs_fn:
                             e.matmul(pb[bank][:, 2 * u + oj:2 * u + oj + 1],
                                      at_sb[:, (2 * jj + oj) * 128:(2 * jj + oj + 1) * 128],
                                      rhs_fn(e),
                                      start=False, stop=(jj == 1), skip_group_check=True)),
                  reads=[rkey, ("at", jj, oj)], writes=[PSK[bank]], mm=True)
            E("scalar", (lambda e, step=step, u=u, bank=bank:
                         e.activation(arena[:, STB + step: STB + 2 * T: T],
                                      pb[bank][:, 2 * u:2 * u + 2], AF.Tanh)),
              reads=[PSK[bank], ("streg",)], writes=[("st", step, 0), ("st", step, 1)])
            if u == 0 and 2 <= r + 1 < n_runs:
                emit_scan_bank_prep(r + 1)
            pump(1)
        fillers.append(win_ffn_gen(6))
        fillers.append(win_outs_gen(7))
        fillers.append(win_ffn_gen(7))
        while fillers:
            pump(1000)

    if STAGES["l1f"]:
        selc = sb("sb_selc", (128, NCORES), F32)
        ipbc = sb("sb_ipbc", (128, 3 * NCH), F32)
        opwb0 = attq[:, 0:1024]
        ptile2 = sb("ptile2", (128, 6 * 128), BF16)
        mrec = recip_sb[:, 0:128]
        ohsb = sb("ohsb", (128, 16 * 8), BF16)
        sgk = sb("sgk", (128, 16 * 4), F32)
        mx8 = sb("mx8", (128, 8), F32)
        ix8 = sb("ix8", (128, 8), mybir.dt.uint32)
        gsc = sb("gsc", (128, 8), F32)
        priors = sb("priors", (1, 8), F32)
        cntall = sb("cntall", (64, 8), F32)
        m64sb = sb("sb_m64", (64, 8), F32)
        iota8r = sb("sb_iota8r", (128, 8), F32)
        sbst = x1[:, 4096:6144]

        for nm, buf in [("selcol", selc), ("ipb_col", ipbc), ("m64", m64sb),
                        ("iota8r", iota8r)]:
            if buf.dtype == F32:
                E("sync", (lambda e, nm=nm, buf=buf: e.dma_start(buf[:, :], dt[nm][:, :])),
                  writes=[("small", nm)], dma_sem="dsm2")
            else:
                E("gpsimd", (lambda e, nm=nm, buf=buf: e.dma_start(buf[:, :], dt[nm][:, :])),
                  writes=[("small", nm)], dma_sem="dsb2")

        for i in range(NCH):
            E("sync", (lambda e, i=i: e.dma_start(dt["ag_xn1_in"][i * 128:(i + 1) * 128, :],
                                                  xn[:, i * T:(i + 1) * T])),
              reads=[("xn1", i, 0), ("xn1", i, 1), ("xn1", i, 2), ("xn1", i, 3)],
              writes=[("agin", i)], dma_sem="dagi")
        E("gpsimd", (lambda e: e.collective_compute(
            "AllGather", ALU.bypass, replica_groups=[list(range(NCORES))],
            ins=[dt["ag_xn1_in"].opt()], outs=[dt["ag_xn1_out"].opt()])),
          reads=[("agin", i) for i in range(NCH)], writes=[("ago_xn1",)], cc=True,
          dma_sem="ccx")
        for i in range(NCH):
            E("gpsimd", (lambda e, i=i: e.dma_start(dt["ag_s_in"][i * 128:(i + 1) * 128, :],
                                                    x1[:, i * T:(i + 1) * T])),
              reads=[("x1", i)], writes=[("agsin", i)], dma_sem="dags")
        E("gpsimd", (lambda e: e.collective_compute(
            "AllGather", ALU.bypass, replica_groups=[list(range(NCORES))],
            ins=[dt["ag_s_in"].opt()], outs=[dt["ag_s_out"].opt()])),
          reads=[("agsin", i) for i in range(NCH)], writes=[("ago_s",)], cc=True,
          dma_sem="ccs")

        fr = x1[:, 8192:9984].bitcast(BF16)
        for bh in range(2):
            for fch in range(32):
                s1 = fch % 2
                E("gpsimd", (lambda e, fch=fch, s1=s1: e.dma_start(
                    fr[:, s1 * 1024:(s1 + 1) * 1024],
                    dt["myw1T"][fch * 128:(fch + 1) * 128, :])),
                  writes=[("fw1", s1)],
                  conflicts=[("fw1", s1)] + ([("x1", 4)] if (bh == 0 and fch < 2) else []),
                  dma_sem="dfw" + str(s1))
                E("gpsimd", (lambda e, fch=fch, s1=s1, bh=bh: e.dma_start(
                    fr[:, 2048 + s1 * 512: 2048 + (s1 + 1) * 512],
                    dt["myw2"][fch * 128:(fch + 1) * 128, bh * 512:(bh + 1) * 512])),
                  writes=[("fw2", s1)], conflicts=[("fw2", s1)], dma_sem="dfv" + str(s1))
                for a in range(8):
                    E("tensor", (lambda e, s1=s1, a=a, fch=fch:
                                 e.matmul(pb[a][:, 0:512],
                                          fr[:, s1 * 1024 + a * 128:
                                             s1 * 1024 + (a + 1) * 128],
                                          fr[:, 2048 + s1 * 512: 2048 + (s1 + 1) * 512],
                                          start=(fch == 0), stop=(fch == 31),
                                          skip_group_check=True)),
                      reads=[("fw1", s1), ("fw2", s1)], writes=[PSK[a]], mm=True)
            for a in range(8):
                E("scalar", (lambda e, a=a: e.copy(fr[:, 3072:3584], pb[a][:, 0:512])),
                  reads=[PSK[a]], writes=[("fstg",)])
                E("sync", (lambda e, a=a, bh=bh: e.dma_start(
                    dt["ag_wf_in"][a * 128:(a + 1) * 128, bh * 512:(bh + 1) * 512],
                    fr[:, 3072:3584])),
                  reads=[("fstg",)], writes=[("wfin", bh, a)], dma_sem="dwfi")
        E("gpsimd", (lambda e: e.dma_start(ohsb[:, 0:32], dt["myb1c"][:, :])),
          writes=[("b1c",)], dma_sem="dsb2")
        for bh in range(2):
            for fch in range(32):
                s1 = fch % 2
                E("gpsimd", (lambda e, fch=fch, s1=s1, bh=bh: e.dma_start(
                    fr[:, 2048 + s1 * 512: 2048 + (s1 + 1) * 512],
                    dt["myw2"][fch * 128:(fch + 1) * 128, bh * 512:(bh + 1) * 512])),
                  writes=[("fw2", s1)], conflicts=[("fw2", s1)], dma_sem="dfv" + str(s1))
                E("tensor", (lambda e, fch=fch, s1=s1:
                             e.matmul(pb[2][0:1, 0:512], ohsb[:, fch:fch + 1],
                                      fr[:, 2048 + s1 * 512: 2048 + (s1 + 1) * 512],
                                      start=(fch == 0), stop=False,
                                      skip_group_check=True)),
                  reads=[("b1c",), ("fw2", s1)], writes=[PSK[2]], mm=True)
            E("gpsimd", (lambda e, bh=bh: e.dma_start(
                pt[0][0:1, 0:512], dt["myb2r"][0:1, bh * 512:(bh + 1) * 512])),
              writes=[("b2r",)], conflicts=[("b2r",), ("rec16",), ("d45",)],
              dma_sem="db2r")
            E("tensor", (lambda e: e.matmul(pb[2][0:1, 0:512], onesr_b[0:1, 0:1],
                                            pt[0][0:1, 0:512], start=False, stop=True,
                                            skip_group_check=True)),
              reads=[("b2r",)] + KC(onesr_b), writes=[PSK[2]], mm=True)
            E("scalar", (lambda e: e.copy(fr[0:1, 3072:3584], pb[2][0:1, 0:512])),
              reads=[PSK[2]], writes=[("fstg",)])
            E("sync", (lambda e, bh=bh: e.dma_start(
                dt["ag_wf_in"][1024:1025, bh * 512:(bh + 1) * 512],
                fr[0:1, 3072:3584])),
              reads=[("fstg",)], writes=[("wfinb", bh)], dma_sem="dwfi")
        E("gpsimd", (lambda e: e.collective_compute(
            "AllGather", ALU.bypass, replica_groups=[list(range(NCORES))],
            ins=[dt["ag_wf_in"].opt()], outs=[dt["ag_wf_out"].opt()])),
          reads=[("wfin", bh, a) for bh in range(2) for a in range(8)]
                + [("wfinb", bh) for bh in range(2)],
          writes=[("ago_wf",)], cc=True, dma_sem="ccw")

        selold = claim(0, 8192, [("cselscr",)])
        for i in range(NCH):
            for hh in range(2):
                E("gpsimd", (lambda e, i=i, hh=hh: e.dma_start(
                    arena[:, 0:8192].rearrange("p (b t) -> p b t", b=4),
                    dt["ag_xn1_out"][hh * 4:(hh + 1) * 4, i * 128:(i + 1) * 128, :]
                        .transpose([1, 0, 2]))),
                  reads=[("ago_xn1",)], writes=[("cselscr",)], dma_sem="dsel",
                  conflicts=(selold if (i == 0 and hh == 0) else ()))
                for w in range(NCORES):
                    mkdst = (lambda e, i=i, hh=hh:
                             xn[:, i * 2048: (i + 1) * 2048]
                             .rearrange("p (t b) -> p t b", b=8)
                             [:, :, hh * 4:(hh + 1) * 4])
                    mksrc = (lambda e, w=w: arena[:, 0:8192]
                             .rearrange("p (b t) -> p t b", b=4)
                             [:, w * 256:(w + 1) * 256, :])
                    if w == 0:
                        E("vector", (lambda e, w=w, mkdst=mkdst, mksrc=mksrc:
                                     e.tensor_scalar(mkdst(e), mksrc(e), selc[:, w:w + 1],
                                                     None, ALU.mult)),
                          reads=[("cselscr",), ("small", "selcol")],
                          writes=[("xn1t", i, hh)],
                          conflicts=[("xn1", i, q_) for q_ in range(4)])
                    else:
                        E("vector", (lambda e, w=w, mkdst=mkdst, mksrc=mksrc:
                                     e.scalar_tensor_tensor(mkdst(e), mksrc(e),
                                                            selc[:, w:w + 1], mkdst(e),
                                                            ALU.mult, ALU.add)),
                          reads=[("cselscr",), ("small", "selcol"), ("xn1t", i, hh)],
                          writes=[("xn1t", i, hh)])
        XNT = lambda i: [("xn1t", i, 0), ("xn1t", i, 1)]

        def xslab(e, i, t0, n=64):
            sl = t0 // 64
            return xn[:, i * 2048 + sl * 512: i * 2048 + (sl + 1) * 512]

        ipold = claim(0, 24576, [("ipw", i) for i in range(NCH)])
        for i in range(NCH):
            E("sync", (lambda e, i=i: e.dma_start(arena[:, i * 3072:(i + 1) * 3072],
                                                  dt["hb_ipb"][i * 128:(i + 1) * 128, :])),
              reads=[("hipc", i, hf_) for hf_ in range(6)],
              writes=[("ipw", i)], conflicts=ipold, dma_sem="dipb")
        QS, KS, VS = 24576, 28672, 32768
        opw_last = {}
        ypsl = arena[:, 32768:36864]
        y1sl = arena[:, 28672:32768]
        qold = claim(QS, QS + 4096, [("qsb", j) for j in range(8)])
        kold = claim(KS, KS + 4096, [("ksb", j) for j in range(8)])
        vold2 = claim(VS, VS + 4096, [("vsb", j, hf) for j in range(4) for hf in range(2)])

        ypsl = x1[:, 10240:12288].bitcast(BF16)
        y1sl = x1[:, 12288:14336].bitcast(BF16)

        def c1_qkv_gen(sl_):
            for och in range(16):
                bank = och % 2
                dstoff = QS + och * 512 if och < 8 else KS + (och - 8) * 512
                for i in range(NCH):
                    E("tensor", (lambda e, och=och, i=i, bank=bank, sl_=sl_:
                                 e.matmul(pb[bank][:, 0:512],
                                          arena[:, i * 3072 + och * 128: i * 3072 + (och + 1) * 128],
                                          xn[:, i * 2048 + sl_ * 512:
                                             i * 2048 + (sl_ + 1) * 512],
                                          start=(i == 0), stop=(i == NCH - 1),
                                          skip_group_check=True)),
                      reads=XNT(i) + [("ipw", i)], writes=[PSK[bank]], mm=True)
                    if i % 3 == 2:
                        yield
                E("scalar", (lambda e, och=och, bank=bank, dstoff=dstoff:
                             e.activation(arena[:, dstoff:dstoff + 512], pb[bank][:, 0:512],
                                          AF.Identity, bias=ipbc[:, och:och + 1])),
                  reads=[PSK[bank], ("small", "ipb_col")],
                  writes=[("qsb", och) if och < 8 else ("ksb", och - 8)],
                  conflicts=((qold if sl_ == 0 else []) if och == 0 else
                             ((kold if sl_ == 0 else []) if och == 8 else []))
                            + ([("ctx", och, nb_) for nb_ in range(4)] if och < 8 else []))
                yield
            for vt in range(4):
                for half in range(2):
                    bank = half
                    for i in range(NCH):
                        E("tensor", (lambda e, vt=vt, half=half, i=i, bank=bank, sl_=sl_:
                                     e.matmul(pb[bank][:, 0:512],
                                              xn[:, i * 2048 + sl_ * 512 + vt * 128:
                                                 i * 2048 + sl_ * 512 + (vt + 1) * 128],
                                              arena[:, i * 3072 + 2048 + half * 512:
                                                    i * 3072 + 2048 + (half + 1) * 512],
                                              start=(i == 0), stop=False,
                                              skip_group_check=True)),
                          reads=XNT(i) + [("ipw", i)], writes=[PSK[bank]], mm=True)
                        if i % 3 == 2:
                            yield
                    E("tensor", (lambda e, half=half, bank=bank:
                                 e.matmul(pb[bank][:, 0:512], onesr_b[0:1, :],
                                          smallsb["ipb_row"][0:1, half * 512:
                                                             (half + 1) * 512],
                                          start=False, stop=True, skip_group_check=True)),
                      reads=[K_SM("ipb_row")] + KC(onesr_b), writes=[PSK[bank]], mm=True)
                    E("scalar", (lambda e, vt=vt, half=half, bank=bank:
                                 e.copy(arena[:, VS + vt * 1024 + half * 512:
                                              VS + vt * 1024 + (half + 1) * 512],
                                        pb[bank][:, 0:512])),
                      reads=[PSK[bank]], writes=[("vsb", vt, half)],
                      conflicts=(vold2 if (sl_ == 0 and vt == 0 and half == 0) else ()))
                    yield

        def c1_tail_gen(sl_):
            t0 = sl_ * 64
            for th in range(2):
                for bk in range(4, 8):
                    E("tensor", (lambda e, bk=bk: e.matmul(
                        pb[bk][:, 0:512], zrow_b[0:1, 0:128], zrow_b[0:1, 0:512],
                        start=True, stop=True)),
                      reads=KC(zrow_b), writes=[PSK[bk]], mm=True)
                for i in range(NCH):
                    ws3 = i % 2
                    wb = opwb0 if ws3 == 0 else h_sb
                    prevk = opw_last.get(ws3)
                    E("sync", (lambda e, i=i, wb=wb, th=th:
                                 e.dma_start(wb[:, 0:1024], dt["hb_opb"][i * 128:(i + 1) * 128, :])),
                      reads=[("hopc", i, 0), ("hopc", i, 1)],
                      writes=[("opwb", sl_, th, i)],
                      conflicts=([prevk] if prevk else
                                 ([("cst", s_) for s_ in range(4)] if ws3 == 0 else [])),
                      dma_sem="dop" + str(ws3))
                    opw_last[ws3] = ("opwb", sl_, th, i)
                    for oc in range(NCH):
                        E("tensor", (lambda e, oc=oc, i=i, wb=wb, th=th:
                                     e.matmul(pb[4 + oc // 2][:, (oc % 2) * 256:
                                                             (oc % 2) * 256 + 256],
                                              wb[:, oc * 128:(oc + 1) * 128],
                                              arena[:, QS + i * 512 + th * 256:
                                                    QS + i * 512 + th * 256 + 256],
                                              start=False, stop=(i == NCH - 1),
                                              skip_group_check=True)),
                          reads=[("ctx", i, 0), ("ctx", i, 1), ("ctx", i, 2), ("ctx", i, 3),
                                 ("opwb", sl_, th, i)],
                          writes=[PSK[4 + oc // 2]], mm=True)
                for oc in range(NCH):
                    E("vector", (lambda e, oc=oc, t0=t0, th=th, sl_=sl_:
                                 e.tensor_add(vsc[0][:, 0:256],
                                              pb[4 + oc // 2][:, (oc % 2) * 256:
                                                              (oc % 2) * 256 + 256],
                                              xn[:, oc * 2048 + sl_ * 512 + th * 256:
                                                 oc * 2048 + sl_ * 512 + th * 256 + 256])),
                      reads=[PSK[4 + oc // 2]] + XNT(oc), writes=[("vsc", id(vsc[0]))])
                    E("vector", (lambda e, oc=oc, th=th:
                                 e.tensor_scalar(ypsl[:, oc * 512 + th * 256:
                                                     oc * 512 + th * 256 + 256],
                                                 vsc[0][:, 0:256],
                                                 smalls["opb_col"][:, oc:oc + 1],
                                                 None, ALU.add)),
                      reads=[("vsc", id(vsc[0])), K_SM("opb_col")], writes=[("ypsl", oc)],
                      conflicts=([("x1", 5)] if (sl_ == 0 and th == 0 and oc == 0) else ()))
                    if th == 1:
                        yield
            yield from emit_ln(ypsl, lambda i, qc: ("ypsl", i), y1sl,
                               lambda i, qc: ("y1sl", i), 512,
                               "ln1g_col", "ln1b_col", onescol=onesc_b,
                               dst_conflicts=([("x1", 6)] if sl_ == 0 else ()))
            for i in range(NCH):
                E("vector", (lambda e, i=i: e.tensor_copy(
                    pt[i % 2][:, 0:512].rearrange("p (b t) -> p b t", b=8),
                    y1sl[:, i * 512:(i + 1) * 512].rearrange("p (t b) -> p b t", b=8))),
                  reads=[("y1sl", i)], writes=[("y1stg", i % 2)],
                  conflicts=([("rec16",), ("d01",), ("d23",), ("d45",)]
                             if (sl_ == 0 and i < 2) else ()))
                E("gpsimd", (lambda e, i=i, t0=t0: e.dma_start(
                    dt["hb_y1"][i * 128:(i + 1) * 128, :]
                        .rearrange("p (b t) -> p b t", b=8)[:, :, t0:t0 + 64],
                    pt[i % 2][:, 0:512].rearrange("p (b t) -> p b t", b=8))),
                  reads=[("y1stg", i % 2)], writes=[("hby1", sl_, i)], dma_sem="dy1")
                yield

        def rr(g1, g2):
            gens = [g for g in (g1, g2) if g is not None]
            while gens:
                nxt = []
                for g in gens:
                    try:
                        next(g)
                        nxt.append(g)
                    except StopIteration:
                        pass
                gens = nxt

        DRAIN(c1_qkv_gen(0))
        pend_qkv = None
        for sl_ in range(4):
            t0 = sl_ * 64
            for nb in range(4):
                E("tensor", (lambda e: e.matmul(pb[1][:, 0:512], zrow_b[0:1, 0:128],
                                                zrow_b[0:1, 0:512], start=True, stop=True)),
                  reads=KC(zrow_b), writes=[PSK[1]], mm=True)
                for h in range(4):
                    bank = 2 + h % 2
                    for ch in range(2):
                        E("tensor", (lambda e, nb=nb, h=h, ch=ch, bank=bank:
                                     e.matmul(pb[bank][:, 0:128],
                                              arena[:, KS + (2 * h + ch) * 512 + nb * 128:
                                                    KS + (2 * h + ch) * 512 + (nb + 1) * 128],
                                              arena[:, QS + (2 * h + ch) * 512 + nb * 128:
                                                    QS + (2 * h + ch) * 512 + (nb + 1) * 128],
                                              start=(ch == 0), stop=(ch == 1),
                                              skip_group_check=True)),
                          reads=[("qsb", 2 * h + ch), ("ksb", 2 * h + ch)],
                          writes=[PSK[bank]], mm=True)
                    E("vector", (lambda e, bank=bank:
                                 e.tensor_add(vsc[0][:, 0:128], pb[bank][:, 0:128],
                                              smallsb["mha_mask"][:, :])),
                      reads=[PSK[bank], K_SM("mha_mask")], writes=[("vsc", id(vsc[0]))])
                    E("scalar", (lambda e, h=h:
                                 e.activation(ptile2[:, 256 + h * 128: 256 + (h + 1) * 128],
                                              vsc[0][:, 0:128], AF.Exp, scale=1.0 / 16.0)),
                      reads=[("vsc", id(vsc[0]))], writes=[("pt2s", h)])
                    E("tensor", (lambda e, h=h:
                                 e.matmul(pb[1][32 * h:32 * h + 1, 0:128], onesc_b[:, 0:1],
                                          ptile2[:, 256 + h * 128: 256 + (h + 1) * 128],
                                          start=False, stop=True, skip_group_check=True,
                                          tile_position=(0, 32 * h))),
                      reads=[("pt2s", h)] + KC(onesc_b), writes=[PSK[1]], mm=True)
                E("vector", (lambda e: e.reciprocal(mrec[:, :], pb[1][:, 0:128])),
                  reads=[PSK[1]], writes=[("mrec",)])
                for h in range(4):
                    E("tensor", (lambda e, h=h: e.matmul(pb[0][:, 0:128],
                                                         onesr_f[32 * h:32 * h + 1, :],
                                                         mrec[32 * h:32 * h + 1, 0:128],
                                                         start=True, stop=True,
                                                         tile_position=(32 * h, 0))),
                      reads=[("mrec",)] + KC(onesr_f), writes=[PSK[0]], mm=True)
                    E("vector", (lambda e, h=h:
                                 e.tensor_mul(ptile2[:, (h % 2) * 128:(h % 2) * 128 + 128],
                                              ptile2[:, 256 + h * 128: 256 + (h + 1) * 128],
                                              pb[0][:, 0:128])),
                      reads=[("pt2s", h), PSK[0]], writes=[("pt2", h % 2)])
                    for ch in range(2):
                        bank = 4 + (2 * h + ch) % 4
                        E("tensor", (lambda e, nb=nb, h=h, ch=ch, bank=bank:
                                     e.matmul(pb[bank][:, 0:128],
                                              arena[:, VS + nb * 1024 + h * 256 + ch * 128:
                                                    VS + nb * 1024 + h * 256 + (ch + 1) * 128],
                                              ptile2[:, (h % 2) * 128:(h % 2) * 128 + 128],
                                              start=True, stop=True)),
                          reads=[("pt2", h % 2), ("vsb", nb, (h * 2 + ch) // 4)],
                          writes=[PSK[bank]], mm=True)
                        E("scalar", (lambda e, nb=nb, h=h, ch=ch, bank=bank:
                                     e.copy(arena[:, QS + (2 * h + ch) * 512 + nb * 128:
                                                  QS + (2 * h + ch) * 512 + (nb + 1) * 128],
                                            pb[bank][:, 0:128])),
                          reads=[PSK[bank]], writes=[("ctx", 2 * h + ch, nb)],
                          conflicts=[("qsb", 2 * h + ch)])
            tail = c1_tail_gen(sl_)
            rr(tail, c1_qkv_gen(sl_ + 1) if sl_ < 3 else None)
        wgb = sb("wgb", (128, NCH * NEXP), BF16)
        slrow = recb[0:1, 0:128]
        for i in range(NCH):
            E("gpsimd", (lambda e, i=i: e.dma_start(wgb[:, i * 8:(i + 1) * 8],
                                                    dt["wg"][i * 128:(i + 1) * 128, :])),
              writes=[("wgb", i)], dma_sem="dwgb")
        for bk in (6, 7):
            E("tensor", (lambda e, bk=bk: e.matmul(pb[bk][:, 0:512], zrow_b[0:1, 0:128],
                                                   zrow_b[0:1, 0:512], start=True, stop=True)),
              reads=KC(zrow_b), writes=[PSK[bk]], mm=True)
        for tt in range(16):
            ws = tt % 2
            E("gpsimd", (lambda e, tt=tt, ws=ws: e.dma_start(
                y1tb[:, ws * 1024:(ws + 1) * 1024].rearrange("p (i c) -> p i c", i=8),
                dt["hb_y1"][:, tt * 128:(tt + 1) * 128].rearrange("(i p) c -> p i c", p=128))),
              reads=[("hby1", s_, i_) for s_ in range(4) for i_ in range(NCH)],
              writes=[("y1g", ws)],
              conflicts=[("y1g", ws)] + ([("attp", p_) for p_ in range(4)]
                                         if tt < 2 else []),
              dma_sem="dyg" + str(ws))
            bank = 2 + ws
            for i in range(NCH):
                E("tensor", (lambda e, tt=tt, ws=ws, i=i, bank=bank:
                             e.matmul(pb[bank][:, 0:8],
                                      y1tb[:, ws * 1024 + i * 128: ws * 1024 + (i + 1) * 128],
                                      wgb[:, i * 8:(i + 1) * 8],
                                      start=(i == 0), stop=(i == NCH - 1),
                                      skip_group_check=True)),
                  reads=[("y1g", ws), ("wgb", i)], writes=[PSK[bank]], mm=True)
            E("vector", (lambda e, bank=bank: e.tensor_copy(gsc[:, :], pb[bank][:, 0:8])),
              reads=[PSK[bank]], writes=[("gsc",)])
            E("vector", (lambda e: e.max(mx8[:, :], gsc[:, :])),
              reads=[("gsc",)], writes=[("mx8",)])
            E("vector", (lambda e: e.max_index(ix8[:, :], mx8[:, :], gsc[:, :])),
              reads=[("gsc",), ("mx8",)], writes=[("ix8",)])
            E("vector", (lambda e: e.tensor_scalar(vsc[0][:, 0:1], mx8[:, 0:1], -1.0,
                                                   None, ALU.mult)),
              reads=[("mx8",)], writes=[("vsc", id(vsc[0]))])
            E("scalar", (lambda e: e.activation(vsc[1][:, 0:8], gsc[:, :], AF.Exp,
                                                bias=vsc[0][:, 0:1])),
              reads=[("gsc",), ("vsc", id(vsc[0]))], writes=[("vsc", id(vsc[1]))])
            E("vector", (lambda e: e.reduce_sum(vsc[0][:, 1:2], vsc[1][:, 0:8],
                                                mybir.AxisListType.X)),
              reads=[("vsc", id(vsc[1]))], writes=[("vsc", id(vsc[0]))])
            E("vector", (lambda e, tt=tt: e.reciprocal(sgk[:, 32 + tt:32 + tt + 1],
                                                       vsc[0][:, 1:2])),
              reads=[("vsc", id(vsc[0]))], writes=[("gv", tt)])
            E("vector", (lambda e, tt=tt: e.tensor_copy(sgk[:, 16 + tt:16 + tt + 1],
                                                        ix8[:, 0:1])),
              reads=[("ix8",)], writes=[("ixf", tt)])
            E("vector", (lambda e, tt=tt: e.tensor_scalar(ohsb[:, tt * 8:(tt + 1) * 8],
                                                          iota8r[:, :],
                                                          sgk[:, 16 + tt:16 + tt + 1],
                                                          None, ALU.is_equal)),
              reads=[("ixf", tt), ("small", "iota8r")], writes=[("oh", tt)],
              conflicts=([("b1c",)] if tt < 4 else []))
            b_ = tt // 2
            E("tensor", (lambda e, tt=tt, b_=b_:
                         e.matmul(pb[6 + b_ // 4][32 * (b_ % 4):32 * (b_ % 4) + 1, 0:8],
                                  onesc_b[:, 0:1], ohsb[:, tt * 8:(tt + 1) * 8],
                                  start=False, stop=(tt % 2 == 1), skip_group_check=True,
                                  tile_position=(0, 32 * (b_ % 4)))),
              reads=[("oh", tt)] + KC(onesc_b), writes=[PSK[6 + b_ // 4]], mm=True)
        cnt_sb = sb("cnt_sb", (128, 16), F32)
        for bk in (6, 7):
            E("scalar", (lambda e, bk=bk: e.copy(cnt_sb[:, (bk - 6) * 8:(bk - 5) * 8],
                                                 pb[bk][:, 0:8])),
              reads=[PSK[bk]], writes=[("cnt_sb", bk)])
        for b_ in range(8):
            E("gpsimd", (lambda e, b_=b_: e.dma_start(
                dt["ag_cnt_in"][b_:b_ + 1, :],
                cnt_sb[32 * (b_ % 4):32 * (b_ % 4) + 1, (b_ // 4) * 8:(b_ // 4) * 8 + 8])),
              reads=[("cnt_sb", 6 + b_ // 4)], writes=[("cntin", b_)], dma_sem="dcnt")
        E("gpsimd", (lambda e: e.collective_compute(
            "AllGather", ALU.bypass, replica_groups=[list(range(NCORES))],
            ins=[dt["ag_cnt_in"].opt()], outs=[dt["ag_cnt_out"].opt()])),
          reads=[("cntin", b_) for b_ in range(8)], writes=[("ago_cnt",)], cc=True,
          dma_sem="ccc")
        E("sync", (lambda e: e.dma_start(cntall[:, :],
                                         dt["ag_cnt_out"].rearrange("c b e -> (c b) e"))),
          reads=[("ago_cnt",)], writes=[("cntall",)], dma_sem="dcnl")
        for bk in (0, 1):
            E("tensor", (lambda e, bk=bk: e.matmul(pb[bk][:, 0:512], zrow_b[0:1, 0:128],
                                                   zrow_b[0:1, 0:512], start=True, stop=True)),
              reads=KC(zrow_b), writes=[PSK[bk]], mm=True)
        for b_ in range(8):
            E("tensor", (lambda e, b_=b_: e.matmul(
                pb[b_ // 4][32 * (b_ % 4):32 * (b_ % 4) + 1, 0:8],
                m64sb[:, b_:b_ + 1], cntall[:, :], start=False, stop=True,
                skip_group_check=True, tile_position=(0, 32 * (b_ % 4)))),
              reads=[("cntall",), K_SM("m64")], writes=[PSK[b_ // 4]], mm=True)
        basesr = recb[:, 128:144]
        for bk in (0, 1):
            E("scalar", (lambda e, bk=bk: e.copy(basesr[:, bk * 8:(bk + 1) * 8],
                                                 pb[bk][:, 0:8])),
              reads=[PSK[bk]], writes=[("basesr", bk)])
        E("gpsimd", (lambda e: e.memset(priors[:, :], 0.0)), writes=[("priors",)])
        for tt in range(16):
            bank = 2 + tt % 2
            E("tensor", (lambda e, tt=tt, bank=bank:
                         e.matmul(pb[bank][:, 0:8], smallsb["tri_all"][:, :],
                                  ohsb[:, tt * 8:(tt + 1) * 8], start=True, stop=True)),
              reads=[("oh", tt), K_SM("tri_all")], writes=[PSK[bank]], mm=True)
            E("tensor", (lambda e, tt=tt, bank=bank:
                         e.matmul(pb[bank][:, 0:8], onesr_f[0:1, :], priors[0:1, :],
                                  start=False, stop=True, skip_group_check=True)),
              reads=[("priors",)] + KC(onesr_f), writes=[PSK[bank]], mm=True)
            b_ = tt // 2
            E("tensor", (lambda e, tt=tt, bank=bank, b_=b_:
                         e.matmul(pb[bank][:, 0:8],
                                  onesr_f[32 * (b_ % 4):32 * (b_ % 4) + 1, :],
                                  basesr[32 * (b_ % 4):32 * (b_ % 4) + 1,
                                         (b_ // 4) * 8:(b_ // 4) * 8 + 8],
                                  start=False, stop=True, skip_group_check=True,
                                  tile_position=(32 * (b_ % 4), 0))),
              reads=[("basesr", b_ // 4)] + KC(onesr_f), writes=[PSK[bank]], mm=True)
            E("vector", (lambda e, tt=tt, bank=bank:
                         e.tensor_mul(vsc[0][:, 0:8], pb[bank][:, 0:8],
                                      ohsb[:, tt * 8:(tt + 1) * 8])),
              reads=[PSK[bank], ("oh", tt)], writes=[("vsc", id(vsc[0]))])
            E("vector", (lambda e: e.reduce_sum(vsc[1][:, 3:4], vsc[0][:, 0:8],
                                                mybir.AxisListType.X)),
              reads=[("vsc", id(vsc[0]))], writes=[("locv",)])
            E("vector", (lambda e: e.tensor_scalar(vsc[1][:, 4:5], vsc[1][:, 3:4],
                                                   float(CAPG) - 0.5, None, ALU.is_lt)),
              reads=[("locv",)], writes=[("keep",)])
            E("vector", (lambda e, tt=tt: e.tensor_mul(sgk[:, 32 + tt:32 + tt + 1],
                                                       sgk[:, 32 + tt:32 + tt + 1],
                                                       vsc[1][:, 4:5])),
              reads=[("keep",), ("gv", tt)], writes=[("gv", tt)])
            E("tensor", (lambda e, tt=tt: e.matmul(pb[1][0:1, 0:8], onesc_b[:, 0:1],
                                                   ohsb[:, tt * 8:(tt + 1) * 8],
                                                   start=True, stop=True)),
              reads=[("oh", tt)] + KC(onesc_b), writes=[PSK[1]], mm=True)
            E("vector", (lambda e: e.tensor_add(priors[0:1, :], priors[0:1, :],
                                                pb[1][0:1, 0:8])),
              reads=[PSK[1], ("priors",)], writes=[("priors",)])
        for tt in range(16):
            E("tensor", (lambda e, tt=tt: e.transpose(pb[0][0:1, 0:128],
                                                      sgk[:, 16 + tt:16 + tt + 1],
                                                      smalls["identf"][:, :])),
              reads=[("ixf", tt), K_SM("identf")], writes=[PSK[0]], mm=True)
            E("scalar", (lambda e: e.copy(slrow[0:1, :], pb[0][0:1, 0:128])),
              reads=[PSK[0]], writes=[("slrow",)])
            E("tensor", (lambda e: e.matmul(pb[1][:, 0:128], onesr_f[0:1, :],
                                            slrow[0:1, :], start=True, stop=True)),
              reads=[("slrow",)] + KC(onesr_f), writes=[PSK[1]], mm=True)
            E("vector", (lambda e, tt=tt: e.tensor_copy(sbst[:, tt * 128:(tt + 1) * 128],
                                                        pb[1][:, 0:128])),
              reads=[PSK[1]], writes=[("sbst", tt // 4)],
              conflicts=([("x1", 2), ("x1", 3)] if tt == 0 else ()))
            E("tensor", (lambda e, tt=tt: e.transpose(pb[0][0:1, 0:128],
                                                      sgk[:, 32 + tt:32 + tt + 1],
                                                      smalls["identf"][:, :])),
              reads=[("gv", tt), K_SM("identf")], writes=[PSK[0]], mm=True)
            E("scalar", (lambda e: e.copy(slrow[0:1, :], pb[0][0:1, 0:128])),
              reads=[PSK[0]], writes=[("slrow",)])
            E("tensor", (lambda e: e.matmul(pb[1][:, 0:128], onesr_f[0:1, :],
                                            slrow[0:1, :], start=True, stop=True)),
              reads=[("slrow",)] + KC(onesr_f), writes=[PSK[1]], mm=True)
            E("scalar", (lambda e, tt=tt: e.copy(y1tb[:, tt * 128:(tt + 1) * 128],
                                                 pb[1][:, 0:128])),
              reads=[PSK[1]], writes=[("gkb", tt // 4)],
              conflicts=[("y1g", 0), ("y1g", 1)])

        WER, MER = 16384, 28672
        sscrold = claim(0, 4096, [("sselscr",)])
        ssout = claim(4096, 4608, [("ssres",)])
        for i in range(NCH):
            for q4 in range(4):
                E("gpsimd", (lambda e, i=i, q4=q4: e.dma_start(
                    arena[:, 0:4096].rearrange("p (b t) -> p b t", b=2),
                    dt["ag_s_out"][q4 * 2:(q4 + 1) * 2, i * 128:(i + 1) * 128, :]
                        .transpose([1, 0, 2]))),
                  reads=[("ago_s",), ("sselscr",)], writes=[("sselscr",)],
                  conflicts=(sscrold if (i == 0 and q4 == 0) else ()), dma_sem="dsel2")
                for w in range(NCORES):
                    mkd = (lambda e: arena[:, 4096:4608]
                           .rearrange("p (b t) -> p b t", b=2))
                    mks = (lambda e, w=w: arena[:, 0:4096]
                           .rearrange("p (b t) -> p b t", b=2)[:, :, w * 256:(w + 1) * 256])
                    if w == 0:
                        E("vector", (lambda e, w=w, mkd=mkd, mks=mks:
                                     e.tensor_scalar(mkd(e), mks(e), selc[:, w:w + 1],
                                                     None, ALU.mult)),
                          reads=[("sselscr",), ("small", "selcol")], writes=[("ssres",)],
                          conflicts=(ssout if (i == 0 and q4 == 0) else ()))
                    else:
                        E("vector", (lambda e, w=w, mkd=mkd, mks=mks:
                                     e.scalar_tensor_tensor(mkd(e), mks(e), selc[:, w:w + 1],
                                                            mkd(e), ALU.mult, ALU.add)),
                          reads=[("sselscr",), ("small", "selcol"), ("ssres",)],
                          writes=[("ssres",)])
                E("gpsimd", (lambda e, i=i, q4=q4: e.dma_start(
                    dt["hb_st"][i * 128:(i + 1) * 128, q4 * 512:(q4 + 1) * 512],
                    arena[:, 4096:4608])),
                  reads=[("ssres",)], writes=[("hbst", i, q4)], dma_sem="dst2")

        y1rold = claim(0, 16384, [("y1res", a) for a in range(8)])
        for a in range(8):
            E("sync", (lambda e, a=a: e.dma_start(
                arena[:, a * 2048:(a + 1) * 2048],
                dt["hb_y1"][a * 128:(a + 1) * 128, :])),
              reads=[("hby1", s_, i_) for s_ in range(4) for i_ in range(NCH)],
              writes=[("y1res", a)],
              conflicts=(y1rold if a == 0 else ()), dma_sem="dyr")

        if "y1" in DEBUG:
            for a in range(8):
                E("gpsimd", (lambda e, a=a: e.dma_start(
                    dt["dbg_y1"][a * 128:(a + 1) * 128, :],
                    arena[:, a * 2048:(a + 1) * 2048])),
                  reads=[("y1res", a)], dma_sem="dbg")
        wfold = claim(WER, WER + 8192, [("wfe", a) for a in range(8)])
        meold = claim(MER, MER + 2048, [("mer",)])
        biasr = x1[0:1, 6144:6656].bitcast(BF16)
        macc = x1[:, 8192:16384].bitcast(BF16)
        E("gpsimd", (lambda e: e.memset(x1[:, 8192:16384], 0.0)),
          writes=[("macc", b_) for b_ in range(8)],
          conflicts=[("fw1", 0), ("fw1", 1), ("fw2", 0), ("fw2", 1), ("fstg",),
                     ("x1", 4), ("x1", 5), ("x1", 6), ("x1", 7)]
                    + [("ypsl", j_) for j_ in range(8)]
                    + [("y1sl", j_) for j_ in range(8)])
        for e_ in range(NEXP):
            for a in range(8):
                E("sync", (lambda e, e_=e_, a=a: e.dma_start(
                    arena[:, WER + a * 1024: WER + (a + 1) * 1024],
                    dt["ag_wf_out"][e_, a * 128:(a + 1) * 128, :])),
                  reads=[("ago_wf",)], writes=[("wfe", a)],
                  conflicts=[("wfe", a)] + (list(wfold) if (e_ == 0 and a == 0) else []),
                  dma_sem="dwf")
            E("sync", (lambda e, e_=e_: e.dma_start(
                biasr[0:1, :], dt["ag_wf_out"][e_, 1024:1025, :])),
              reads=[("ago_wf",)], writes=[("biasr",)],
              conflicts=[("biasr",), ("x1", 3)], dma_sem="dwfb")
            E("vector", (lambda e, e_=e_: e.tensor_scalar(
                arena[:, MER:MER + 2048], sbst[:, :], float(e_), None, ALU.is_equal)),
              reads=[("sbst", q_) for q_ in range(4)], writes=[("mer",)],
              conflicts=(meold if e_ == 0 else [("mer",)]))
            E("vector", (lambda e: e.tensor_mul(arena[:, MER:MER + 2048],
                                                arena[:, MER:MER + 2048],
                                                y1tb[:, 0:2048])),
              reads=[("mer",)] + [("gkb", q_) for q_ in range(4)], writes=[("mer",)])
            for a in range(8):
                E("vector", (lambda e, a=a: e.tensor_mul(
                    xn[:, a * 2048:(a + 1) * 2048],
                    arena[:, a * 2048:(a + 1) * 2048],
                    arena[:, MER:MER + 2048])),
                  reads=[("y1res", a), ("mer",)], writes=[("y1m", a)],
                  conflicts=([("xn1t", a, 0), ("xn1t", a, 1)] if e_ == 0
                             else [("y1m", a)]))
            for b_ in range(8):
                for tokc in range(4):
                    bank = (b_ * 4 + tokc) % 2
                    for a in range(8):
                        E("tensor", (lambda e, b_=b_, tokc=tokc, a=a, bank=bank:
                                     e.matmul(pb[bank][:, 0:512],
                                              arena[:, WER + a * 1024 + b_ * 128:
                                                    WER + a * 1024 + (b_ + 1) * 128],
                                              xn[:, a * 2048 + tokc * 512:
                                                 a * 2048 + (tokc + 1) * 512],
                                              start=(a == 0), stop=False,
                                              skip_group_check=True)),
                          reads=[("wfe", a), ("y1m", a)], writes=[PSK[bank]], mm=True)
                    E("tensor", (lambda e, b_=b_, tokc=tokc, bank=bank:
                                 e.matmul(pb[bank][:, 0:512],
                                          biasr[0:1, b_ * 128:(b_ + 1) * 128],
                                          arena[0:1, MER + tokc * 512:
                                                MER + (tokc + 1) * 512],
                                          start=False, stop=True,
                                          skip_group_check=True)),
                      reads=[("biasr",), ("mer",)], writes=[PSK[bank]], mm=True)
                    E("vector", (lambda e, b_=b_, tokc=tokc, bank=bank:
                                 e.tensor_add(
                                     macc[:, b_ * 2048 + tokc * 512:
                                          b_ * 2048 + (tokc + 1) * 512],
                                     macc[:, b_ * 2048 + tokc * 512:
                                          b_ * 2048 + (tokc + 1) * 512],
                                     pb[bank][:, 0:512])),
                      reads=[PSK[bank], ("macc", b_)], writes=[("macc", b_)])

        ypsl2 = arena[:, 24576:28672]
        y1sl2 = arena[:, WER:WER + 4096]
        ypsl2c = claim(24576, 28672, [("ypsl2", i) for i in range(NCH)])
        y1sl2c = claim(WER, WER + 4096, [("y1sl2", i) for i in range(NCH)])
        outsb = x1
        for tc in range(4):
            for m in range(NCH):
                E("vector", (lambda e, tc=tc, m=m:
                             e.tensor_add(ypsl2[:, m * 512:(m + 1) * 512],
                                          arena[:, m * 2048 + tc * 512:
                                                m * 2048 + (tc + 1) * 512],
                                          macc[:, m * 2048 + tc * 512:
                                               m * 2048 + (tc + 1) * 512])),
                  reads=[("y1res", m), ("macc", m)], writes=[("ypsl2", m)],
                  conflicts=(list(ypsl2c) if (tc == 0 and m == 0) else ()))
            DRAIN(emit_ln(ypsl2, lambda i, qc: ("ypsl2", i), y1sl2,
                          lambda i, qc: ("y1sl2", i),
                          512, "ln2g_col", "ln2b_col", onescol=onesc_b,
                          dst_conflicts=(list(y1sl2c) if tc == 0 else ())))
            for m in range(NCH):
                ws = m % 2
                E("gpsimd", (lambda e, tc=tc, m=m, ws=ws: e.dma_start(
                    pt[ws][:, 0:512],
                    dt["hb_st"][m * 128:(m + 1) * 128, tc * 512:(tc + 1) * 512])),
                  reads=[("hbst", m, tc)], writes=[("y1f", ws)],
                  conflicts=[("y1f", ws)] + ([("y1stg", ws), ("b2r",)]
                                             if (tc == 0 and m < 2) else []),
                  dma_sem="dy1f" + str(ws))
                E("vector", (lambda e, tc=tc, m=m, ws=ws:
                             e.tensor_add(x1[:, ws * 512:(ws + 1) * 512],
                                          y1sl2[:, m * 512:(m + 1) * 512],
                                          pt[ws][:, 0:512])),
                  reads=[("y1sl2", m), ("y1f", ws)], writes=[("outsb", ws)],
                  conflicts=[("outdma", ws)] + ([("x1", 0)] if (tc == 0 and m < 2)
                                                else []))
                for hb in range(2):
                    E("sync", (lambda e, tc=tc, m=m, ws=ws, hb=hb: e.dma_start(
                        dt["out_fm"][2 * tc + hb, m * 128:(m + 1) * 128, :],
                        x1[:, ws * 512 + hb * 256: ws * 512 + hb * 256 + 256])),
                      reads=[("outsb", ws)], writes=[("outdma", ws)], dma_sem="dout")
    if "xn1" in DEBUG:
        for i in range(NCH):
            E("gpsimd", (lambda e, i=i: e.dma_start(dt["dbg_xn1"][i * 128:(i + 1) * 128, :],
                                                    xn[:, i * T:(i + 1) * T])),
              reads=[("xn1", i, 0), ("xn1", i, 1), ("xn1", i, 2), ("xn1", i, 3)], dma_sem="dbg")
    if "states" in DEBUG and STAGES["l1a"]:
        for j in range(2):
            E("gpsimd", (lambda e, j=j: e.dma_start(dt["dbg_states"][j * 128:(j + 1) * 128, :],
                                                    arena[:, 28672 + j * T: 28672 + (j + 1) * T])),
              reads=[("st", T - 1, j)], dma_sem="dbg")
    if "ypre" in DEBUG and STAGES["l1a"]:
        for i in range(NCH):
            E("gpsimd", (lambda e, i=i: e.dma_start(dt["dbg_ypre"][i * 128:(i + 1) * 128, :],
                                                    arena[:, 34816 + i * 256: 34816 + (i + 1) * 256])),
              reads=[("ypre", 7, i)], dma_sem="dbg")
    if "y1a" in DEBUG and STAGES["l1a"]:
        for i in range(NCH):
            E("gpsimd", (lambda e, i=i: e.dma_start(dt["dbg_y1a"][i * 128:(i + 1) * 128, :],
                                                    arena[:, 32768 + i * 256: 32768 + (i + 1) * 256])),
              reads=[("y1a", 7, i)], dma_sem="dbg")
    if "h" in DEBUG and STAGES["l1a"]:
        for sub in range(4):
            E("gpsimd", (lambda e, sub=sub: e.dma_start(dt["dbg_h"][sub * 128:(sub + 1) * 128, :],
                                                        h_sb[:, sub * 256:(sub + 1) * 256])),
              reads=[("h", sub)], dma_sem="dbg")
    if "x1" in DEBUG:
        for i in range(NCH):
            E("sync", (lambda e, i=i: e.dma_start(dt["dbg_x1"][i * 128:(i + 1) * 128, :],
                                                  x1[:, i * T:(i + 1) * T])),
              reads=[("x1", i)], dma_sem="dout")

    if not STAGES["l1f"]:
        for bb in range(B):
            for i in range(NCH):
                E("sync", (lambda e, bb=bb, i=i: e.dma_start(
                    dt["out_fm"][bb, i * 128:(i + 1) * 128, :],
                    x1[:, i * T: i * T + TW])),
                  reads=[("x1", i)], dma_sem="dout")

    P.resolve()

    for name, cntv in P.final_counts.items():
        if name.startswith("D:dout") or name.startswith("D:dbg"):
            P.final_waits.append(("gpsimd", name, cntv))

    semnames = sorted(set(P.semnames) | {n for _, n, _ in P.final_waits})
    sems = {}
    for n in semnames:
        sems[n] = es.enter_context(nc.semaphore(n.replace(":", "_")))
    with nc.Block() as block:
        bodies = P.make_bodies(sems)
        block.sync(bodies["sync"])
        block.gpsimd(bodies["gpsimd"])
        block.tensor(bodies["tensor"])
        block.scalar(bodies["scalar"])
        block.vector(bodies["vector"])
    es.close()
    build_program.last_P = P
    return nc, len(P.recs)


def prep_inputs(inputs):
    f = np.float32
    g = {k: np.asarray(v) for k, v in inputs.items()}

    def col(v):
        return np.ascontiguousarray(v.reshape(-1, 128).T).astype(f)

    com = {}
    com["wq"] = np.ascontiguousarray(g["L0_attn_Wq"])
    com["wk"] = np.ascontiguousarray(g["L0_attn_Wk"])
    com["wv"] = np.ascontiguousarray(g["L0_attn_Wv"])
    com["bq_col"] = col(g["L0_attn_bq"]); com["bk_col"] = col(g["L0_attn_bk"])
    com["bv_row"] = g["L0_attn_bv"].reshape(1, -1).astype(f)
    com["w0r"] = np.ascontiguousarray(g["L0_ffn_W_r"].T)
    com["w0k"] = np.ascontiguousarray(g["L0_ffn_W_k"].T)
    com["w0v"] = np.ascontiguousarray(g["L0_ffn_W_v"].T)
    com["mur_col"] = col(g["L0_ffn_mu_r"]); com["muk_col"] = col(g["L0_ffn_mu_k"])
    com["bmT"] = np.ascontiguousarray(g["L1_attn_Bm"].T)
    com["dT"] = np.ascontiguousarray(g["L1_attn_D"].T)
    com["aT"] = np.ascontiguousarray(g["L1_attn_A"].T)
    com["cT"] = np.ascontiguousarray(g["L1_attn_C"].T)
    com["lnw_col"] = col(g["L1_attn_ln_w"]); com["lnb_col"] = col(g["L1_attn_ln_b"])
    com["w1T"] = np.ascontiguousarray(g["L1_attn_W1"].T)
    com["b1_col"] = col(g["L1_attn_b1"])
    com["w2T"] = np.ascontiguousarray(g["L1_attn_W2"].T)
    com["b2_col"] = col(g["L1_attn_b2"])
    com["ipw"] = np.ascontiguousarray(g["L1_ffn_in_proj_w"].T)
    com["ipb_row"] = g["L1_ffn_in_proj_b"][2048:].reshape(1, -1).astype(f)
    com["ipb_col"] = col(g["L1_ffn_in_proj_b"])
    com["opw"] = np.ascontiguousarray(g["L1_ffn_out_proj_w"].T)
    com["opb_col"] = col(g["L1_ffn_out_proj_b"])
    com["ln1g_col"] = col(g["L1_ffn_ln1_g"]); com["ln1b_col"] = col(g["L1_ffn_ln1_b"])
    com["ln2g_col"] = col(g["L1_ffn_ln2_g"]); com["ln2b_col"] = col(g["L1_ffn_ln2_b"])
    com["wg"] = np.ascontiguousarray(g["L1_ffn_wg"])
    jj = np.arange(128)
    uu_ = jj // 8
    com["mha_mask"] = np.where(uu_[:, None] == uu_[None, :], 0.0, -480.0).astype(f)
    com["tri_all"] = (jj[:, None] < jj[None, :]).astype(f)

    com["ident"] = np.eye(128, dtype=f)
    com["identf"] = np.eye(128, dtype=f)
    com["iota8r"] = np.broadcast_to(np.arange(8, dtype=f), (128, 8)).copy()

    x = np.asarray(g["L0_attn_hidden_states"])
    in_maps = []
    for c in range(NCORES):
        m = dict(com)
        m["xT"] = np.ascontiguousarray(x[c].T).astype(f)
        m["myw1T"] = np.ascontiguousarray(np.asarray(g["L1_ffn_w1"])[c].T).astype(f)
        m["myw2"] = np.ascontiguousarray(np.asarray(g["L1_ffn_w2"])[c]).astype(f)
        m["myb1c"] = np.ascontiguousarray(
            np.asarray(g["L1_ffn_b1"])[c].reshape(-1, 128).T).astype(f)
        m["myb2r"] = np.asarray(g["L1_ffn_b2"])[c].reshape(1, -1).astype(f)
        m["selcol"] = np.broadcast_to((np.arange(NCORES) == c).astype(f), (128, NCORES)).copy()
        m64 = np.zeros((64, NEXP), f)
        for cp in range(NCORES):
            for bp in range(NEXP):
                for b_ in range(NEXP):
                    m64[cp * 8 + bp, b_] = (float(bp < b_ and cp != c)
                                            + float(cp < c and bp == b_))
        m["m64"] = m64
        in_maps.append(m)
    return in_maps


_CACHED = {}


def kernel(**inputs):
    key = "prog"
    if key not in _CACHED:
        _CACHED[key] = build_program()
    nc, _ = _CACHED[key]
    in_maps = prep_inputs(inputs)
    res = run_bass_kernel_spmd(nc, in_maps, core_ids=list(range(NCORES)))
    out = np.concatenate([r["out_fm"].transpose(0, 2, 1) for r in res.results], axis=1)
    return out.astype(np.float32)
